# version 03
# baseline (speedup 1.0000x reference)
import numpy as np
from contextlib import ExitStack
import concourse.bass as bass
import concourse.mybir as mybir
from concourse.bass_utils import run_bass_kernel_spmd

F32 = mybir.dt.float32
BF16 = mybir.dt.bfloat16
U32 = mybir.dt.uint32
U8 = mybir.dt.uint8
AX = mybir.AxisListType
ALU = mybir.AluOpType
AF = mybir.ActivationFunctionType

NCORES = 8
S = 4096
D = 1024
PT = 128
NT = S // PT
NEG = -3.0e38

ENGS = ("sync", "tensor", "vector", "scalar", "gpsimd")


class Em:

    def __init__(self, nc):
        self.nc = nc
        self.items = {e: [] for e in ENGS}

    def op(self, eng, fn):
        self.items[eng].append(("op", fn))

    def dma(self, eng, fn):
        self.items[eng].append(("dma", fn))

    def bar(self):
        for e in ENGS:
            self.items[e].append(("bar",))

    def build(self, ctx):
        nc = self.nc
        bsem = ctx.enter_context(nc.semaphore("bsem"))
        dsems = {e: ctx.enter_context(nc.semaphore(f"dsem_{e}")) for e in ENGS}
        block = ctx.enter_context(nc.Block())
        for e in ENGS:
            def mk(e=e, items=self.items[e]):
                def run(eng):
                    barcount = 0
                    dmacount = 0
                    dma_waited = 0
                    last = None
                    for it in items:
                        kind = it[0]
                        if kind == "op":
                            r = it[1](eng)
                            last = r if r is not None else last
                        elif kind == "dma":
                            it[1](eng).then_inc(dsems[e], 16)
                            dmacount += 1
                            last = None
                        else:
                            if dmacount > dma_waited:
                                eng.wait_ge(dsems[e], 16 * dmacount)
                                dma_waited = dmacount
                            if last is not None:
                                last.then_inc(bsem, 1)
                            else:
                                eng.nop().then_inc(bsem, 1)
                            last = None
                            barcount += 1
                            eng.wait_ge(bsem, len(ENGS) * barcount)
                return run
            getattr(block, e)(mk())


def _ident_bf16():
    import ml_dtypes
    return np.eye(128, dtype=ml_dtypes.bfloat16)

def _l1_mask():
    i = np.arange(512)[:, None]
    j = np.arange(512)[None, :]
    m = np.zeros((512, 1024), np.float32)
    m[:, 512:] = np.where(j > i, NEG, 0.0)
    return np.ascontiguousarray(m.reshape(4, 128, 1024).transpose(1, 0, 2))

def _l1_rot():
    d = 64
    inv = 1.0 / 10000.0 ** (np.arange(0, d, 2, dtype=np.float64) / d)
    t = np.arange(S, dtype=np.float64)
    fr = t[:, None] * inv[None, :]
    cos = np.cos(fr).astype(np.float32)
    sin = np.sin(fr).astype(np.float32)
    sc = np.float32(d ** (-0.5))
    def til(a):
        return np.ascontiguousarray(a.reshape(NT, PT, 32).transpose(1, 0, 2))
    return {"qcos": til(cos * sc), "qsin": til(sin * sc),
            "kcos": til(cos), "ksin": til(sin)}

def _wtile(w):
    k, n = w.shape
    return np.ascontiguousarray(w.reshape(k // PT, PT, n).transpose(1, 0, 2))


def emit_ln_group(em, xs, outs, xcs, sq, cols, dmodel, n, gain=None, bias=None):
    inv = 1.0 / dmodel
    for i in range(n):
        em.op("vector", lambda v, i=i: v.reduce_sum(
            cols["s"][:, i:i + 1], xs[i], axis=AX.X))
    em.bar()
    em.op("vector", lambda v: v.tensor_scalar(
        cols["s"][:, 0:n], cols["s"][:, 0:n], inv, None, ALU.mult))
    em.bar()
    for i in range(n):
        em.op("vector", lambda v, i=i: v.tensor_scalar(
            xcs[i], xs[i], cols["s"][:, i:i + 1], None, ALU.subtract))
    em.bar()
    for i in range(n):
        em.op("scalar", lambda sc, i=i: sc.activation(
            sq, xcs[i], AF.Square, accum_out=cols["ss"][:, i:i + 1]))
    em.bar()
    em.op("scalar", lambda sc: sc.activation(
        cols["r"][:, 0:n], cols["ss"][:, 0:n], AF.Sqrt, scale=inv,
        bias=cols["eps"][:, 0:1]))
    em.bar()
    em.op("vector", lambda v: v.reciprocal(cols["r"][:, 0:n], cols["r"][:, 0:n]))
    em.bar()
    if gain is None:
        for i in range(n):
            em.op("vector", lambda v, i=i: v.tensor_scalar(
                outs[i], xcs[i], cols["r"][:, i:i + 1], None, ALU.mult))
    else:
        for i in range(n):
            em.op("vector", lambda v, i=i: v.scalar_tensor_tensor(
                outs[i], xcs[i], cols["r"][:, i:i + 1], gain, ALU.mult, ALU.mult))
        em.bar()
        for i in range(n):
            em.op("vector", lambda v, i=i: v.tensor_add(outs[i], outs[i], bias))
    em.bar()


def emit_transpose_tiles(em, srcs, dst_fn, iden, p_tr, n, nkc=8):
    for i in range(n):
        def tr(t, i=i):
            r = None
            for kc in range(nkc):
                r = t.transpose(p_tr[:, kc * 128:(kc + 1) * 128],
                                srcs[i][:, kc * 128:(kc + 1) * 128], iden)
            return r
        em.op("tensor", tr)
        em.bar()
        def ev(v, i=i):
            r = None
            for kc in range(nkc):
                r = v.tensor_copy(dst_fn(i, kc), p_tr[:, kc * 128:(kc + 1) * 128])
            return r
        em.op("vector", ev)
        em.bar()



def emit_row_broadcast(em, dst, n, chunk=256):
    def f(g, dst=dst, n=n, chunk=chunk):
        r = None
        for o in range(0, n, chunk):
            r = g.partition_broadcast(dst[:, o:o + chunk], dst[0:1, o:o + chunk])
        return r
    em.op("gpsimd", f)

def emit_l1_attn(nc, em, pfx, x_in, y_out, dd):
    SB = lambda n, s, d: nc.sbuf_tensor(pfx + n, s, d)
    PS = lambda n, s, d: nc.psum_tensor(pfx + n, s, d)
    DR = lambda n, s, d: nc.dram_tensor(pfx + n, s, d).ap()
    Wq_d = dd["Wq"]; Wkv_d = dd["Wkv"]; Wo_d = dd["Wo"]; Wo0_d = dd["Wo0"]
    bo_d = dd["bo"]; bo0_d = dd["bo0"]; iden_d = dd["iden"]; mask_d = dd["mask"]
    rot_d = {k: dd[k] for k in ("qcos", "qsin", "kcos", "ksin")}
    qT2 = DR("qT2", [8, 128, S], BF16)
    kvT2 = DR("kvT2", [8, 128, S], BF16)
    kvt_d = DR("kvt_scr", [S, D], BF16)

    x_t = x_in.rearrange("(t p) d -> p t d", p=PT)
    y_t = y_out.rearrange("(t p) d -> p t d", p=PT)
    kvt_t = kvt_d.rearrange("(t p) d -> p t d", p=PT)

    G = 4
    NG = NT // G
    SEG = 512
    H = 16

    with ExitStack() as ctx:
        E = ctx.enter_context
        WA_s = E(SB("WA_s", [128, 8, D], BF16))
        WB_s = E(SB("WB_s", [128, 8, D], BF16))
        iden = E(SB("iden_s", [128, 128], BF16))
        bo_b = E(SB("bo_b", [128, D], F32))
        bo0_b = E(SB("bo0_b", [128, D], F32))
        eps_c = E(SB("eps_c", [128, 1], F32))
        rot_g = {k: E(SB(f"{k}_g", [128, G, 32], F32))
                 for k in ("qcos", "qsin", "kcos", "ksin")}
        sq = E(SB("sq", [128, D], F32))
        rsc = E(SB("rsc", [128, 2, 512], F32))
        arF = E(SB("arF", [128, 12288], F32))
        xg_a = arF[:, 0:4096].rearrange("p (i d) -> p i d", d=D)
        xc_a = arF[:, 4096:8192].rearrange("p (i d) -> p i d", d=D)
        probs4 = arF[:, 0:4096].rearrange("p (a k) -> p a k", k=1024)
        mask_s = arF[:, 4096:8192].rearrange("p (a k) -> p a k", k=1024)
        yb = arF[:, 8192:9216]
        ybc = arF[:, 9216:10240]
        xin2 = arF[:, 10240:11264]
        arB = E(SB("arB", [128, 16384], BF16))
        xln_a = arB[:, 0:4096].rearrange("p (i d) -> p i d", d=D)
        xT_a = arB[:, 4096:8192].rearrange("p (kc m) -> p kc m", m=G * PT)
        qrot_a = arB[:, 8192:12288].rearrange("p (i d) -> p i d", d=D)
        kvTs = arB[:, 0:8192].rearrange("p (f t) -> p f t", t=2 * SEG)
        kvts = arB[:, 8192:16384].rearrange("p (i d) -> p i d", d=D)
        qTs = E(SB("qTs", [128, 8, SEG], BF16))
        probsb = E(SB("probsb", [128, 4, 1024], BF16))
        probsT = E(SB("probsT", [128, 8, SEG], BF16))
        oTs = E(SB("oTs", [128, 8, SEG], BF16))
        mx4 = E(SB("mx4", [128, 4], F32))
        den4 = E(SB("den4", [128, 4], F32))
        cols = {"s": E(SB("ln_s", [128, G], F32)),
                "ss": E(SB("ln_ss", [128, G], F32)),
                "r": E(SB("ln_r", [128, G], F32)),
                "eps": eps_c}
        p_a = E(PS("p_a", [128, 2, 512], F32))
        p_b = E(PS("p_b", [128, 2, 512], F32))
        p_tr = E(PS("p_tr", [128, 1024], BF16))
        p_tr2 = E(PS("p_tr2", [128, 1024], BF16))
        p_av = E(PS("p_av", [64, 512], F32))


        em.dma("gpsimd", lambda g: g.dma_start(out=WA_s[:], in_=Wq_d))
        em.dma("gpsimd", lambda g: g.dma_start(out=WB_s[:], in_=Wkv_d))
        em.dma("sync", lambda s_: s_.dma_start(out=iden[:], in_=iden_d))
        em.dma("sync", lambda s_: s_.dma_start(out=bo_b[:], in_=bo_d.broadcast_to([128, D])))
        em.dma("sync", lambda s_: s_.dma_start(out=bo0_b[:], in_=bo0_d.broadcast_to([128, D])))
        em.op("vector", lambda v: v.memset(eps_c[:], 1e-5))
        em.bar()

        for g in range(NG):
            t0 = g * G
            for i in range(G):
                em.dma("sync", lambda s_, i=i, t0=t0: s_.dma_start(
                    out=xg_a[:, i, :], in_=x_t[:, t0 + i, :]))
            for k in ("qcos", "qsin", "kcos", "ksin"):
                em.dma("sync", lambda s_, k=k, t0=t0: s_.dma_start(
                    out=rot_g[k][:], in_=rot_d[k][:, t0:t0 + G, :]))
            em.bar()
            emit_ln_group(em, [xg_a[:, i, :] for i in range(G)],
                          [xln_a[:, i, :] for i in range(G)],
                          [xc_a[:, i, :] for i in range(G)], sq[:], cols, D, G)
            emit_transpose_tiles(
                em, [xln_a[:, i, :] for i in range(G)],
                lambda i, kc: xT_a[:, kc, i * 128:(i + 1) * 128], iden[:],
                p_tr[:], G)
            for nm, W_s, scrT, ckey, skey in (
                    ("q", WA_s, qT2, "qcos", "qsin"),
                    ("kv", WB_s, kvT2, "kcos", "ksin")):
                for i in range(G):
                    def mm1(t, i=i, W_s=W_s):
                        for kc in range(8):
                            t.matmul(p_a[:, 0, :], xT_a[:, kc, i * 128:(i + 1) * 128],
                                     W_s[:, kc, 0:512], start=(kc == 0), stop=(kc == 7))
                        r = None
                        for kc in range(8):
                            r = t.matmul(p_b[:, 0, :], xT_a[:, kc, i * 128:(i + 1) * 128],
                                         W_s[:, kc, 512:1024], start=(kc == 0), stop=(kc == 7))
                        return r
                    em.op("tensor", mm1)
                    em.bar()
                    em.op("vector", lambda v, i=i: v.tensor_copy(
                        xg_a[:, i, 0:512], p_a[:, 0, :]))
                    em.op("scalar", lambda s_, i=i: s_.copy(
                        xg_a[:, i, 512:1024], p_b[:, 0, :]))
                    em.bar()
                for i in range(G):
                    q3 = xg_a[:, i, :].rearrange("p (h two d) -> p h two d", h=H, two=2)
                    o3 = qrot_a[:, i, :].rearrange("p (h two d) -> p h two d", h=H, two=2)
                    cb = rot_g[ckey][:, i, :].unsqueeze(1).broadcast_to([128, H, 32])
                    sb = rot_g[skey][:, i, :].unsqueeze(1).broadcast_to([128, H, 32])
                    t_a = rsc[:, 0, :].rearrange("p (h d) -> p h d", h=H)
                    t_b = rsc[:, 1, :].rearrange("p (h d) -> p h d", h=H)
                    em.op("vector", lambda v, q3=q3, sb=sb, t_a=t_a:
                          v.tensor_mul(t_a, q3[:, :, 1, :], sb))
                    em.op("gpsimd", lambda g_, q3=q3, sb=sb, t_b=t_b:
                          g_.tensor_mul(t_b, q3[:, :, 0, :], sb))
                    em.bar()
                    def rot2(v, q3=q3, cb=cb, o3=o3):
                        v.tensor_mul(o3[:, :, 0, :], q3[:, :, 0, :], cb)
                        return v.tensor_mul(o3[:, :, 1, :], q3[:, :, 1, :], cb)
                    em.op("vector", rot2)
                    em.bar()
                    em.op("vector", lambda v, o3=o3, t_a=t_a:
                          v.tensor_sub(o3[:, :, 0, :], o3[:, :, 0, :], t_a))
                    em.op("gpsimd", lambda g_, o3=o3, t_b=t_b:
                          g_.tensor_add(o3[:, :, 1, :], o3[:, :, 1, :], t_b))
                    em.bar()
                xcb = arF[:, 4096:8192].bitcast(BF16)
                emit_transpose_tiles(
                    em, [qrot_a[:, i, :] for i in range(G)],
                    lambda i, kc: xcb[:, (i * 8 + kc) * 128:(i * 8 + kc + 1) * 128],
                    iden[:], p_tr2[:], G)
                for i in range(G):
                    for kc in range(8):
                        em.dma("sync", lambda s_, i=i, kc=kc, scrT=scrT, t0=t0, xcb=xcb:
                               s_.dma_start(
                                   out=scrT[kc, :, (t0 + i) * 128:(t0 + i + 1) * 128],
                                   in_=xcb[:, (i * 8 + kc) * 128:(i * 8 + kc + 1) * 128]))
                if nm == "kv":
                    for i in range(G):
                        em.dma("sync", lambda s_, i=i, t0=t0: s_.dma_start(
                            out=kvt_t[:, t0 + i, :], in_=qrot_a[:, i, :]))
                em.bar()

        em.dma("gpsimd", lambda g: g.dma_start(out=WA_s[:], in_=Wo_d))
        em.dma("gpsimd", lambda g: g.dma_start(out=WB_s[:], in_=Wo0_d))
        em.dma("sync", lambda s_: s_.dma_start(out=mask_s[:, :, :], in_=mask_d))
        em.bar()

        for s in range(8):
            em.dma("sync", lambda s_, s=s: s_.dma_start(
                out=qTs[:], in_=qT2.rearrange("f p t -> p f t")[:, :, s * SEG:(s + 1) * SEG]))
            lo = max(0, (s - 1) * SEG)
            em.dma("sync", lambda s_, s=s, lo=lo: s_.dma_start(
                out=kvTs[:, :, (0 if s > 0 else SEG):],
                in_=kvT2.rearrange("f p t -> p f t")[:, :, lo:(s + 1) * SEG]))
            nti = 8 if s > 0 else 4
            for i in range(nti):
                ti = ((s - 1) * 4 + i) if s > 0 else i
                sl = i if s > 0 else (4 + i)
                em.dma("sync", lambda s_, sl=sl, ti=ti: s_.dma_start(
                    out=kvts[:, sl, :], in_=kvt_t[:, ti, :]))
            em.bar()
            nk = 2 * SEG if s > 0 else SEG
            ko = 0 if s > 0 else SEG
            ntk = nk // 128
            for h in range(H):
                hb = (h % 2) * 64
                fc = h // 2
                for r in range(2):
                    def mmsc(t, h=h, hb=hb, fc=fc, s=s, r=r):
                        rr = None
                        for j in range(2):
                            tqc = 2 * r + j
                            lhs = qTs[hb:hb + 64, fc, tqc * 128:(tqc + 1) * 128]
                            if s > 0:
                                t.matmul(p_a[:, j, :], lhs,
                                         kvTs[hb:hb + 64, fc, 0:512],
                                         start=True, stop=True)
                            rr = t.matmul(p_b[:, j, :], lhs,
                                          kvTs[hb:hb + 64, fc, SEG:2 * SEG],
                                          start=True, stop=True)
                        return rr
                    em.op("tensor", mmsc)
                    em.bar()
                    def evsc(v, s=s, r=r):
                        if s > 0:
                            v.tensor_add(probs4[:, 2 * r:2 * r + 2, 0:512],
                                         p_a[:, :, :], mask_s[:, 2 * r:2 * r + 2, 0:512])
                        return v.tensor_add(probs4[:, 2 * r:2 * r + 2, 512:1024],
                                            p_b[:, :, :], mask_s[:, 2 * r:2 * r + 2, 512:1024])
                    em.op("vector", evsc)
                    em.bar()
                em.op("vector", lambda v, ko=ko: v.tensor_reduce(
                    mx4[:], probs4[:, :, ko:1024], AX.X, ALU.max, negate=True))
                em.bar()
                def sfm1(v, ko=ko):
                    rr = None
                    for a in range(4):
                        rr = v.tensor_scalar(probs4[:, a, ko:1024],
                                             probs4[:, a, ko:1024],
                                             mx4[:, a:a + 1], None, ALU.add)
                    return rr
                em.op("vector", sfm1)
                em.bar()
                em.op("scalar", lambda sc, ko=ko: sc.activation(
                    probs4[:, :, ko:1024], probs4[:, :, ko:1024], AF.Exp))
                em.bar()
                em.op("vector", lambda v, ko=ko: v.tensor_reduce(
                    den4[:], probs4[:, :, ko:1024], AX.X, ALU.add))
                em.bar()
                em.op("vector", lambda v: v.reciprocal(den4[:], den4[:]))
                em.bar()
                def sfm2(v, ko=ko):
                    rr = None
                    for a in range(4):
                        rr = v.tensor_scalar(probsb[:, a, ko:1024],
                                             probs4[:, a, ko:1024],
                                             den4[:, a:a + 1], None, ALU.mult)
                    return rr
                em.op("vector", sfm2)
                em.bar()
                for tqc in range(4):
                    pp = p_tr if tqc % 2 == 0 else p_tr2
                    def trp(t, tqc=tqc, pp=pp, ko=ko, ntk=ntk):
                        rr = None
                        for tkc in range(ntk):
                            rr = t.transpose(
                                pp[:, tkc * 128:(tkc + 1) * 128],
                                probsb[:, tqc, ko + tkc * 128:ko + (tkc + 1) * 128],
                                iden[:])
                        return rr
                    em.op("tensor", trp)
                    em.bar()
                    def evp(v, tqc=tqc, pp=pp, ntk=ntk):
                        return v.tensor_copy(
                            probsT[:, 0:ntk, tqc * 128:(tqc + 1) * 128],
                            pp[:, 0:ntk * 128].rearrange("p (a m) -> p a m", m=128))
                    em.op("vector", evp)
                    em.bar()
                def mmav(t, h=h, s=s, ntk=ntk):
                    rr = None
                    for tkc in range(ntk):
                        kvtile = kvts[:, (tkc if s > 0 else 4 + tkc), h * 64:(h + 1) * 64]
                        rr = t.matmul(p_av[:], kvtile, probsT[:, tkc, :],
                                      start=(tkc == 0), stop=(tkc == ntk - 1))
                    return rr
                em.op("tensor", mmav)
                em.bar()
                em.op("vector", lambda v, hb=hb, fc=fc: v.tensor_copy(
                    oTs[hb:hb + 64, fc, :], p_av[:]))
                em.bar()
            W_s = WB_s if s == 0 else WA_s
            bias_b = bo0_b if s == 0 else bo_b
            for i in range(4):
                em.dma("sync", lambda s_, s=s, i=i: s_.dma_start(
                    out=xin2[:], in_=x_t[:, s * 4 + i, :]))
                def mmo(t, i=i, W_s=W_s):
                    for kc in range(8):
                        t.matmul(p_a[:, 0, :], oTs[:, kc, i * 128:(i + 1) * 128],
                                 W_s[:, kc, 0:512], start=(kc == 0), stop=(kc == 7))
                    rr = None
                    for kc in range(8):
                        rr = t.matmul(p_b[:, 0, :], oTs[:, kc, i * 128:(i + 1) * 128],
                                      W_s[:, kc, 512:1024], start=(kc == 0), stop=(kc == 7))
                    return rr
                em.op("tensor", mmo)
                em.bar()
                def evo(v, bias_b=bias_b):
                    v.tensor_add(yb[:, 0:512], p_a[:, 0, :], bias_b[:, 0:512])
                    return v.tensor_add(yb[:, 512:1024], p_b[:, 0, :], bias_b[:, 512:1024])
                em.op("vector", evo)
                em.bar()
                emit_ln_group(em, [yb[:]], [ybc[:]], [ybc[:]], sq[:], cols, D, 1)
                em.op("vector", lambda v: v.tensor_add(xin2[:], xin2[:], ybc[:]))
                em.bar()
                em.dma("sync", lambda s_, s=s, i=i: s_.dma_start(
                    out=y_t[:, s * 4 + i, :], in_=xin2[:]))
                em.bar()




def _l0ffn_xpos():
    dh = 64
    half = np.arange(0, dh, 2, dtype=np.float64)
    inv = 1.0 / 10000.0 ** (half / dh)
    t = np.arange(S, dtype=np.float64)
    fr = t[:, None] * inv[None, :]
    cos = np.cos(fr)
    sin = np.sin(fr)
    base = (half + 0.4 * dh) / (1.4 * dh)
    power = (t - S // 2) / 256.0
    scale = base[None, :] ** power[:, None]
    def til(a):
        return np.ascontiguousarray(
            a.astype(np.float32).reshape(NT, PT, 32).transpose(1, 0, 2))
    return {"fqc": til(cos * scale * 8.0), "fqs": til(sin * scale * 8.0),
            "fkc": til(cos / scale), "fks": til(sin / scale)}

def _l0ffn_mask():
    il = np.arange(512)[:, None]
    jl = np.arange(512)[None, :]
    m = np.zeros((512, 1024), np.float32)
    m[:, 0:512] = np.where(jl < il, NEG, 0.0)
    m[:, 512:] = np.where(jl > il, NEG, 0.0)
    return np.ascontiguousarray(m.reshape(4, 128, 1024).transpose(1, 0, 2))

def emit_l0_ffn(nc, em, pfx, x_in, y_out, dd):
    SB = lambda n, s, d: nc.sbuf_tensor(pfx + n, s, d)
    PS = lambda n, s, d: nc.psum_tensor(pfx + n, s, d)
    DR = lambda n, s, d: nc.dram_tensor(pfx + n, s, d).ap()
    iden_d = dd["iden"]; mask_d = dd["mask"]
    rot_d = {k: dd[k] for k in ("fqc", "fqs", "fkc", "fks")}
    WL = [{k: dd[f"{k}{l}"] for k in ("wqkv", "wo", "w1a", "w1g", "w2p", "qs",
                                      "ks", "lng", "lnb", "b1a", "b1g", "b2")}
          for l in range(2)]
    xcur = DR("xcur", [S, D], F32)
    qT2 = DR("qT2", [4, 128, S], BF16)
    kT2 = DR("kT2", [4, 128, S], BF16)
    vt2 = DR("vt2", [S, 512], BF16)
    hT2 = DR("hT2", [8, 128, S], BF16)
    h3a2 = DR("h3a2", [S, 2730], F32)

    x_t = x_in.rearrange("(t p) d -> p t d", p=PT)
    y_t = y_out.rearrange("(t p) d -> p t d", p=PT)
    xc_t = xcur.rearrange("(t p) d -> p t d", p=PT)
    vt_t = vt2.rearrange("(t p) d -> p t d", p=PT)
    h3a_t2 = h3a2.rearrange("(t p) d -> p t d", p=PT)

    G = 4
    NG = NT // G
    SEG = 512
    H = 8

    with ExitStack() as ctx:
        E = ctx.enter_context
        iden = E(SB("iden_s", [128, 128], BF16))
        eps_c = E(SB("eps_c", [128, 1], F32))
        sq = E(SB("sq", [128, D], F32))
        rsc = E(SB("rsc", [128, 2, 512], F32))
        ss8 = E(SB("ss8", [128, 16], F32))
        r8 = E(SB("r8", [128, 16], F32))
        qsb = E(SB("qsb", [128, 64], F32))
        ksb = E(SB("ksb", [128, 64], F32))
        lngb = E(SB("lngb", [128, D], F32))
        lnbb = E(SB("lnbb", [128, D], F32))
        b2b = E(SB("b2b", [128, D], F32))
        rotg = {k: E(SB(f"{k}_g", [128, G, 32], F32))
                for k in ("fqc", "fqs", "fkc", "fks")}
        cols = {"s": E(SB("ln_s", [128, G], F32)),
                "ss": E(SB("ln_ss", [128, G], F32)),
                "r": E(SB("ln_r", [128, G], F32)),
                "eps": eps_c}
        mx4 = E(SB("mx4", [128, 4], F32))
        den4 = E(SB("den4", [128, 4], F32))
        arF = E(SB("arF", [128, 18432], F32))
        xg = arF[:, 0:4096].rearrange("p (i d) -> p i d", d=D)
        qkvf = arF[:, 4096:10240].rearrange("p (i d) -> p i d", d=1536)
        probs4 = arF[:, 10240:14336].rearrange("p (a k) -> p a k", k=1024)
        mask_s = arF[:, 14336:18432].rearrange("p (a k) -> p a k", k=1024)
        hx = arF[:, 0:1024]
        b1ab = arF[:, 1024:3754]
        h3a_s = arF[:, 3754:6484]
        h3g_s = arF[:, 6484:9214]
        xt_c = arF[:, 9214:10238]
        arB = E(SB("arB", [128, 55304], BF16))
        wqkv_s = arB[:, 0:12288].rearrange("p (k n) -> p k n", n=1536)
        wo_s = arB[:, 12288:16384].rearrange("p (k n) -> p k n", n=1024)
        xTg = arB[:, 16384:20488].rearrange("p (k m) -> p k m", m=513)
        xbg = arB[:, 20488:24584].rearrange("p (i d) -> p i d", d=D)
        qrot = arB[:, 24584:26632].rearrange("p (i d) -> p i d", d=512)
        krot = arB[:, 26632:28680].rearrange("p (i d) -> p i d", d=512)
        vbf = arB[:, 28680:30728].rearrange("p (i d) -> p i d", d=512)
        qTs = arB[:, 34824:36872].rearrange("p (f t) -> p f t", t=512)
        kTs = arB[:, 36872:40968].rearrange("p (f t) -> p f t", t=1024)
        vts = arB[:, 40968:45064].rearrange("p (i d) -> p i d", d=512)
        probsb = arB[:, 45064:49160].rearrange("p (a k) -> p a k", k=1024)
        probsT = arB[:, 49160:53256].rearrange("p (a m) -> p a m", m=512)
        oTs = arB[:, 53256:55304].rearrange("p (f t) -> p f t", t=512)
        w1x_s = arB[:, 0:21840].rearrange("p (k n) -> p k n", n=2730)
        hb_c = arB[:, 21840:22864]
        hT_c1 = arB[:, 22864:23888].rearrange("p (k m) -> p k m", m=128)
        w2p_s = arB[:, 21840:44368].rearrange("p (k n) -> p k n", n=1024)
        actb = arB[:, 44368:47184]
        actT = arB[:, 47184:50000].rearrange("p (k m) -> p k m", m=128)
        hT_c2 = arB[:, 50000:51024].rearrange("p (k m) -> p k m", m=128)
        p_a = E(PS("p_a", [128, 2, 512], F32))
        p_b = E(PS("p_b", [128, 2, 512], F32))
        p_tr = E(PS("p_tr", [128, 1024], BF16))
        p_tr2 = E(PS("p_tr2", [128, 1024], BF16))
        p_av = E(PS("p_av", [64, 512], F32))


        em.dma("sync", lambda s_: s_.dma_start(out=iden[:], in_=iden_d))
        em.dma("sync", lambda s_: s_.dma_start(out=mask_s, in_=mask_d))
        em.op("vector", lambda v: v.memset(eps_c[:], 1e-5))
        em.bar()

        for g in range(NG):
            t0 = g * G
            for i in range(G):
                em.dma("sync", lambda s_, i=i, t0=t0: s_.dma_start(
                    out=xg[:, i, :], in_=x_t[:, t0 + i, :]))
            em.bar()
            emit_ln_group(em, [xg[:, i, :] for i in range(G)],
                          [qkvf[:, i, 0:1024] for i in range(G)],
                          [qkvf[:, i, 0:1024] for i in range(G)], sq[:], cols, D, G)
            for i in range(G):
                em.dma("sync", lambda s_, i=i, t0=t0: s_.dma_start(
                    out=xc_t[:, t0 + i, :], in_=qkvf[:, i, 0:1024]))
            em.bar()

        for l in range(2):
            W = WL[l]
            em.dma("gpsimd", lambda g_, W=W: g_.dma_start(out=wqkv_s, in_=W["wqkv"]))
            em.dma("gpsimd", lambda g_, W=W: g_.dma_start(out=wo_s, in_=W["wo"]))
            em.dma("sync", lambda s_, W=W: s_.dma_start(
                out=qsb[:], in_=W["qs"].broadcast_to([128, 64])))
            em.dma("sync", lambda s_, W=W: s_.dma_start(
                out=ksb[:], in_=W["ks"].broadcast_to([128, 64])))
            em.bar()
            for g in range(NG):
                t0 = g * G
                if g == 0:
                    em.op("vector", lambda v: v.memset(xTg[:, :, 0:1], 0.0))
                else:
                    em.op("vector", lambda v: v.tensor_copy(
                        xTg[:, :, 0:1], xTg[:, :, 512:513]))
                for i in range(G):
                    em.dma("sync", lambda s_, i=i, t0=t0: s_.dma_start(
                        out=xg[:, i, :], in_=xc_t[:, t0 + i, :]))
                for k in ("fqc", "fqs", "fkc", "fks"):
                    em.dma("sync", lambda s_, k=k, t0=t0: s_.dma_start(
                        out=rotg[k][:], in_=rot_d[k][:, t0:t0 + G, :]))
                em.bar()
                em.op("vector", lambda v: v.tensor_copy(
                    arB[:, 20488:24584], arF[:, 0:4096]))
                em.bar()
                emit_transpose_tiles(
                    em, [xbg[:, i, :] for i in range(G)],
                    lambda i, kc: xTg[:, kc, 1 + i * 128:1 + (i + 1) * 128],
                    iden[:], p_tr[:], G)
                for i in range(G):
                    def mmq(t, i=i):
                        r = None
                        for nb, pp in ((0, p_a[:, 0, :]), (1, p_a[:, 1, :]),
                                       (2, p_b[:, 0, :])):
                            for kc in range(8):
                                off = 1 if kc < 4 else 0
                                r = t.matmul(pp, xTg[:, kc, off + i * 128:off + (i + 1) * 128],
                                             wqkv_s[:, kc, nb * 512:(nb + 1) * 512],
                                             start=(kc == 0), stop=(kc == 7))
                        return r
                    em.op("tensor", mmq)
                    em.bar()
                    def evq(v, i=i):
                        v.tensor_copy(qkvf[:, i, 0:512], p_a[:, 0, :])
                        return v.tensor_copy(qkvf[:, i, 1024:1536], p_b[:, 0, :])
                    em.op("vector", evq)
                    em.op("scalar", lambda sc, i=i: sc.copy(
                        qkvf[:, i, 512:1024], p_a[:, 1, :]))
                    em.bar()
                for i in range(G):
                    em.op("scalar", lambda sc, i=i: sc.activation(
                        sq[:, 0:512], qkvf[:, i, 0:512], AF.Square))
                    em.op("vector", lambda v, i=i: v.tensor_copy(
                        vbf[:, i, :], qkvf[:, i, 1024:1536]))
                    em.bar()
                    em.op("scalar", lambda sc, i=i: sc.activation(
                        sq[:, 512:1024], qkvf[:, i, 512:1024], AF.Square))
                    em.op("vector", lambda v: v.tensor_reduce(
                        ss8[:, 0:8], sq[:, 0:512].rearrange("p (h d) -> p h d", d=64),
                        AX.X, ALU.add))
                    em.bar()
                    em.op("vector", lambda v: v.tensor_reduce(
                        ss8[:, 8:16], sq[:, 512:1024].rearrange("p (h d) -> p h d", d=64),
                        AX.X, ALU.add))
                    em.bar()
                    em.op("scalar", lambda sc: sc.activation(r8[:], ss8[:], AF.Sqrt))
                    em.bar()
                    em.op("vector", lambda v: v.tensor_scalar_max(r8[:], r8[:], 1e-12))
                    em.bar()
                    em.op("vector", lambda v: v.reciprocal(r8[:], r8[:]))
                    em.bar()
                    def nrm(v, i=i):
                        r = None
                        for h in range(8):
                            v.scalar_tensor_tensor(
                                qkvf[:, i, h * 64:(h + 1) * 64],
                                qkvf[:, i, h * 64:(h + 1) * 64],
                                r8[:, h:h + 1], qsb[:], ALU.mult, ALU.mult)
                            r = v.scalar_tensor_tensor(
                                qkvf[:, i, 512 + h * 64:512 + (h + 1) * 64],
                                qkvf[:, i, 512 + h * 64:512 + (h + 1) * 64],
                                r8[:, 8 + h:9 + h], ksb[:], ALU.mult, ALU.mult)
                        return r
                    em.op("vector", nrm)
                    em.bar()
                    for (o0, ck, sk, dst) in ((0, "fqc", "fqs", qrot),
                                              (512, "fkc", "fks", krot)):
                        q3 = qkvf[:, i, o0:o0 + 512].rearrange(
                            "p (h two d) -> p h two d", h=H, two=2)
                        o3 = dst[:, i, :].rearrange(
                            "p (h two d) -> p h two d", h=H, two=2)
                        cb = rotg[ck][:, i, :].unsqueeze(1).broadcast_to([128, H, 32])
                        sb = rotg[sk][:, i, :].unsqueeze(1).broadcast_to([128, H, 32])
                        t_a = rsc[:, 0, 0:256].rearrange("p (h d) -> p h d", h=H)
                        t_b = rsc[:, 1, 0:256].rearrange("p (h d) -> p h d", h=H)
                        def rt1(v, q3=q3, sb=sb, t_a=t_a, t_b=t_b):
                            v.tensor_mul(t_a, q3[:, :, 1, :], sb)
                            return v.tensor_mul(t_b, q3[:, :, 0, :], sb)
                        em.op("vector", rt1)
                        em.bar()
                        def rt2(v, q3=q3, cb=cb, o3=o3):
                            v.tensor_mul(o3[:, :, 0, :], q3[:, :, 0, :], cb)
                            return v.tensor_mul(o3[:, :, 1, :], q3[:, :, 1, :], cb)
                        em.op("vector", rt2)
                        em.bar()
                        def rt3(v, o3=o3, t_a=t_a, t_b=t_b):
                            v.tensor_sub(o3[:, :, 0, :], o3[:, :, 0, :], t_a)
                            return v.tensor_add(o3[:, :, 1, :], o3[:, :, 1, :], t_b)
                        em.op("vector", rt3)
                        em.bar()
                for i in range(G):
                    em.dma("sync", lambda s_, i=i, t0=t0: s_.dma_start(
                        out=vt_t[:, t0 + i, :], in_=vbf[:, i, :]))
                for nm, src, scrT in (("q", qrot, qT2), ("k", krot, kT2)):
                    emit_transpose_tiles(
                        em, [src[:, i, :] for i in range(G)],
                        lambda i, kc: arB[:, 30728 + (i * 4 + kc) * 128:
                                          30728 + (i * 4 + kc + 1) * 128],
                        iden[:], p_tr2[:], G, nkc=4)
                    for i in range(G):
                        for kc in range(4):
                            em.dma("sync", lambda s_, i=i, kc=kc, scrT=scrT, t0=t0:
                                   s_.dma_start(
                                       out=scrT[kc, :, (t0 + i) * 128:(t0 + i + 1) * 128],
                                       in_=arB[:, 30728 + (i * 4 + kc) * 128: 30728 + (i * 4 + kc + 1) * 128]))
                    em.bar()
            for w in range(8):
                em.dma("sync", lambda s_, w=w: s_.dma_start(
                    out=qTs, in_=qT2.rearrange("f p t -> p f t")[:, :, w * SEG:(w + 1) * SEG]))
                lo = max(0, (w - 1) * SEG)
                em.dma("sync", lambda s_, w=w, lo=lo: s_.dma_start(
                    out=kTs[:, :, (0 if w > 0 else SEG):],
                    in_=kT2.rearrange("f p t -> p f t")[:, :, lo:(w + 1) * SEG]))
                nti = 8 if w > 0 else 4
                for i in range(nti):
                    ti = ((w - 1) * 4 + i) if w > 0 else i
                    sl = i if w > 0 else (4 + i)
                    em.dma("sync", lambda s_, sl=sl, ti=ti: s_.dma_start(
                        out=vts[:, sl, :], in_=vt_t[:, ti, :]))
                em.bar()
                nk = 2 * SEG if w > 0 else SEG
                ko = 0 if w > 0 else SEG
                ntk = nk // 128
                for h in range(H):
                    hb = (h % 2) * 64
                    fc = h // 2
                    for r in range(2):
                        def mmsc(t, hb=hb, fc=fc, w=w, r=r):
                            rr = None
                            for j in range(2):
                                tqc = 2 * r + j
                                lhs = qTs[hb:hb + 64, fc, tqc * 128:(tqc + 1) * 128]
                                if w > 0:
                                    t.matmul(p_a[:, j, :], lhs,
                                             kTs[hb:hb + 64, fc, 0:512],
                                             start=True, stop=True)
                                rr = t.matmul(p_b[:, j, :], lhs,
                                              kTs[hb:hb + 64, fc, SEG:2 * SEG],
                                              start=True, stop=True)
                            return rr
                        em.op("tensor", mmsc)
                        em.bar()
                        def evsc(v, w=w, r=r):
                            if w > 0:
                                v.tensor_add(probs4[:, 2 * r:2 * r + 2, 0:512],
                                             p_a[:, :, :], mask_s[:, 2 * r:2 * r + 2, 0:512])
                            return v.tensor_add(probs4[:, 2 * r:2 * r + 2, 512:1024],
                                                p_b[:, :, :], mask_s[:, 2 * r:2 * r + 2, 512:1024])
                        em.op("vector", evsc)
                        em.bar()
                    em.op("vector", lambda v, ko=ko: v.tensor_reduce(
                        mx4[:], probs4[:, :, ko:1024], AX.X, ALU.max, negate=True))
                    em.bar()
                    def sfm1(v, ko=ko):
                        rr = None
                        for a in range(4):
                            rr = v.tensor_scalar(probs4[:, a, ko:1024],
                                                 probs4[:, a, ko:1024],
                                                 mx4[:, a:a + 1], None, ALU.add)
                        return rr
                    em.op("vector", sfm1)
                    em.bar()
                    em.op("scalar", lambda sc, ko=ko: sc.activation(
                        probs4[:, :, ko:1024], probs4[:, :, ko:1024], AF.Exp))
                    em.bar()
                    em.op("vector", lambda v, ko=ko: v.tensor_reduce(
                        den4[:], probs4[:, :, ko:1024], AX.X, ALU.add))
                    em.bar()
                    em.op("vector", lambda v: v.reciprocal(den4[:], den4[:]))
                    em.bar()
                    def sfm2(v, ko=ko):
                        rr = None
                        for a in range(4):
                            rr = v.tensor_scalar(probsb[:, a, ko:1024],
                                                 probs4[:, a, ko:1024],
                                                 den4[:, a:a + 1], None, ALU.mult)
                        return rr
                    em.op("vector", sfm2)
                    em.bar()
                    for tqc in range(4):
                        pp = p_tr if tqc % 2 == 0 else p_tr2
                        def trp(t, tqc=tqc, pp=pp, ko=ko, ntk=ntk):
                            rr = None
                            for tkc in range(ntk):
                                rr = t.transpose(
                                    pp[:, tkc * 128:(tkc + 1) * 128],
                                    probsb[:, tqc, ko + tkc * 128:ko + (tkc + 1) * 128],
                                    iden[:])
                            return rr
                        em.op("tensor", trp)
                        em.bar()
                        def evp(v, tqc=tqc, pp=pp, ntk=ntk):
                            return v.tensor_copy(
                                probsT[:, 0:ntk, tqc * 128:(tqc + 1) * 128],
                                pp[:, 0:ntk * 128].rearrange("p (a m) -> p a m", m=128))
                        em.op("vector", evp)
                        em.bar()
                    def mmav(t, h=h, w=w, ntk=ntk):
                        rr = None
                        for tkc in range(ntk):
                            vtile = vts[:, (tkc if w > 0 else 4 + tkc), h * 64:(h + 1) * 64]
                            rr = t.matmul(p_av[:], vtile, probsT[:, tkc, :],
                                          start=(tkc == 0), stop=(tkc == ntk - 1))
                        return rr
                    em.op("tensor", mmav)
                    em.bar()
                    em.op("vector", lambda v, hb=hb, fc=fc: v.tensor_copy(
                        oTs[hb:hb + 64, fc, :], p_av[:]))
                    em.bar()
                for i in range(4):
                    em.dma("sync", lambda s_, w=w, i=i: s_.dma_start(
                        out=xt_c, in_=xc_t[:, w * 4 + i, :]))
                    def mmo(t, i=i):
                        for kc in range(4):
                            t.matmul(p_a[:, 0, :], oTs[:, kc, i * 128:(i + 1) * 128],
                                     wo_s[:, kc, 0:512], start=(kc == 0), stop=(kc == 3))
                        rr = None
                        for kc in range(4):
                            rr = t.matmul(p_b[:, 0, :], oTs[:, kc, i * 128:(i + 1) * 128],
                                          wo_s[:, kc, 512:1024], start=(kc == 0), stop=(kc == 3))
                        return rr
                    em.op("tensor", mmo)
                    em.bar()
                    def evo(v):
                        v.tensor_add(xt_c[:, 0:512], p_a[:, 0, :], xt_c[:, 0:512])
                        return v.tensor_add(xt_c[:, 512:1024], p_b[:, 0, :], xt_c[:, 512:1024])
                    em.op("vector", evo)
                    em.bar()
                    em.dma("sync", lambda s_, w=w, i=i: s_.dma_start(
                        out=xc_t[:, w * 4 + i, :], in_=xt_c))
                    em.bar()
            em.dma("gpsimd", lambda g_, W=W: g_.dma_start(out=w1x_s, in_=W["w1a"]))
            em.dma("sync", lambda s_, W=W: s_.dma_start(
                out=b1ab, in_=W["b1a"].broadcast_to([128, 2730])))
            em.dma("sync", lambda s_, W=W: s_.dma_start(
                out=lngb[:], in_=W["lng"].broadcast_to([128, D])))
            em.dma("sync", lambda s_, W=W: s_.dma_start(
                out=lnbb[:], in_=W["lnb"].broadcast_to([128, D])))
            em.bar()
            for phase in (1, 2):
                if phase == 2:
                    em.dma("gpsimd", lambda g_, W=W: g_.dma_start(out=w1x_s, in_=W["w1g"]))
                    em.dma("gpsimd", lambda g_, W=W: g_.dma_start(out=w2p_s, in_=W["w2p"]))
                    em.dma("sync", lambda s_, W=W: s_.dma_start(
                        out=b1ab, in_=W["b1g"].broadcast_to([128, 2730])))
                    em.dma("sync", lambda s_, W=W: s_.dma_start(
                        out=b2b[:], in_=W["b2"].broadcast_to([128, D])))
                    em.op("vector", lambda v: v.memset(actb[:, 2730:2816], 0.0))
                    em.bar()
                for t_i in range(NT):
                    if phase == 1:
                        em.dma("sync", lambda s_, t_i=t_i: s_.dma_start(
                            out=hx[:, 0:512], in_=xc_t[:, t_i, 0:512]))
                        if t_i == 0:
                            em.op("vector", lambda v: v.memset(hx[0:1, 512:1024], 0.0))
                            em.dma("sync", lambda s_: s_.dma_start(
                                out=hx[1:128, 512:1024], in_=xcur[0:127, 512:1024]))
                        else:
                            em.dma("sync", lambda s_, t_i=t_i: s_.dma_start(
                                out=hx[:, 512:1024],
                                in_=xcur[t_i * 128 - 1:t_i * 128 + 127, 512:1024]))
                        em.bar()
                        emit_ln_group(em, [hx], [hb_c], [arF[:, 6484:7508]],
                                      sq[:], cols, D, 1,
                                      gain=lngb[:], bias=lnbb[:])
                        emit_transpose_tiles(em, [hb_c], lambda i, kc, t_i=t_i:
                                             hT_c1[:, kc, :], iden[:], p_tr[:], 1)
                        em.dma("sync", lambda s_, t_i=t_i: s_.dma_start(
                            out=hT2.rearrange("k p t -> p k t")[:, :, t_i * 128:(t_i + 1) * 128],
                            in_=hT_c1[:, :, :]))
                        hTv = hT_c1
                    else:
                        em.dma("sync", lambda s_, t_i=t_i: s_.dma_start(
                            out=hT_c2[:, :, :],
                            in_=hT2.rearrange("k p t -> p k t")[:, :, t_i * 128:(t_i + 1) * 128]))
                        em.dma("sync", lambda s_, t_i=t_i: s_.dma_start(
                            out=h3a_s, in_=h3a_t2[:, t_i, :]))
                        em.dma("sync", lambda s_, t_i=t_i: s_.dma_start(
                            out=xt_c, in_=xc_t[:, t_i, :]))
                        em.bar()
                        hTv = hT_c2
                    dstf = h3a_s if phase == 1 else h3g_s
                    for cpair in range(3):
                        def mmf(t, cpair=cpair, hTv=hTv):
                            rr = None
                            for j in range(2):
                                nb = cpair * 2 + j
                                nw = 512 if nb < 5 else 170
                                no = nb * 512
                                pp = (p_a if j == 0 else p_b)[:, 0, 0:nw]
                                for kc in range(8):
                                    rr = t.matmul(pp, hTv[:, kc, :],
                                                  w1x_s[:, kc, no:no + nw],
                                                  start=(kc == 0), stop=(kc == 7))
                            return rr
                        em.op("tensor", mmf)
                        em.bar()
                        def evf(v, cpair=cpair, dstf=dstf):
                            rr = None
                            for j in range(2):
                                nb = cpair * 2 + j
                                nw = 512 if nb < 5 else 170
                                no = nb * 512
                                pp = (p_a if j == 0 else p_b)[:, 0, 0:nw]
                                rr = v.tensor_add(dstf[:, no:no + nw], pp,
                                                  b1ab[:, no:no + nw])
                            return rr
                        em.op("vector", evf)
                        em.bar()
                    if phase == 1:
                        em.dma("sync", lambda s_, t_i=t_i: s_.dma_start(
                            out=h3a_t2[:, t_i, :], in_=h3a_s))
                        em.bar()
                        continue
                    em.op("vector", lambda v: v.tensor_mul(h3a_s, h3a_s, h3g_s))
                    em.bar()
                    em.op("scalar", lambda sc: sc.activation(
                        h3g_s, h3g_s, AF.Erf, scale=0.7071067811865476))
                    em.bar()
                    em.op("vector", lambda v: v.tensor_mul(h3g_s, h3a_s, h3g_s))
                    em.bar()
                    em.op("vector", lambda v: v.tensor_add(h3g_s, h3g_s, h3a_s))
                    em.bar()
                    em.op("vector", lambda v: v.tensor_scalar(
                        actb[:, 0:2730], h3g_s, 0.5, None, ALU.mult))
                    em.bar()
                    for bb, pp in ((0, p_tr), (1, p_tr2), (2, p_tr)):
                        nch = 8 if bb < 2 else 6
                        def trf(t, bb=bb, pp=pp, nch=nch):
                            rr = None
                            for c in range(nch):
                                kc = bb * 8 + c
                                rr = t.transpose(pp[:, c * 128:(c + 1) * 128],
                                                 actb[:, kc * 128:(kc + 1) * 128],
                                                 iden[:])
                            return rr
                        em.op("tensor", trf)
                        em.bar()
                        def evt(v, bb=bb, pp=pp, nch=nch):
                            return v.tensor_copy(
                                actT[:, bb * 8:bb * 8 + nch, :],
                                pp[:, 0:nch * 128].rearrange("p (a m) -> p a m", m=128))
                        em.op("vector", evt)
                        em.bar()
                    def mmy(t):
                        for kc in range(22):
                            t.matmul(p_a[:, 0, :], actT[:, kc, :], w2p_s[:, kc, 0:512],
                                     start=(kc == 0), stop=(kc == 21))
                        rr = None
                        for kc in range(22):
                            rr = t.matmul(p_b[:, 0, :], actT[:, kc, :], w2p_s[:, kc, 512:1024],
                                          start=(kc == 0), stop=(kc == 21))
                        return rr
                    em.op("tensor", mmy)
                    em.bar()
                    def evy(v):
                        v.tensor_add(rsc[:, 0, :], p_a[:, 0, :], b2b[:, 0:512])
                        return v.tensor_add(rsc[:, 1, :], p_b[:, 0, :], b2b[:, 512:1024])
                    em.op("vector", evy)
                    em.bar()
                    em.op("vector", lambda v: v.tensor_add(
                        xt_c, xt_c, rsc[:].rearrange("p a d -> p (a d)")))
                    em.bar()
                    em.dma("sync", lambda s_, t_i=t_i: s_.dma_start(
                        out=xc_t[:, t_i, :], in_=xt_c))
                    em.bar()

        for g in range(NG):
            t0 = g * G
            for i in range(G):
                em.dma("sync", lambda s_, i=i, t0=t0: s_.dma_start(
                    out=xg[:, i, :], in_=xc_t[:, t0 + i, :]))
                em.dma("sync", lambda s_, i=i, t0=t0: s_.dma_start(
                    out=qkvf[:, i, 512:1536], in_=x_t[:, t0 + i, :]))
            em.bar()
            emit_ln_group(em, [xg[:, i, :] for i in range(G)],
                          [xg[:, i, :] for i in range(G)],
                          [xg[:, i, :] for i in range(G)], sq[:], cols, D, G)
            em.op("vector", lambda v: v.tensor_add(
                xg[:, :, :], xg[:, :, :], qkvf[:, 0:G, 512:1536]))
            em.bar()
            for i in range(G):
                em.dma("sync", lambda s_, i=i, t0=t0: s_.dma_start(
                    out=y_t[:, t0 + i, :], in_=xg[:, i, :]))
            em.bar()



def run_l0_ffn(x_full, wqkv, q_scale, k_scale, wo, ln_g, ln_b, w1, b1, w2, b2):
    if "l0f" not in _CACHE:
        _CACHE["l0f"] = build_l0_ffn()
    nc = _CACHE["l0f"]
    consts = {"iden": _ident_bf16(), "mask": _l0ffn_mask(), **_l0ffn_xpos()}
    for l in range(2):
        w2p = np.zeros((2816, 1024), np.float32)
        w2p[0:2730] = np.asarray(w2[l])
        consts.update({
            f"wqkv{l}": _wtile(np.asarray(wqkv[l])),
            f"wo{l}": _wtile(np.asarray(wo[l])),
            f"w1a{l}": _wtile(np.ascontiguousarray(np.asarray(w1[l])[:, 0:2730])),
            f"w1g{l}": _wtile(np.ascontiguousarray(np.asarray(w1[l])[:, 2730:])),
            f"w2p{l}": _wtile(w2p),
            f"qs{l}": np.asarray(q_scale[l]).reshape(1, 64),
            f"ks{l}": np.asarray(k_scale[l]).reshape(1, 64),
            f"lng{l}": np.asarray(ln_g[l]).reshape(1, D),
            f"lnb{l}": np.asarray(ln_b[l]).reshape(1, D),
            f"b1a{l}": np.ascontiguousarray(np.asarray(b1[l])[0:2730]).reshape(1, 2730),
            f"b1g{l}": np.ascontiguousarray(np.asarray(b1[l])[2730:]).reshape(1, 2730),
            f"b2{l}": np.asarray(b2[l]).reshape(1, D),
        })
    in_maps = [{"x": np.ascontiguousarray(x_full[c]), **consts}
               for c in range(NCORES)]
    res = run_bass_kernel_spmd(nc, in_maps, core_ids=list(range(NCORES)))
    return np.stack([np.asarray(r["y"]) for r in res.results])



def _l0attn_consts():
    ut = np.triu(np.ones((128, 128), np.float32))
    trilS = np.tril(np.ones((128, 128), np.float32), -1)
    triuS = np.triu(np.ones((128, 128), np.float32), 1)
    triuI = np.triu(np.ones((128, 128), np.float32))
    idf = np.eye(128, dtype=np.float32)
    ones1 = np.ones((1, 128), np.float32)
    ones128 = np.ones((128, 128), np.float32)
    return {"utones": ut, "trilS": trilS, "triuS": triuS, "triuI": triuI,
            "idf32": idf, "ones1": ones1, "ones128": ones128}

def emit_l0_attn(nc, em, pfx, x_in, y_out, dd):
    SB = lambda n, s, d: nc.sbuf_tensor(pfx + n, s, d)
    PS = lambda n, s, d: nc.psum_tensor(pfx + n, s, d)
    DR = lambda n, s, d: nc.dram_tensor(pfx + n, s, d).ap()
    Win_d = dd["Win"]; Wctrl_d = dd["Wctrl"]; Wout_d = dd["Wout"]
    bctrl_d = dd["bctrl"]; bout_d = dd["bout"]; ramp_d = dd["ramp"]
    wconv_d = dd["wconv"]; iden_d = dd["iden"]
    cst_d = {k: dd[k] for k in _l0attn_consts()}
    qn2 = DR("qn2", [S, D], BF16)
    kn2 = DR("kn2", [S, D], BF16)
    vn2 = DR("vn2", [S, D], BF16)
    og2 = DR("og2", [S, D], F32)
    rd2 = DR("rd2", [S, D], F32)

    x_t = x_in.rearrange("(t p) d -> p t d", p=PT)
    y_t = y_out.rearrange("(t p) d -> p t d", p=PT)
    qn_t = qn2.rearrange("(t p) d -> p t d", p=PT)
    kn_t = kn2.rearrange("(t p) d -> p t d", p=PT)
    vn_t = vn2.rearrange("(t p) d -> p t d", p=PT)
    og_t = og2.rearrange("(t p) d -> p t d", p=PT)
    rd_t = rd2.rearrange("(t p) d -> p t d", p=PT)

    G = 4
    NG = NT // G

    with ExitStack() as ctx:
        E = ctx.enter_context
        iden = E(SB("iden_s", [128, 128], BF16))
        cst = {k: E(SB(f"{k}_s", list(v.shape), F32))
               for k, v in _l0attn_consts().items()}
        eps_c = E(SB("eps_c", [128, 1], F32))
        eps6 = E(SB("eps6", [128, 1], F32))
        sq = E(SB("sq", [128, D], F32))
        cols = {"s": E(SB("ln_s", [128, G], F32)),
                "ss": E(SB("ln_ss", [128, G], F32)),
                "r": E(SB("ln_r", [128, G], F32)),
                "eps": eps_c}
        Wbig = E(SB("Wbig", [128, 8, 1024], BF16))
        Wctrl_s = E(SB("Wctrl_s", [128, 8, 1032], BF16))
        wcv = E(SB("wcv", [128, 8, 12], F32))
        bctrl_b = E(SB("bctrl_b", [128, 1032], F32))
        bout_b = E(SB("bout_b", [128, D], F32))
        rampb = E(SB("rampb", [128, D], F32))
        stats = E(SB("stats", [128, NT, 12], F32))
        ctf = E(SB("ctf", [128, 1032], F32))
        ss4 = E(SB("ss4", [128, 4], F32))
        r4n = E(SB("r4n", [128, 4], F32))
        bnd = E(SB("bnd", [128, 3, 8, 3], F32))
        lgc = E(SB("lgc", [128, 4], F32))
        g4 = E(SB("g4", [128, 4], F32))
        rc4 = E(SB("rc4", [128, 4], F32))
        gC4 = E(SB("gC4", [128, 4], F32))
        lgT = E(SB("lgT", [4, 128], F32))
        nlgT = E(SB("nlgT", [4, 128], F32))
        S4 = E(SB("S4", [128, 4, 2, 256], F32))
        S4b = E(SB("S4b", [128, 4, 2, 256], BF16))
        AF1 = E(SB("AF1", [128, 4096], F32))
        AF2 = E(SB("AF2", [128, 4096], F32))
        AF3 = E(SB("AF3", [128, 4120], F32))
        AF4 = E(SB("AF4", [128, 4096], F32))
        AB1 = E(SB("AB1", [128, 4096], BF16))
        AB2 = E(SB("AB2", [128, 4096], BF16))
        AB3 = E(SB("AB3", [128, 4096], BF16))
        AB4 = E(SB("AB4", [128, 6144], BF16))
        xg = AF1[:].rearrange("p (i d) -> p i d", d=D)
        xcg = AF2[:].rearrange("p (i d) -> p i d", d=D)
        fmi = AF3[:, 0:4120].rearrange("p (a b) -> p a b", b=515)
        fmo = AF4[:].rearrange("p (a b) -> p a b", b=512)
        xlnb = AB1[:].rearrange("p (i d) -> p i d", d=D)
        xT = AB2[:].rearrange("p (k m) -> p k m", m=512)
        fmb = AB3[:].rearrange("p (a b) -> p a b", b=512)
        tkb = AB4[:, 0:4096].rearrange("p (i d) -> p i d", d=D)
        tkf = AF2[:].rearrange("p (i d) -> p i d", d=D)
        X4 = AF1[:, 0:512].rearrange("p (a b) -> p a b", b=128)
        X2_4 = AF1[:, 512:1024].rearrange("p (a b) -> p a b", b=128)
        r4s = AF1[:, 1024:1536].rearrange("p (a b) -> p a b", b=128)
        rT4 = AF1[:, 1536:2048].rearrange("p (a b) -> p a b", b=128)
        KK4 = AF1[:, 2048:2560].rearrange("p (a b) -> p a b", b=128)
        KQ4 = AF1[:, 2560:3072].rearrange("p (a b) -> p a b", b=128)
        scr4 = AF1[:, 3072:3584].rearrange("p (a b) -> p a b", b=128)
        scrM = AF1[:, 3584:4096]
        Ub4 = AF2[:, 0:1024].rearrange("p (a b) -> p a b", b=256)
        O14 = AF2[:, 1024:2048].rearrange("p (a b) -> p a b", b=256)
        rdt = AF2[:, 2048:3072]
        Qt_c = AB1[:, 0:1024]
        Kt_c = AB1[:, 1024:2048]
        Vt_c = AB1[:, 2048:3072]
        Kg_c = AB1[:, 3072:4096]
        Ks_c = AB2[:, 0:1024]
        KtT = AB2[:, 1024:2048].rearrange("p (a b m) -> p a b m", b=2, m=128)
        QtT = AB2[:, 2048:3072].rearrange("p (a b m) -> p a b m", b=2, m=128)
        Wt4 = AB2[:, 3072:4096].rearrange("p (a b m) -> p a b m", b=2, m=128)
        Ns4 = AB3[:, 0:512].rearrange("p (a b) -> p a b", b=128)
        Nt4 = AB3[:, 512:1024].rearrange("p (a b) -> p a b", b=128)
        Ms4 = AB3[:, 1024:1536].rearrange("p (a b) -> p a b", b=128)
        Mt4 = AB3[:, 1536:2048].rearrange("p (a b) -> p a b", b=128)
        Ps4 = AB3[:, 2048:2560].rearrange("p (a b) -> p a b", b=128)
        PtP4 = AB3[:, 2560:3072].rearrange("p (a b) -> p a b", b=128)
        At4 = AB3[:, 3072:3584].rearrange("p (a b) -> p a b", b=128)
        Pb4 = AB3[:, 3584:4096].rearrange("p (a b) -> p a b", b=128)
        dU4 = AB4[:, 4096:5120].rearrange("p (a b) -> p a b", b=256)
        ogt = AF3[:, 0:4096].rearrange("p (i d) -> p i d", d=D)
        rdn = AB1[:].rearrange("p (i d) -> p i d", d=D)
        rdT = AB2[:].rearrange("p (k m) -> p k m", m=512)
        p1 = E(PS("p1", [128, 512], F32))
        p2 = E(PS("p2", [128, 512], F32))
        p3 = E(PS("p3", [128, 1024], F32))
        p4 = E(PS("p4", [128, 1024], F32))
        ptr = E(PS("ptr", [128, 1024], BF16))
        pcs = E(PS("pcs", [128, 16], F32))


        em.dma("sync", lambda s_: s_.dma_start(out=iden[:], in_=iden_d))
        for k in cst:
            em.dma("sync", lambda s_, k=k: s_.dma_start(out=cst[k][:], in_=cst_d[k]))
        em.dma("gpsimd", lambda g_: g_.dma_start(out=Wctrl_s[:], in_=Wctrl_d))
        em.dma("sync", lambda s_: s_.dma_start(out=wcv[:], in_=wconv_d))
        em.dma("sync", lambda s_: s_.dma_start(
            out=bctrl_b[:], in_=bctrl_d.broadcast_to([128, 1032])))
        em.op("vector", lambda v: v.memset(eps_c[:], 1e-5))
        em.op("scalar", lambda sc: sc.memzero(eps6[:]))
        em.bar()
        em.op("vector", lambda v: v.memset(eps6[:], 1e-6))
        em.bar()

        for g in range(NG):
            t0 = g * G
            for i in range(G):
                em.dma("sync", lambda s_, i=i, t0=t0: s_.dma_start(
                    out=xg[:, i, :], in_=x_t[:, t0 + i, :]))
            em.bar()
            emit_ln_group(em, [xg[:, i, :] for i in range(G)],
                          [xlnb[:, i, :] for i in range(G)],
                          [xcg[:, i, :] for i in range(G)], sq[:], cols, D, G)
            emit_transpose_tiles(
                em, [xlnb[:, i, :] for i in range(G)],
                lambda i, kc: xT[:, kc, i * 128:(i + 1) * 128], iden[:],
                ptr[:], G)
            for i in range(G):
                def mmc(t, i=i):
                    r = None
                    for no, nw, pp in ((0, 512, p1[:, 0:512]), (512, 512, p2[:, 0:512]),
                                       (1024, 8, pcs[:, 0:8])):
                        for kc in range(8):
                            r = t.matmul(pp, xT[:, kc, i * 128:(i + 1) * 128],
                                         Wctrl_s[:, kc, no:no + nw],
                                         start=(kc == 0), stop=(kc == 7))
                    return r
                em.op("tensor", mmc)
                em.bar()
                def evc(v):
                    v.tensor_add(ctf[:, 0:512], p1[:], bctrl_b[:, 0:512])
                    v.tensor_add(ctf[:, 512:1024], p2[:], bctrl_b[:, 512:1024])
                    return v.tensor_add(ctf[:, 1024:1032], pcs[:, 0:8],
                                        bctrl_b[:, 1024:1032])
                em.op("vector", evc)
                em.bar()
                em.op("scalar", lambda sc: sc.activation(
                    ctf[:, 0:8], ctf[:, 0:8], AF.Sigmoid))
                em.bar()
                def sg2(v, i=i, t0=t0):
                    v.tensor_scalar(ctf[:, 0:4], ctf[:, 0:4], 0.98, 0.01,
                                    ALU.mult, ALU.add)
                    v.tensor_copy(stats[:, t0 + i, 4:8], ctf[:, 4:8])
                    return v.tensor_scalar(stats[:, t0 + i, 8:12], ctf[:, 4:8],
                                           -1.0, None, ALU.mult)
                em.op("vector", sg2)
                em.bar()
                em.op("scalar", lambda sc, i=i, t0=t0: sc.activation(
                    stats[:, t0 + i, 0:4], ctf[:, 0:4], AF.Ln))
                em.bar()
                em.op("scalar", lambda sc: sc.activation(
                    ctf[:, 8:1032], ctf[:, 8:1032], AF.Sigmoid))
                em.bar()
                em.dma("sync", lambda s_, i=i, t0=t0: s_.dma_start(
                    out=og_t[:, t0 + i, :], in_=ctf[:, 8:1032]))
            em.bar()
            for ti, scr_t in ((0, qn_t), (1, kn_t), (2, vn_t)):
                em.dma("gpsimd", lambda g_, ti=ti: g_.dma_start(
                    out=Wbig[:], in_=Win_d[:, :, ti * 1024:(ti + 1) * 1024]))
                if g == 0:
                    em.op("vector", lambda v: v.memset(fmi[:, :, 0:3], 0.0))
                else:
                    em.op("vector", lambda v, ti=ti: v.tensor_copy(
                        fmi[:, :, 0:3], bnd[:, ti, :, :]))
                em.bar()
                for fc in range(8):
                    pp = p3 if fc % 2 == 0 else p4
                    def mmf(t, fc=fc, pp=pp):
                        r = None
                        for kc in range(8):
                            r = t.matmul(pp[:, 0:512],
                                         Wbig[:, kc, fc * 128:(fc + 1) * 128],
                                         xT[:, kc, :], start=(kc == 0), stop=(kc == 7))
                        return r
                    em.op("tensor", mmf)
                    em.bar()
                    em.op("vector", lambda v, fc=fc, pp=pp: v.tensor_copy(
                        fmi[:, fc, 3:515], pp[:, 0:512]))
                    em.bar()
                def cva(v, ti=ti):
                    r = None
                    for kc in range(8):
                        r = v.tensor_scalar(fmo[:, kc, :], fmi[:, kc, 0:512],
                                            wcv[:, kc, ti * 4:ti * 4 + 1], None, ALU.mult)
                    return r
                em.op("vector", cva)
                em.bar()
                for j in range(1, 4):
                    def cvb(v, j=j, ti=ti):
                        r = None
                        for kc in range(8):
                            r = v.scalar_tensor_tensor(
                                fmo[:, kc, :], fmi[:, kc, j:j + 512],
                                wcv[:, kc, ti * 4 + j:ti * 4 + j + 1],
                                fmo[:, kc, :], ALU.mult, ALU.add)
                        return r
                    em.op("vector", cvb)
                    em.bar()
                em.op("vector", lambda v, ti=ti: v.tensor_copy(
                    bnd[:, ti, :, :], fmi[:, :, 512:515]))
                em.op("scalar", lambda sc: sc.activation(
                    AB3[:, 0:4096], AF4[:, 0:4096], AF.Silu))
                em.bar()
                for i in range(G):
                    def trb(t, i=i):
                        r = None
                        for fc in range(8):
                            r = t.transpose(ptr[:, fc * 128:(fc + 1) * 128],
                                            fmb[:, fc, i * 128:(i + 1) * 128], iden[:])
                        return r
                    em.op("tensor", trb)
                    em.bar()
                    em.op("vector", lambda v, i=i: v.tensor_copy(tkf[:, i, :], ptr[:]))
                    em.bar()
                for i in range(G):
                    if ti < 2:
                        em.op("scalar", lambda sc, i=i: sc.activation(
                            sq[:], tkf[:, i, :], AF.Square))
                        em.bar()
                        em.op("vector", lambda v: v.tensor_reduce(
                            ss4[:], sq[:].rearrange("p (h d) -> p h d", d=256),
                            AX.X, ALU.add))
                        em.bar()
                        em.op("scalar", lambda sc: sc.activation(
                            r4n[:], ss4[:], AF.Sqrt))
                        em.bar()
                        em.op("vector", lambda v: v.tensor_scalar_max(
                            r4n[:], r4n[:], 1e-6))
                        em.bar()
                        em.op("vector", lambda v: v.reciprocal(r4n[:], r4n[:]))
                        em.bar()
                    def apn(v, i=i, ti=ti):
                        r = None
                        for h in range(4):
                            r = v.tensor_scalar(tkb[:, i, h * 256:(h + 1) * 256],
                                                tkf[:, i, h * 256:(h + 1) * 256],
                                                r4n[:, h:h + 1] if ti < 2 else 1.0,
                                                None, ALU.mult)
                        return r
                    em.op("vector", apn)
                    em.bar()
                    em.dma("sync", lambda s_, i=i, t0=t0, scr_t=scr_t: s_.dma_start(
                        out=scr_t[:, t0 + i, :], in_=tkb[:, i, :]))
                em.bar()

        em.op("vector", lambda v: v.memset(S4[:].rearrange("p a b c -> p (a b c)"), 0.0))
        em.bar()
        for c in range(NT):
            em.dma("sync", lambda s_, c=c: s_.dma_start(out=Qt_c, in_=qn_t[:, c, :]))
            em.dma("sync", lambda s_, c=c: s_.dma_start(out=Kt_c, in_=kn_t[:, c, :]))
            em.dma("sync", lambda s_, c=c: s_.dma_start(out=Vt_c, in_=vn_t[:, c, :]))
            em.op("tensor", lambda t, c=c: t.matmul(
                pcs[:, 0:4], cst["utones"][:], stats[:, c, 0:4], start=True, stop=True))
            em.bar()
            em.op("vector", lambda v: v.tensor_copy(lgc[:], pcs[:, 0:4]))
            em.bar()
            em.op("scalar", lambda sc: sc.activation(g4[:], lgc[:], AF.Exp))
            def bxr(v, c=c):
                r = None
                for h in range(4):
                    r = v.tensor_scalar(scr4[:, h, :], cst["triuI"][:],
                                        stats[:, c, h:h + 1], None, ALU.mult)
                return r
            em.op("vector", bxr)
            em.bar()
            def mmx(t):
                r = None
                for h in range(4):
                    r = t.matmul(p3[:, h * 128:(h + 1) * 128], cst["ones128"][:],
                                 scr4[:, h, :], start=True, stop=True)
                return r
            em.op("tensor", mmx)
            em.bar()
            em.op("vector", lambda v: v.tensor_copy(
                X4[:].rearrange("p a b -> p (a b)"), p3[:, 0:512]))
            em.bar()
            def bx2(v):
                r = None
                for h in range(4):
                    v.tensor_scalar(X2_4[:, h, :], X4[:, h, :], lgc[:, h:h + 1],
                                    0.0, ALU.subtract, ALU.min)
                    r = v.tensor_scalar(scr4[:, h, :], X4[:, h, :], lgc[:, h:h + 1],
                                        0.0, ALU.subtract, ALU.max)
                return r
            em.op("vector", bx2)
            em.bar()
            def evx2(sc):
                sc.activation(rT4[:].rearrange("p a b -> p (a b)"),
                              X2_4[:].rearrange("p a b -> p (a b)"), AF.Exp)
                return sc.activation(r4s[:].rearrange("p a b -> p (a b)"),
                                     scr4[:].rearrange("p a b -> p (a b)"),
                                     AF.Exp, scale=-1.0)
            em.op("scalar", evx2)
            em.bar()
            def rcg(v):
                r = None
                for h in range(4):
                    r = v.tensor_copy(rc4[:, h:h + 1], rT4[:, h, 127:128])
                return r
            em.op("vector", rcg)
            em.bar()
            em.op("vector", lambda v: v.tensor_mul(gC4[:], g4[:], rc4[:]))
            em.bar()
            def kgs(v):
                r = None
                for h in range(4):
                    v.tensor_scalar(Kg_c[:, h * 256:(h + 1) * 256],
                                    Kt_c[:, h * 256:(h + 1) * 256],
                                    g4[:, h:h + 1], None, ALU.mult)
                    r = v.tensor_scalar(Ks_c[:, h * 256:(h + 1) * 256],
                                        Kt_c[:, h * 256:(h + 1) * 256],
                                        rc4[:, h:h + 1], None, ALU.mult)
                return r
            em.op("vector", kgs)
            em.bar()
            def trk(t):
                r = None
                for z in range(8):
                    h, kc2 = divmod(z, 2)
                    r = t.transpose(ptr[:, z * 128:(z + 1) * 128],
                                    Kt_c[h * 0:128, h * 256 + kc2 * 128:h * 256 + (kc2 + 1) * 128]
                                    if False else Kt_c[:, h * 256 + kc2 * 128:h * 256 + (kc2 + 1) * 128].rearrange("p m -> p m"),
                                    iden[:])
                return r
            def trk2(t):
                r = None
                for z in range(8):
                    h, kc2 = divmod(z, 2)
                    r = t.transpose(
                        ptr[:, z * 128:(z + 1) * 128],
                        AB1[:, 1024 + h * 256 + kc2 * 128:1024 + h * 256 + (kc2 + 1) * 128],
                        iden[:])
                return r
            em.op("tensor", trk2)
            em.bar()
            em.op("vector", lambda v: v.tensor_copy(AB2[:, 1024:2048], ptr[:]))
            em.bar()
            def trq2(t):
                r = None
                for z in range(8):
                    r = t.transpose(ptr[:, z * 128:(z + 1) * 128],
                                    AB1[:, z * 128:(z + 1) * 128], iden[:])
                return r
            em.op("tensor", trq2)
            em.bar()
            em.op("vector", lambda v: v.tensor_copy(AB2[:, 2048:3072], ptr[:]))
            em.bar()
            def mmg(t):
                r = None
                for h in range(4):
                    for kc2 in range(2):
                        t.matmul(p1[:, h * 128:(h + 1) * 128], KtT[:, h, kc2, :],
                                 KtT[:, h, kc2, :], start=(kc2 == 0), stop=(kc2 == 1))
                    for kc2 in range(2):
                        r = t.matmul(p2[:, h * 128:(h + 1) * 128], KtT[:, h, kc2, :],
                                     QtT[:, h, kc2, :], start=(kc2 == 0), stop=(kc2 == 1))
                return r
            em.op("tensor", mmg)
            em.bar()
            def evg(v):
                v.tensor_copy(KK4[:].rearrange("p a b -> p (a b)"), p1[:])
                return v.tensor_copy(KQ4[:].rearrange("p a b -> p (a b)"), p2[:])
            em.op("vector", evg)
            em.bar()
            def bbr(v, c=c):
                r = None
                for h in range(4):
                    r = v.tensor_scalar(scr4[:, h, :], cst["idf32"][:],
                                        stats[:, c, 4 + h:5 + h], None, ALU.mult)
                return r
            em.op("vector", bbr)
            em.bar()
            def mmbr(t):
                r = None
                for h in range(4):
                    r = t.matmul(p3[:, h * 128:(h + 1) * 128], cst["ones128"][:],
                                 scr4[:, h, :], start=True, stop=True)
                return r
            em.op("tensor", mmbr)
            em.bar()
            def ns1(v, c=c):
                v.tensor_mul(scr4[:].rearrange("p a b -> p (a b)"),
                             KK4[:].rearrange("p a b -> p (a b)"),
                             r4s[:].rearrange("p a b -> p (a b)"))
                r = None
                for h in range(4):
                    r = v.tensor_scalar(scr4[:, h, :], scr4[:, h, :],
                                        stats[:, c, 8 + h:9 + h], None, ALU.mult)
                return r
            em.op("vector", ns1)
            em.bar()
            def ns2(v):
                r = None
                for h in range(4):
                    r = v.tensor_mul(Ns4[:, h, :], scr4[:, h, :], cst["trilS"][:])
                return r
            em.op("vector", ns2)
            em.bar()
            def nt1(v):
                v.tensor_mul(scr4[:].rearrange("p a b -> p (a b)"),
                             KK4[:].rearrange("p a b -> p (a b)"),
                             rT4[:].rearrange("p a b -> p (a b)"))
                return v.tensor_mul(scr4[:].rearrange("p a b -> p (a b)"),
                                    scr4[:].rearrange("p a b -> p (a b)"),
                                    p3[:, 0:512])
            em.op("vector", nt1)
            em.bar()
            def nt2(v):
                r = None
                for h in range(4):
                    v.tensor_scalar(scr4[:, h, :], scr4[:, h, :], -1.0, None, ALU.mult)
                    r = v.tensor_mul(Nt4[:, h, :], scr4[:, h, :], cst["triuS"][:])
                return r
            em.op("vector", nt2)
            em.bar()
            def m0b(v):
                r = None
                for h in range(4):
                    v.tensor_add(Ms4[:, h, :], Ns4[:, h, :], iden[:])
                    r = v.tensor_add(Mt4[:, h, :], Nt4[:, h, :], iden[:])
                return r
            em.op("vector", m0b)
            em.bar()
            def p0c(v):
                v.tensor_copy(Ps4[:].rearrange("p a b -> p (a b)"),
                              Ns4[:].rearrange("p a b -> p (a b)"))
                return v.tensor_copy(PtP4[:].rearrange("p a b -> p (a b)"),
                                     Nt4[:].rearrange("p a b -> p (a b)"))
            em.op("vector", p0c)
            em.bar()
            for lvl in range(6):
                def mmp(t):
                    r = None
                    for h in range(4):
                        t.matmul(p1[:, h * 128:(h + 1) * 128], PtP4[:, h, :],
                                 Ps4[:, h, :], start=True, stop=True)
                        r = t.matmul(p2[:, h * 128:(h + 1) * 128], Ps4[:, h, :],
                                     PtP4[:, h, :], start=True, stop=True)
                    return r
                em.op("tensor", mmp)
                em.bar()
                def evp2(v):
                    v.tensor_copy(Ps4[:].rearrange("p a b -> p (a b)"), p1[:])
                    return v.tensor_copy(PtP4[:].rearrange("p a b -> p (a b)"), p2[:])
                em.op("vector", evp2)
                em.bar()
                def mmm(t):
                    r = None
                    for h in range(4):
                        t.matmul(p3[:, h * 128:(h + 1) * 128], Mt4[:, h, :],
                                 Ps4[:, h, :], start=True, stop=True)
                        r = t.matmul(p4[:, h * 128:(h + 1) * 128], Ps4[:, h, :],
                                     Mt4[:, h, :], start=True, stop=True)
                    return r
                em.op("tensor", mmm)
                em.bar()
                def evm(v):
                    v.tensor_copy(scrM, AB3[:, 1024:1536])
                    v.tensor_add(AB3[:, 1024:1536], scrM, p3[:, 0:512])
                    v.tensor_copy(scrM, AB3[:, 1536:2048])
                    return v.tensor_add(AB3[:, 1536:2048], scrM, p4[:, 0:512])
                em.op("vector", evm)
                em.bar()
            def at1(v, c=c):
                r = None
                for h in range(4):
                    r = v.tensor_scalar(At4[:, h, :], Mt4[:, h, :],
                                        stats[:, c, 4 + h:5 + h], None, ALU.mult)
                return r
            em.op("vector", at1)
            em.bar()
            def pb1(v):
                v.tensor_mul(scr4[:].rearrange("p a b -> p (a b)"),
                             KQ4[:].rearrange("p a b -> p (a b)"),
                             rT4[:].rearrange("p a b -> p (a b)"))
                r = None
                for h in range(4):
                    r = v.tensor_mul(Pb4[:, h, :], scr4[:, h, :], cst["triuI"][:])
                return r
            em.op("vector", pb1)
            em.op("scalar", lambda sc: sc.copy(
                S4b[:].rearrange("p a b c -> p (a b c)"),
                S4[:].rearrange("p a b c -> p (a b c)")))
            em.bar()
            def mmu(t):
                r = None
                for h in range(4):
                    t.matmul(p3[:, h * 256:(h + 1) * 256], At4[:, h, :],
                             Vt_c[:, h * 256:(h + 1) * 256].rearrange("p m -> p m")
                             if False else AB1[:, 2048 + h * 256:2048 + (h + 1) * 256],
                             start=True, stop=True)
                    for kc2 in range(2):
                        r = t.matmul(p4[:, h * 256 + kc2 * 128:h * 256 + (kc2 + 1) * 128],
                                     AB1[:, 3072 + h * 256 + kc2 * 128:3072 + h * 256 + (kc2 + 1) * 128],
                                     At4[:, h, :], start=True, stop=True)
                return r
            em.op("tensor", mmu)
            em.bar()
            def evu(v):
                v.tensor_copy(Ub4[:].rearrange("p a b -> p (a b)"), p3[:])
                return v.tensor_copy(Wt4[:].rearrange("p a b m -> p (a b m)"), p4[:])
            em.op("vector", evu)
            em.bar()
            def mmd(t):
                r = None
                for h in range(4):
                    for kc2 in range(2):
                        r = t.matmul(p3[:, h * 256:(h + 1) * 256], Wt4[:, h, kc2, :],
                                     S4b[:, h, kc2, :], start=(kc2 == 0), stop=(kc2 == 1))
                return r
            em.op("tensor", mmd)
            em.bar()
            em.op("vector", lambda v: v.tensor_sub(
                dU4[:].rearrange("p a b -> p (a b)"),
                Ub4[:].rearrange("p a b -> p (a b)"), p3[:]))
            em.bar()
            def mmo1(t):
                r = None
                for h in range(4):
                    for kc2 in range(2):
                        r = t.matmul(p4[:, h * 256:(h + 1) * 256], QtT[:, h, kc2, :],
                                     S4b[:, h, kc2, :], start=(kc2 == 0), stop=(kc2 == 1))
                return r
            em.op("tensor", mmo1)
            em.bar()
            def evo1(v):
                r = None
                for h in range(4):
                    r = v.tensor_scalar(O14[:, h, :], p4[:, h * 256:(h + 1) * 256],
                                        g4[:, h:h + 1], None, ALU.mult)
                return r
            em.op("vector", evo1)
            em.bar()
            def mmo2(t):
                r = None
                for h in range(4):
                    r = t.matmul(p3[:, h * 256:(h + 1) * 256], Pb4[:, h, :],
                                 dU4[:, h, :], start=True, stop=True)
                return r
            em.op("tensor", mmo2)
            em.bar()
            em.op("vector", lambda v: v.tensor_add(
                rdt, O14[:].rearrange("p a b -> p (a b)"), p3[:]))
            em.bar()
            em.dma("sync", lambda s_, c=c: s_.dma_start(out=rd_t[:, c, :], in_=rdt))
            def mms2(t):
                r = None
                for h in range(2):
                    for kc2 in range(2):
                        r = t.matmul(p4[:, (h * 2 + kc2) * 256:(h * 2 + kc2 + 1) * 256],
                                     AB2[:, h * 256 + kc2 * 128:h * 256 + (kc2 + 1) * 128],
                                     dU4[:, h, :], start=True, stop=True)
                return r
            def mms3(t):
                r = None
                for h in range(2, 4):
                    for kc2 in range(2):
                        r = t.matmul(p3[:, ((h - 2) * 2 + kc2) * 256:((h - 2) * 2 + kc2 + 1) * 256],
                                     AB2[:, h * 256 + kc2 * 128:h * 256 + (kc2 + 1) * 128],
                                     dU4[:, h, :], start=True, stop=True)
                return r
            em.op("tensor", mms2)
            em.op("tensor", mms3)
            em.bar()
            def evs(v):
                r = None
                for h in range(4):
                    pp = p4 if h < 2 else p3
                    hh = h if h < 2 else h - 2
                    for kc2 in range(2):
                        r = v.scalar_tensor_tensor(
                            S4[:, h, kc2, :], S4[:, h, kc2, :],
                            gC4[:, h:h + 1],
                            pp[:, (hh * 2 + kc2) * 256:(hh * 2 + kc2 + 1) * 256],
                            ALU.mult, ALU.add)
                return r
            em.op("vector", evs)
            em.bar()

        em.dma("gpsimd", lambda g_: g_.dma_start(out=Wbig[:], in_=Wout_d))
        em.dma("sync", lambda s_: s_.dma_start(
            out=bout_b[:], in_=bout_d.broadcast_to([128, D])))
        em.dma("sync", lambda s_: s_.dma_start(
            out=rampb[:], in_=ramp_d.broadcast_to([128, D])))
        em.bar()
        em.op("scalar", lambda sc: sc.activation(rampb[:], rampb[:], AF.Sigmoid))
        em.bar()
        for g in range(NG):
            t0 = g * G
            for i in range(G):
                em.dma("sync", lambda s_, i=i, t0=t0: s_.dma_start(
                    out=xg[:, i, :], in_=rd_t[:, t0 + i, :]))
                em.dma("sync", lambda s_, i=i, t0=t0: s_.dma_start(
                    out=ogt[:, i, :], in_=og_t[:, t0 + i, :]))
            em.bar()
            for i in range(G):
                em.op("scalar", lambda sc, i=i: sc.activation(
                    sq[:], xg[:, i, :], AF.Square))
                em.bar()
                em.op("vector", lambda v: v.tensor_reduce(
                    ss4[:], sq[:].rearrange("p (h d) -> p h d", d=256), AX.X, ALU.add))
                em.bar()
                em.op("scalar", lambda sc: sc.activation(
                    r4n[:], ss4[:], AF.Sqrt, scale=1.0 / 256.0, bias=eps6[:, 0:1]))
                em.bar()
                em.op("vector", lambda v: v.reciprocal(r4n[:], r4n[:]))
                em.bar()
                def apr(v, i=i):
                    r = None
                    for h in range(4):
                        r = v.tensor_scalar(rdn[:, i, h * 256:(h + 1) * 256],
                                            xg[:, i, h * 256:(h + 1) * 256],
                                            r4n[:, h:h + 1], None, ALU.mult)
                    return r
                em.op("vector", apr)
                em.bar()
            emit_transpose_tiles(
                em, [rdn[:, i, :] for i in range(G)],
                lambda i, kc: rdT[:, kc, i * 128:(i + 1) * 128], iden[:], ptr[:], G)
            for i in range(G):
                def mmw(t, i=i):
                    for kc in range(8):
                        t.matmul(p1[:], rdT[:, kc, i * 128:(i + 1) * 128],
                                 Wbig[:, kc, 0:512], start=(kc == 0), stop=(kc == 7))
                    r = None
                    for kc in range(8):
                        r = t.matmul(p2[:], rdT[:, kc, i * 128:(i + 1) * 128],
                                     Wbig[:, kc, 512:1024], start=(kc == 0), stop=(kc == 7))
                    return r
                em.op("tensor", mmw)
                em.bar()
                def evw(v, i=i):
                    v.tensor_add(tkf[:, i, 0:512], p1[:], bout_b[:, 0:512])
                    return v.tensor_add(tkf[:, i, 512:1024], p2[:],
                                        bout_b[:, 512:1024])
                em.op("vector", evw)
                em.bar()
            em.op("vector", lambda v: v.tensor_mul(
                AF2[:, 0:4096], AF2[:, 0:4096], AF3[:, 0:4096]))
            em.bar()
            def rmp(v):
                r = None
                for i in range(G):
                    r = v.tensor_mul(tkf[:, i, :], tkf[:, i, :], rampb[:])
                return r
            em.op("vector", rmp)
            em.bar()
            for i in range(G):
                em.dma("sync", lambda s_, i=i, t0=t0: s_.dma_start(
                    out=ogt[:, i, :], in_=x_t[:, t0 + i, :]))
            em.bar()
            emit_ln_group(em, [tkf[:, i, :] for i in range(G)],
                          [tkf[:, i, :] for i in range(G)],
                          [tkf[:, i, :] for i in range(G)], sq[:], cols, D, G)
            em.op("vector", lambda v: v.tensor_add(
                AF2[:, 0:4096], AF2[:, 0:4096], AF3[:, 0:4096]))
            em.bar()
            for i in range(G):
                em.dma("sync", lambda s_, i=i, t0=t0: s_.dma_start(
                    out=y_t[:, t0 + i, :], in_=tkf[:, i, :]))
            em.bar()



def run_l0_attn(x_full, W_in, W_ctrl, b_ctrl, w_q, w_k, w_v, W_out, b_out,
                ramp_logit):
    if "l0a" not in _CACHE:
        _CACHE["l0a"] = build_l0_attn()
    nc = _CACHE["l0a"]
    wcv = np.zeros((1024, 12), np.float32)
    wcv[:, 0:4] = np.asarray(w_q)[:, 0, :]
    wcv[:, 4:8] = np.asarray(w_k)[:, 0, :]
    wcv[:, 8:12] = np.asarray(w_v)[:, 0, :]
    wcvt = np.ascontiguousarray(wcv.reshape(8, 128, 12).transpose(1, 0, 2))
    consts = {"iden": _ident_bf16(), "Win": _wtile(np.asarray(W_in)),
              "Wctrl": _wtile(np.asarray(W_ctrl)),
              "Wout": _wtile(np.asarray(W_out)),
              "bctrl": np.asarray(b_ctrl).reshape(1, 1032),
              "bout": np.asarray(b_out).reshape(1, D),
              "ramp": np.asarray(ramp_logit).reshape(1, D),
              "wconv": wcvt, **_l0attn_consts()}
    in_maps = [{"x": np.ascontiguousarray(x_full[c]), **consts}
               for c in range(NCORES)]
    res = run_bass_kernel_spmd(nc, in_maps, core_ids=list(range(NCORES)))
    return np.stack([np.asarray(r["y"]) for r in res.results])



def emit_l1_ffn(nc, em, pfx, x_in, y_out, dd):
    SB = lambda n, s, d: nc.sbuf_tensor(pfx + n, s, d)
    PS = lambda n, s, d: nc.psum_tensor(pfx + n, s, d)
    Wq_d = dd["Wq"]; bq_d = dd["bq"]; K1T_d = dd["K1T"]; K2T_d = dd["K2T"]
    Wdn_d = dd["Wdn"]; Wup_d = dd["Wup"]; iden_d = dd["iden"]

    x_t = x_in.rearrange("(t p) d -> p t d", p=PT)
    y_t = y_out.rearrange("(t p) d -> p t d", p=PT)
    NH = 8
    TK = 10

    with ExitStack() as ctx:
        E = ctx.enter_context
        iden = E(SB("iden_s", [128, 128], BF16))
        eps_c = E(SB("eps_c", [128, 1], F32))
        sq = E(SB("sq", [128, 2048], F32))
        cols = {"s": E(SB("ln_s", [128, 4], F32)),
                "ss": E(SB("ln_ss", [128, 4], F32)),
                "r": E(SB("ln_r", [128, 4], F32)),
                "eps": eps_c}
        Wq_s = E(SB("Wq_s", [128, 8, 2048], BF16))
        bq_b = E(SB("bq_b", [128, 2048], F32))
        K1T_s = E(SB("K1T_s", [128, 256], BF16))
        K2T_s = E(SB("K2T_s", [128, 256], BF16))
        x0 = E(SB("x0", [128, D], F32))
        x0b = E(SB("x0b", [128, D], BF16))
        xin_t = E(SB("xin_t", [128, D], F32))
        xTt = E(SB("xTt", [128, 8, 128], BF16))
        qf = E(SB("qf", [128, 2048], F32))
        qb = E(SB("qb", [128, 2048], BF16))
        qT = E(SB("qT", [128, 16, 128], BF16))
        sc1 = E(SB("sc1", [128, NH, 256], F32))
        sc2 = E(SB("sc2", [128, NH, 256], F32))
        scm = E(SB("scm", [128, NH, 256], F32))
        m1a = E(SB("m1a", [128, NH, 8], F32))
        m1b = E(SB("m1b", [128, NH, 8], F32))
        m2a = E(SB("m2a", [128, NH, 8], F32))
        m2b = E(SB("m2b", [128, NH, 8], F32))
        i1a = E(SB("i1a", [128, NH, 8], U32))
        i1b = E(SB("i1b", [128, NH, 8], U32))
        i2a = E(SB("i2a", [128, NH, 8], U32))
        i2b = E(SB("i2b", [128, NH, 8], U32))
        v1f = E(SB("v1f", [128, NH, TK], F32))
        v2f = E(SB("v2f", [128, NH, TK], F32))
        i1f = E(SB("i1f", [128, NH, TK], F32))
        i2f = E(SB("i2f", [128, NH, TK], F32))
        cs4 = E(SB("cs4", [128, NH, TK, TK], F32))
        ea4 = E(SB("ea4", [128, NH, TK, TK], F32))
        csm = E(SB("csm", [128, NH, 100], F32))
        cm1 = E(SB("cm1", [128, NH, 8], F32))
        cm2 = E(SB("cm2", [128, NH, 8], F32))
        cvals = E(SB("cvals", [128, NH, TK], F32))
        eqm = E(SB("eqm", [128, NH, 100], F32))
        esf = E(SB("esf", [128, NH, TK], F32))
        esel = E(SB("esel", [128, NH, TK], U32))
        hsc = E(SB("hsc", [128, NH, TK], F32))
        hd = E(SB("hd", [128, NH, TK], F32))
        den8 = E(SB("den8", [128, NH], F32))
        Wdg = E(SB("Wdg", [128, 2, 4, 1024], BF16))
        Wug = E(SB("Wug", [128, 2, 4, 1024], F32))
        dts = E(SB("dts", [128, 4, 1024], F32))
        acc = E(SB("acc", [128, D], F32))
        p1 = E(PS("p1", [128, 512], F32))
        p2 = E(PS("p2", [128, 512], F32))
        p3 = E(PS("p3", [128, 1024], F32))
        p4 = E(PS("p4", [128, 1024], F32))
        ptr = E(PS("ptr", [128, 1024], BF16))
        ptr2 = E(PS("ptr2", [128, 1024], BF16))


        em.dma("sync", lambda s_: s_.dma_start(out=iden[:], in_=iden_d))
        em.dma("gpsimd", lambda g_: g_.dma_start(out=Wq_s[:], in_=Wq_d))
        em.dma("gpsimd", lambda g_: g_.dma_start(out=K1T_s[:], in_=K1T_d))
        em.dma("gpsimd", lambda g_: g_.dma_start(out=K2T_s[:], in_=K2T_d))
        em.dma("sync", lambda s_: s_.dma_start(
            out=bq_b[:], in_=bq_d.broadcast_to([128, 2048])))
        em.op("vector", lambda v: v.memset(eps_c[:], 1e-5))
        em.bar()

        for t_i in range(NT):
            em.dma("sync", lambda s_, t_i=t_i: s_.dma_start(
                out=xin_t[:], in_=x_t[:, t_i, :]))
            em.bar()
            emit_ln_group(em, [xin_t[:]], [x0[:]], [x0[:]], sq[:, 0:D], cols, D, 1)
            em.op("vector", lambda v: v.tensor_copy(x0b[:], x0[:]))
            em.bar()
            def trx(t):
                r = None
                for kc in range(8):
                    r = t.transpose(ptr[:, kc * 128:(kc + 1) * 128],
                                    x0b[:, kc * 128:(kc + 1) * 128], iden[:])
                return r
            em.op("tensor", trx)
            em.bar()
            em.op("vector", lambda v: v.tensor_copy(
                xTt[:].rearrange("p a m -> p (a m)"), ptr[:]))
            em.bar()
            for half in range(2):
                def mmq(t, half=half):
                    r = None
                    for j in range(2):
                        nb = half * 2 + j
                        pp = p1 if j == 0 else p2
                        for kc in range(8):
                            r = t.matmul(pp[:], xTt[:, kc, :],
                                         Wq_s[:, kc, nb * 512:(nb + 1) * 512],
                                         start=(kc == 0), stop=(kc == 7))
                    return r
                em.op("tensor", mmq)
                em.bar()
                def evq(v, half=half):
                    v.tensor_add(qf[:, half * 1024:half * 1024 + 512], p1[:],
                                 bq_b[:, half * 1024:half * 1024 + 512])
                    return v.tensor_add(qf[:, half * 1024 + 512:half * 1024 + 1024],
                                        p2[:], bq_b[:, half * 1024 + 512:half * 1024 + 1024])
                em.op("vector", evq)
                em.bar()
            em.op("vector", lambda v: v.tensor_copy(qb[:], qf[:]))
            em.bar()
            for half in range(2):
                def trq(t, half=half):
                    r = None
                    for c8 in range(8):
                        r = t.transpose(ptr[:, c8 * 128:(c8 + 1) * 128],
                                        qb[:, half * 1024 + c8 * 128:half * 1024 + (c8 + 1) * 128],
                                        iden[:])
                    return r
                em.op("tensor", trq)
                em.bar()
                em.op("vector", lambda v, half=half: v.tensor_copy(
                    qT[:, half * 8:(half + 1) * 8, :].rearrange("p a m -> p (a m)"),
                    ptr[:]))
                em.bar()
            for rr_ in range(2):
                def mms(t, rr_=rr_):
                    r = None
                    for hh in range(4):
                        h = rr_ * 4 + hh
                        r = t.matmul(p3[:, hh * 256:(hh + 1) * 256],
                                     qT[:, h * 2, :], K1T_s[:], start=True, stop=True)
                        r = t.matmul(p4[:, hh * 256:(hh + 1) * 256],
                                     qT[:, h * 2 + 1, :], K2T_s[:], start=True, stop=True)
                    return r
                em.op("tensor", mms)
                em.bar()
                def evs(v, rr_=rr_):
                    v.tensor_copy(sc1[:, rr_ * 4:(rr_ + 1) * 4, :]
                                  .rearrange("p a b -> p (a b)"), p3[:])
                    return v.tensor_copy(sc2[:, rr_ * 4:(rr_ + 1) * 4, :]
                                         .rearrange("p a b -> p (a b)"), p4[:])
                em.op("vector", evs)
                em.bar()
            def mx1(v):
                r = None
                for h in range(NH):
                    v.max(out=m1a[:, h, :], in_=sc1[:, h, :])
                    r = v.max(out=m2a[:, h, :], in_=sc2[:, h, :])
                return r
            em.op("vector", mx1)
            em.bar()
            def ix1(v):
                r = None
                for h in range(NH):
                    v.max_index(out=i1a[:, h, :], in_max=m1a[:, h, :], in_values=sc1[:, h, :])
                    r = v.max_index(out=i2a[:, h, :], in_max=m2a[:, h, :], in_values=sc2[:, h, :])
                return r
            em.op("vector", ix1)
            em.bar()
            def mr1(v):
                r = None
                for h in range(NH):
                    r = v.match_replace(out=scm[:, h, :], in_to_replace=m1a[:, h, :],
                                        in_values=sc1[:, h, :], imm_value=NEG)
                return r
            em.op("vector", mr1)
            em.bar()
            def mx1b(v):
                r = None
                for h in range(NH):
                    r = v.max(out=m1b[:, h, :], in_=scm[:, h, :])
                return r
            em.op("vector", mx1b)
            em.bar()
            def ix1b(v):
                r = None
                for h in range(NH):
                    r = v.max_index(out=i1b[:, h, :], in_max=m1b[:, h, :], in_values=scm[:, h, :])
                return r
            em.op("vector", ix1b)
            em.bar()
            def mr2(v):
                r = None
                for h in range(NH):
                    r = v.match_replace(out=scm[:, h, :], in_to_replace=m2a[:, h, :],
                                        in_values=sc2[:, h, :], imm_value=NEG)
                return r
            em.op("vector", mr2)
            em.bar()
            def mx2b(v):
                r = None
                for h in range(NH):
                    r = v.max(out=m2b[:, h, :], in_=scm[:, h, :])
                return r
            em.op("vector", mx2b)
            em.bar()
            def ix2b(v):
                r = None
                for h in range(NH):
                    r = v.max_index(out=i2b[:, h, :], in_max=m2b[:, h, :], in_values=scm[:, h, :])
                return r
            em.op("vector", ix2b)
            em.bar()
            def asm(v):
                v.tensor_copy(v1f[:, :, 0:8], m1a[:])
                v.tensor_copy(v1f[:, :, 8:10], m1b[:, :, 0:2])
                v.tensor_copy(v2f[:, :, 0:8], m2a[:])
                v.tensor_copy(v2f[:, :, 8:10], m2b[:, :, 0:2])
                v.tensor_copy(i1f[:, :, 0:8], i1a[:])
                v.tensor_copy(i1f[:, :, 8:10], i1b[:, :, 0:2])
                v.tensor_copy(i2f[:, :, 0:8], i2a[:])
                return v.tensor_copy(i2f[:, :, 8:10], i2b[:, :, 0:2])
            em.op("vector", asm)
            em.bar()
            em.op("vector", lambda v: v.tensor_scalar(
                i1f[:].rearrange("p a b -> p (a b)"),
                i1f[:].rearrange("p a b -> p (a b)"), 256.0, None, ALU.mult))
            em.bar()
            def bld(v):
                v.tensor_scalar(cs4[:],
                                v1f[:].unsqueeze(2).broadcast_to([128, NH, TK, TK]),
                                1.0, None, ALU.mult)
                return v.tensor_scalar(
                    ea4[:], i1f[:].unsqueeze(2).broadcast_to([128, NH, TK, TK]),
                    1.0, None, ALU.mult)
            em.op("vector", bld)
            em.bar()
            def bld2(v):
                cs_sw = cs4[:].rearrange("p h b a -> p h a b")
                ea_sw = ea4[:].rearrange("p h b a -> p h a b")
                v.tensor_add(cs_sw, cs_sw,
                             v2f[:].unsqueeze(2).broadcast_to([128, NH, TK, TK]))
                return v.tensor_add(ea_sw, ea_sw,
                                    i2f[:].unsqueeze(2).broadcast_to([128, NH, TK, TK]))
            em.op("vector", bld2)
            em.bar()
            def cx1(v):
                r = None
                for h in range(NH):
                    r = v.max(out=cm1[:, h, :],
                              in_=cs4[:, h, :, :].rearrange("p a b -> p (a b)"))
                return r
            em.op("vector", cx1)
            em.bar()
            def cmr(v):
                r = None
                for h in range(NH):
                    r = v.match_replace(out=csm[:, h, :], in_to_replace=cm1[:, h, :],
                                        in_values=cs4[:, h, :, :].rearrange("p a b -> p (a b)"),
                                        imm_value=NEG)
                return r
            em.op("vector", cmr)
            em.bar()
            def cx2(v):
                r = None
                for h in range(NH):
                    r = v.max(out=cm2[:, h, :], in_=csm[:, h, :])
                return r
            em.op("vector", cx2)
            em.bar()
            def casm(v):
                v.tensor_copy(cvals[:, :, 0:8], cm1[:])
                return v.tensor_copy(cvals[:, :, 8:10], cm2[:, :, 0:2])
            em.op("vector", casm)
            em.bar()
            for k in range(TK):
                def eq1(v, k=k):
                    r = None
                    for h in range(NH):
                        r = v.tensor_scalar(eqm[:, h, :],
                                            cs4[:, h, :, :].rearrange("p a b -> p (a b)"),
                                            cvals[:, h, k:k + 1], None, ALU.is_equal)
                    return r
                em.op("vector", eq1)
                em.bar()
                em.op("vector", lambda v: v.tensor_mul(
                    eqm[:].rearrange("p a b -> p (a b)"),
                    eqm[:].rearrange("p a b -> p (a b)"),
                    ea4[:].rearrange("p h b a -> p (h b a)")))

                em.bar()
                em.op("vector", lambda v, k=k: v.tensor_reduce(
                    esf[:, :, k], eqm[:], AX.X, ALU.max))
                em.bar()
            em.op("vector", lambda v: v.tensor_copy(
                esel[:].rearrange("p a b -> p (a b)"),
                esf[:].rearrange("p a b -> p (a b)")))
            def sfx(v):
                r = None
                for h in range(NH):
                    r = v.tensor_scalar(hsc[:, h, :], cvals[:, h, :],
                                        cvals[:, h, 0:1], None, ALU.subtract)
                return r
            em.op("vector", sfx)
            em.bar()
            em.op("scalar", lambda sc: sc.activation(
                hsc[:].rearrange("p a b -> p (a b)"),
                hsc[:].rearrange("p a b -> p (a b)"), AF.Exp))
            em.bar()
            em.op("vector", lambda v: v.tensor_reduce(den8[:], hsc[:], AX.X, ALU.add))
            em.bar()
            em.op("vector", lambda v: v.reciprocal(den8[:], den8[:]))
            em.bar()
            def sfx2(v):
                r = None
                for h in range(NH):
                    r = v.tensor_scalar(hsc[:, h, :], hsc[:, h, :],
                                        den8[:, h:h + 1], None, ALU.mult)
                return r
            em.op("vector", sfx2)
            em.bar()
            steps = [(k, half) for k in range(TK) for half in range(2)]
            for si in range(len(steps) + 1):
                if si < len(steps):
                    k, half = steps[si]
                    for hh in range(4):
                        def gd(g_, k=k, half=half, hh=hh, pb=si % 2):
                            return g_.indirect_dma_start(
                                out=Wdg[:, pb, hh, :], out_offset=None,
                                in_=Wdn_d[0:128, :],
                                in_offset=bass.IndirectOffsetOnAxis(
                                    ap=esel[:, half * 4 + hh, k:k + 1], axis=0))
                        em.dma("gpsimd", gd)
                if si >= 1:
                    pk, phalf = steps[si - 1]
                    def dod(v, pk=pk, phalf=phalf, ppb=(si - 1) % 2):
                        for hh in range(4):
                            v.tensor_mul(dts[:, hh, :], Wdg[:, ppb, hh, :], x0b[:])
                        r = None
                        for hh in range(4):
                            r = v.tensor_reduce(
                                hd[:, phalf * 4 + hh, pk:pk + 1], dts[:, hh, :],
                                AX.X, ALU.add)
                        return r
                    em.op("vector", dod)
                em.bar()
            em.op("scalar", lambda sc: sc.activation(
                esf[:].rearrange("p a b -> p (a b)"),
                hd[:].rearrange("p a b -> p (a b)"), AF.Erf,
                scale=0.7071067811865476))
            em.bar()
            def gel(v):
                t1 = esf[:].rearrange("p a b -> p (a b)")
                h1 = hd[:].rearrange("p a b -> p (a b)")
                v.tensor_mul(t1, t1, h1)
                v.tensor_add(t1, t1, h1)
                v.tensor_mul(h1, t1, hsc[:].rearrange("p a b -> p (a b)"))
                return v.tensor_scalar(h1, h1, 0.5, None, ALU.mult)
            em.op("vector", gel)
            em.op("gpsimd", lambda g_: g_.memset(acc[:], 0.0))
            em.bar()
            for si in range(len(steps) + 1):
                if si < len(steps):
                    k, half = steps[si]
                    for hh in range(4):
                        def gu(g_, k=k, half=half, hh=hh, pb=si % 2):
                            return g_.indirect_dma_start(
                                out=Wug[:, pb, hh, :], out_offset=None,
                                in_=Wup_d[0:128, :],
                                in_offset=bass.IndirectOffsetOnAxis(
                                    ap=esel[:, half * 4 + hh, k:k + 1], axis=0))
                        em.dma("gpsimd", gu)
                if si >= 1:
                    pk, phalf = steps[si - 1]
                    def upa(v, pk=pk, phalf=phalf, ppb=(si - 1) % 2):
                        r = None
                        for hh in range(4):
                            r = v.scalar_tensor_tensor(
                                acc[:], Wug[:, ppb, hh, :],
                                hd[:, phalf * 4 + hh, pk:pk + 1],
                                acc[:], ALU.mult, ALU.add)
                        return r
                    em.op("vector", upa)
                em.bar()
            emit_ln_group(em, [acc[:]], [acc[:]], [acc[:]], sq[:, 0:D], cols, D, 1)
            em.op("vector", lambda v: v.tensor_add(xin_t[:], xin_t[:], acc[:]))
            em.bar()
            em.dma("sync", lambda s_, t_i=t_i: s_.dma_start(
                out=y_t[:, t_i, :], in_=xin_t[:]))
            em.bar()



def run_l1_ffn(x_full, Wq, bq, K1, K2, W_up, W_down):
    if "l1f" not in _CACHE:
        _CACHE["l1f"] = build_l1_ffn()
    nc = _CACHE["l1f"]
    consts = {"iden": _ident_bf16(), "Wq": _wtile(np.asarray(Wq)),
              "bq": np.asarray(bq).reshape(1, 2048),
              "K1T": np.ascontiguousarray(np.asarray(K1).T),
              "K2T": np.ascontiguousarray(np.asarray(K2).T),
              "Wdn": np.asarray(W_down), "Wup": np.asarray(W_up)}
    in_maps = [{"x": np.ascontiguousarray(x_full[c]), **consts}
               for c in range(NCORES)]
    res = run_bass_kernel_spmd(nc, in_maps, core_ids=list(range(NCORES)))
    return np.stack([np.asarray(r["y"]) for r in res.results])


_CACHE = {}

_L0A_KEYS = ("Win", "Wctrl", "Wout", "bctrl", "bout", "ramp", "wconv",
             "utones", "trilS", "triuS", "triuI", "idf32", "ones1", "ones128")
_L0F_KEYS = tuple([f"{k}{l}" for l in range(2)
                   for k in ("wqkv", "wo", "w1a", "w1g", "w2p", "qs", "ks",
                             "lng", "lnb", "b1a", "b1g", "b2")]
                  + ["fqc", "fqs", "fkc", "fks", "mask"])
_L1A_KEYS = ("Wq", "Wkv", "Wo", "Wo0", "bo", "bo0", "mask", "qcos", "qsin",
             "kcos", "ksin")
_L1F_KEYS = ("Wq", "bq", "K1T", "K2T", "Wdn", "Wup")


def build_fused():
    nc = bass.Bass("TRN2", target_bir_lowering=False)
    x_in = nc.dram_tensor("x", [S, D], F32, kind="ExternalInput").ap()
    y_out = nc.dram_tensor("y", [S, D], F32, kind="ExternalOutput").ap()
    iden_d = nc.dram_tensor("iden", [128, 128], BF16, kind="ExternalInput").ap()
    x01 = nc.dram_tensor("x01", [S, D], F32).ap()
    x12 = nc.dram_tensor("x12", [S, D], F32).ap()
    x23 = nc.dram_tensor("x23", [S, D], F32).ap()

    def DI(name, shape, dt=F32):
        return nc.dram_tensor(name, shape, dt, kind="ExternalInput").ap()

    dd_a = {"Win": DI("a_Win", [128, 8, 3072]),
            "Wctrl": DI("a_Wctrl", [128, 8, 1032]),
            "Wout": DI("a_Wout", [128, 8, 1024]),
            "bctrl": DI("a_bctrl", [1, 1032]), "bout": DI("a_bout", [1, D]),
            "ramp": DI("a_ramp", [1, D]), "wconv": DI("a_wconv", [128, 8, 12]),
            "iden": iden_d}
    for k, v in _l0attn_consts().items():
        dd_a[k] = DI("a_" + k, list(v.shape))

    dd_f = {"iden": iden_d, "mask": DI("f_mask", [128, 4, 1024])}
    for k in ("fqc", "fqs", "fkc", "fks"):
        dd_f[k] = DI("f_" + k, [128, NT, 32])
    shapes_f = {"wqkv": [128, 8, 1536], "wo": [128, 4, 1024],
                "w1a": [128, 8, 2730], "w1g": [128, 8, 2730],
                "w2p": [128, 22, 1024], "qs": [1, 64], "ks": [1, 64],
                "lng": [1, D], "lnb": [1, D], "b1a": [1, 2730],
                "b1g": [1, 2730], "b2": [1, D]}
    for l in range(2):
        for k, shp in shapes_f.items():
            dd_f[f"{k}{l}"] = DI(f"f_{k}{l}", shp)

    dd_s = {"Wq": DI("s_Wq", [128, 8, D]), "Wkv": DI("s_Wkv", [128, 8, D]),
            "Wo": DI("s_Wo", [128, 8, D]), "Wo0": DI("s_Wo0", [128, 8, D]),
            "bo": DI("s_bo", [1, D]), "bo0": DI("s_bo0", [1, D]),
            "iden": iden_d, "mask": DI("s_mask", [128, 4, 1024])}
    for k in ("qcos", "qsin", "kcos", "ksin"):
        dd_s[k] = DI("s_" + k, [128, NT, 32])

    dd_p = {"Wq": DI("p_Wq", [128, 8, 2048]), "bq": DI("p_bq", [1, 2048]),
            "K1T": DI("p_K1T", [128, 256]), "K2T": DI("p_K2T", [128, 256]),
            "Wdn": DI("p_Wdn", [65536, 1024]), "Wup": DI("p_Wup", [65536, 1024]),
            "iden": iden_d}

    with ExitStack() as ctx:
        em = Em(nc)
        emit_l0_attn(nc, em, "a_", x_in, x01, dd_a)
        emit_l0_ffn(nc, em, "f_", x01, x12, dd_f)
        emit_l1_attn(nc, em, "s_", x12, x23, dd_s)
        emit_l1_ffn(nc, em, "p_", x23, y_out, dd_p)
        em.build(ctx)
    return nc


def _host_consts(inputs):
    c = {"iden": _ident_bf16()}
    wcv = np.zeros((1024, 12), np.float32)
    wcv[:, 0:4] = np.asarray(inputs["L0_attn_w_q"])[:, 0, :]
    wcv[:, 4:8] = np.asarray(inputs["L0_attn_w_k"])[:, 0, :]
    wcv[:, 8:12] = np.asarray(inputs["L0_attn_w_v"])[:, 0, :]
    c["a_wconv"] = np.ascontiguousarray(wcv.reshape(8, 128, 12).transpose(1, 0, 2))
    c["a_Win"] = _wtile(np.asarray(inputs["L0_attn_W_in"]))
    c["a_Wctrl"] = _wtile(np.asarray(inputs["L0_attn_W_ctrl"]))
    c["a_Wout"] = _wtile(np.asarray(inputs["L0_attn_W_out"]))
    c["a_bctrl"] = np.asarray(inputs["L0_attn_b_ctrl"]).reshape(1, 1032)
    c["a_bout"] = np.asarray(inputs["L0_attn_b_out"]).reshape(1, D)
    c["a_ramp"] = np.asarray(inputs["L0_attn_ramp_logit"]).reshape(1, D)
    for k, v in _l0attn_consts().items():
        c["a_" + k] = v
    for k, v in _l0ffn_xpos().items():
        c["f_" + k] = v
    c["f_mask"] = _l0ffn_mask()
    w1 = np.asarray(inputs["L0_ffn_w1"])
    w2 = np.asarray(inputs["L0_ffn_w2"])
    for l in range(2):
        w2p = np.zeros((2816, 1024), np.float32)
        w2p[0:2730] = w2[l]
        c[f"f_wqkv{l}"] = _wtile(np.asarray(inputs["L0_ffn_wqkv"][l]))
        c[f"f_wo{l}"] = _wtile(np.asarray(inputs["L0_ffn_wo"][l]))
        c[f"f_w1a{l}"] = _wtile(np.ascontiguousarray(w1[l][:, 0:2730]))
        c[f"f_w1g{l}"] = _wtile(np.ascontiguousarray(w1[l][:, 2730:]))
        c[f"f_w2p{l}"] = _wtile(w2p)
        c[f"f_qs{l}"] = np.asarray(inputs["L0_ffn_q_scale"][l]).reshape(1, 64)
        c[f"f_ks{l}"] = np.asarray(inputs["L0_ffn_k_scale"][l]).reshape(1, 64)
        c[f"f_lng{l}"] = np.asarray(inputs["L0_ffn_ln_g"][l]).reshape(1, D)
        c[f"f_lnb{l}"] = np.asarray(inputs["L0_ffn_ln_b"][l]).reshape(1, D)
        c[f"f_b1a{l}"] = np.ascontiguousarray(
            np.asarray(inputs["L0_ffn_b1"][l])[0:2730]).reshape(1, 2730)
        c[f"f_b1g{l}"] = np.ascontiguousarray(
            np.asarray(inputs["L0_ffn_b1"][l])[2730:]).reshape(1, 2730)
        c[f"f_b2{l}"] = np.asarray(inputs["L0_ffn_b2"][l]).reshape(1, D)
    c["s_Wq"] = _wtile(np.asarray(inputs["L1_attn_Wq"]))
    c["s_Wkv"] = _wtile(np.asarray(inputs["L1_attn_Wkv"]))
    c["s_Wo"] = _wtile(np.asarray(inputs["L1_attn_Wo"]))
    c["s_Wo0"] = _wtile(np.asarray(inputs["L1_attn_Wo0"]))
    c["s_bo"] = np.asarray(inputs["L1_attn_bo"]).reshape(1, D)
    c["s_bo0"] = np.asarray(inputs["L1_attn_bo0"]).reshape(1, D)
    c["s_mask"] = _l1_mask()
    for k, v in _l1_rot().items():
        c["s_" + k] = v
    c["p_Wq"] = _wtile(np.asarray(inputs["L1_ffn_Wq"]))
    c["p_bq"] = np.asarray(inputs["L1_ffn_bq"]).reshape(1, 2048)
    c["p_K1T"] = np.ascontiguousarray(np.asarray(inputs["L1_ffn_K1"]).T)
    c["p_K2T"] = np.ascontiguousarray(np.asarray(inputs["L1_ffn_K2"]).T)
    c["p_Wdn"] = np.asarray(inputs["L1_ffn_W_down"])
    c["p_Wup"] = np.asarray(inputs["L1_ffn_W_up"])
    return c


def kernel(**inputs):
    if "fused" not in _CACHE:
        _CACHE["fused"] = build_fused()
    nc = _CACHE["fused"]
    c = _host_consts(inputs)
    xall = np.asarray(inputs["L0_attn_x"], np.float32)
    in_maps = [{"x": np.ascontiguousarray(xall[ci]), **c} for ci in range(NCORES)]
    res = run_bass_kernel_spmd(nc, in_maps, core_ids=list(range(NCORES)))
    return np.stack([np.asarray(r["y"]) for r in res.results]).astype(np.float32)


# version 08
# speedup vs baseline: 1.0318x; 1.0318x over previous
import numpy as np
from contextlib import ExitStack
import concourse.bass as bass
import concourse.mybir as mybir
from concourse.bass_utils import run_bass_kernel_spmd

F32 = mybir.dt.float32
BF16 = mybir.dt.bfloat16
U32 = mybir.dt.uint32
U8 = mybir.dt.uint8
AX = mybir.AxisListType
ALU = mybir.AluOpType
AF = mybir.ActivationFunctionType

NCORES = 8
S = 4096
D = 1024
PT = 128
NT = S // PT
NEG = -3.0e38

ENGS = ("sync", "tensor", "vector", "scalar", "gpsimd")


class Em:

    def __init__(self, nc):
        self.nc = nc
        self.items = {e: [] for e in ENGS}

    def op(self, eng, fn):
        self.items[eng].append(("op", fn))

    def dma(self, eng, fn):
        self.items[eng].append(("dma", fn))

    def bar(self):
        for e in ENGS:
            self.items[e].append(("bar",))

    def build(self, ctx):
        nc = self.nc
        bsem = ctx.enter_context(nc.semaphore("bsem"))
        dsems = {e: ctx.enter_context(nc.semaphore(f"dsem_{e}")) for e in ENGS}
        block = ctx.enter_context(nc.Block())
        for e in ENGS:
            def mk(e=e, items=self.items[e]):
                def run(eng):
                    barcount = 0
                    dmacount = 0
                    dma_waited = 0
                    last = None
                    for it in items:
                        kind = it[0]
                        if kind == "op":
                            r = it[1](eng)
                            last = r if r is not None else last
                        elif kind == "dma":
                            it[1](eng).then_inc(dsems[e], 16)
                            dmacount += 1
                            last = None
                        else:
                            if dmacount > dma_waited:
                                eng.wait_ge(dsems[e], 16 * dmacount)
                                dma_waited = dmacount
                            if last is not None:
                                last.then_inc(bsem, 1)
                            else:
                                eng.nop().then_inc(bsem, 1)
                            last = None
                            barcount += 1
                            eng.wait_ge(bsem, len(ENGS) * barcount)
                return run
            getattr(block, e)(mk())


def _ident_bf16():
    import ml_dtypes
    return np.eye(128, dtype=ml_dtypes.bfloat16)

def _l1_mask():
    i = np.arange(512)[:, None]
    j = np.arange(512)[None, :]
    m = np.zeros((512, 1024), np.float32)
    m[:, 512:] = np.where(j > i, NEG, 0.0)
    return np.ascontiguousarray(m.reshape(4, 128, 1024).transpose(1, 0, 2))

def _l1_rot():
    d = 64
    inv = 1.0 / 10000.0 ** (np.arange(0, d, 2, dtype=np.float64) / d)
    t = np.arange(S, dtype=np.float64)
    fr = t[:, None] * inv[None, :]
    cos = np.cos(fr).astype(np.float32)
    sin = np.sin(fr).astype(np.float32)
    sc = np.float32(d ** (-0.5))
    def til(a):
        return np.ascontiguousarray(a.reshape(NT, PT, 32).transpose(1, 0, 2))
    return {"qcos": til(cos * sc), "qsin": til(sin * sc),
            "kcos": til(cos), "ksin": til(sin)}

def _wtile(w):
    k, n = w.shape
    return np.ascontiguousarray(w.reshape(k // PT, PT, n).transpose(1, 0, 2))


def emit_ln_group(em, xs, outs, xcs, sq, cols, dmodel, n, gain=None, bias=None):
    inv = 1.0 / dmodel
    for i in range(n):
        em.op("vector", lambda v, i=i: v.reduce_sum(
            cols["s"][:, i:i + 1], xs[i], axis=AX.X))
    em.bar()
    em.op("vector", lambda v: v.tensor_scalar(
        cols["s"][:, 0:n], cols["s"][:, 0:n], inv, None, ALU.mult))
    em.bar()
    for i in range(n):
        em.op("vector", lambda v, i=i: v.tensor_scalar(
            xcs[i], xs[i], cols["s"][:, i:i + 1], None, ALU.subtract))
    em.bar()
    for i in range(n):
        em.op("scalar", lambda sc, i=i: sc.activation(
            sq, xcs[i], AF.Square, accum_out=cols["ss"][:, i:i + 1]))
    em.bar()
    em.op("scalar", lambda sc: sc.activation(
        cols["r"][:, 0:n], cols["ss"][:, 0:n], AF.Sqrt, scale=inv,
        bias=cols["eps"][:, 0:1]))
    em.bar()
    em.op("vector", lambda v: v.reciprocal(cols["r"][:, 0:n], cols["r"][:, 0:n]))
    em.bar()
    if gain is None:
        for i in range(n):
            em.op("vector", lambda v, i=i: v.tensor_scalar(
                outs[i], xcs[i], cols["r"][:, i:i + 1], None, ALU.mult))
    else:
        for i in range(n):
            em.op("vector", lambda v, i=i: v.scalar_tensor_tensor(
                outs[i], xcs[i], cols["r"][:, i:i + 1], gain, ALU.mult, ALU.mult))
        em.bar()
        for i in range(n):
            em.op("vector", lambda v, i=i: v.tensor_add(outs[i], outs[i], bias))
    em.bar()


def emit_transpose_tiles(em, srcs, dst_fn, iden, p_tr, n, nkc=8):
    for i in range(n):
        def tr(t, i=i):
            r = None
            for kc in range(nkc):
                r = t.transpose(p_tr[:, kc * 128:(kc + 1) * 128],
                                srcs[i][:, kc * 128:(kc + 1) * 128], iden)
            return r
        em.op("tensor", tr)
        em.bar()
        def ev(v, i=i):
            r = None
            for kc in range(nkc):
                r = v.tensor_copy(dst_fn(i, kc), p_tr[:, kc * 128:(kc + 1) * 128])
            return r
        em.op("vector", ev)
        em.bar()



def emit_row_broadcast(em, dst, n, chunk=256):
    def f(g, dst=dst, n=n, chunk=chunk):
        r = None
        for o in range(0, n, chunk):
            r = g.partition_broadcast(dst[:, o:o + chunk], dst[0:1, o:o + chunk])
        return r
    em.op("gpsimd", f)

def emit_l1_attn(nc, em, pfx, x_in, y_out, dd):
    SB = lambda n, s, d: nc.sbuf_tensor(pfx + n, s, d)
    PS = lambda n, s, d: nc.psum_tensor(pfx + n, s, d)
    DR = lambda n, s, d: nc.dram_tensor(pfx + n, s, d).ap()
    Wq_d = dd["Wq"]; Wkv_d = dd["Wkv"]; Wo_d = dd["Wo"]; Wo0_d = dd["Wo0"]
    bo_d = dd["bo"]; bo0_d = dd["bo0"]; iden_d = dd["iden"]; mask_d = dd["mask"]
    rot_d = {k: dd[k] for k in ("qcos", "qsin", "kcos", "ksin")}
    qT2 = DR("qT2", [8, 128, S], BF16)
    kvT2 = DR("kvT2", [8, 128, S], BF16)
    kvt_d = DR("kvt_scr", [S, D], BF16)

    x_t = x_in.rearrange("(t p) d -> p t d", p=PT)
    y_t = y_out.rearrange("(t p) d -> p t d", p=PT)
    kvt_t = kvt_d.rearrange("(t p) d -> p t d", p=PT)

    G = 4
    NG = NT // G
    SEG = 512
    H = 16

    with ExitStack() as ctx:
        E = ctx.enter_context
        WA_s = E(SB("WA_s", [128, 8, D], BF16))
        WB_s = E(SB("WB_s", [128, 8, D], BF16))
        iden = E(SB("iden_s", [128, 128], BF16))
        bo_b = E(SB("bo_b", [128, D], F32))
        bo0_b = E(SB("bo0_b", [128, D], F32))
        eps_c = E(SB("eps_c", [128, 1], F32))
        rot_g = {k: E(SB(f"{k}_g", [128, G, 32], F32))
                 for k in ("qcos", "qsin", "kcos", "ksin")}
        sq = E(SB("sq", [128, D], F32))
        rsc = E(SB("rsc", [128, 2, 512], F32))
        arF = E(SB("arF", [128, 12288], F32))
        xg_a = arF[:, 0:4096].rearrange("p (i d) -> p i d", d=D)
        xc_a = arF[:, 4096:8192].rearrange("p (i d) -> p i d", d=D)
        probs4 = arF[:, 0:4096].rearrange("p (a k) -> p a k", k=1024)
        mask_s = arF[:, 4096:8192].rearrange("p (a k) -> p a k", k=1024)
        yb = arF[:, 8192:9216]
        ybc = arF[:, 9216:10240]
        xin2 = arF[:, 10240:11264]
        arB = E(SB("arB", [128, 16384], BF16))
        xln_a = arB[:, 0:4096].rearrange("p (i d) -> p i d", d=D)
        xT_a = arB[:, 4096:8192].rearrange("p (kc m) -> p kc m", m=G * PT)
        qrot_a = arB[:, 8192:12288].rearrange("p (i d) -> p i d", d=D)
        kvTs = arB[:, 0:8192].rearrange("p (f t) -> p f t", t=2 * SEG)
        kvts = arB[:, 8192:16384].rearrange("p (i d) -> p i d", d=D)
        qTs = E(SB("qTs", [128, 8, SEG], BF16))
        probsb = E(SB("probsb", [128, 4, 1024], BF16))
        probsT = E(SB("probsT", [128, 8, SEG], BF16))
        oTs = E(SB("oTs", [128, 8, SEG], BF16))
        mx4 = E(SB("mx4", [128, 4], F32))
        den4 = E(SB("den4", [128, 4], F32))
        cols = {"s": E(SB("ln_s", [128, G], F32)),
                "ss": E(SB("ln_ss", [128, G], F32)),
                "r": E(SB("ln_r", [128, G], F32)),
                "eps": eps_c}
        p_a = E(PS("p_a", [128, 2, 512], F32))
        p_b = E(PS("p_b", [128, 2, 512], F32))
        p_tr = E(PS("p_tr", [128, 1024], BF16))
        p_tr2 = E(PS("p_tr2", [128, 1024], BF16))
        p_av = E(PS("p_av", [64, 512], F32))


        em.dma("gpsimd", lambda g: g.dma_start(out=WA_s[:], in_=Wq_d))
        em.dma("gpsimd", lambda g: g.dma_start(out=WB_s[:], in_=Wkv_d))
        em.dma("sync", lambda s_: s_.dma_start(out=iden[:], in_=iden_d))
        em.dma("sync", lambda s_: s_.dma_start(out=bo_b[:], in_=bo_d.broadcast_to([128, D])))
        em.dma("sync", lambda s_: s_.dma_start(out=bo0_b[:], in_=bo0_d.broadcast_to([128, D])))
        em.op("vector", lambda v: v.memset(eps_c[:], 1e-5))
        em.bar()

        for g in range(NG):
            t0 = g * G
            for i in range(G):
                em.dma("sync", lambda s_, i=i, t0=t0: s_.dma_start(
                    out=xg_a[:, i, :], in_=x_t[:, t0 + i, :]))
            for k in ("qcos", "qsin", "kcos", "ksin"):
                em.dma("sync", lambda s_, k=k, t0=t0: s_.dma_start(
                    out=rot_g[k][:], in_=rot_d[k][:, t0:t0 + G, :]))
            em.bar()
            emit_ln_group(em, [xg_a[:, i, :] for i in range(G)],
                          [xln_a[:, i, :] for i in range(G)],
                          [xc_a[:, i, :] for i in range(G)], sq[:], cols, D, G)
            emit_transpose_tiles(
                em, [xln_a[:, i, :] for i in range(G)],
                lambda i, kc: xT_a[:, kc, i * 128:(i + 1) * 128], iden[:],
                p_tr[:], G)
            for nm, W_s, scrT, ckey, skey in (
                    ("q", WA_s, qT2, "qcos", "qsin"),
                    ("kv", WB_s, kvT2, "kcos", "ksin")):
                for i in range(G):
                    def mm1(t, i=i, W_s=W_s):
                        for kc in range(8):
                            t.matmul(p_a[:, 0, :], xT_a[:, kc, i * 128:(i + 1) * 128],
                                     W_s[:, kc, 0:512], start=(kc == 0), stop=(kc == 7))
                        r = None
                        for kc in range(8):
                            r = t.matmul(p_b[:, 0, :], xT_a[:, kc, i * 128:(i + 1) * 128],
                                         W_s[:, kc, 512:1024], start=(kc == 0), stop=(kc == 7))
                        return r
                    em.op("tensor", mm1)
                    em.bar()
                    em.op("vector", lambda v, i=i: v.tensor_copy(
                        xg_a[:, i, 0:512], p_a[:, 0, :]))
                    em.op("scalar", lambda s_, i=i: s_.copy(
                        xg_a[:, i, 512:1024], p_b[:, 0, :]))
                    em.bar()
                for i in range(G):
                    q3 = xg_a[:, i, :].rearrange("p (h two d) -> p h two d", h=H, two=2)
                    o3 = qrot_a[:, i, :].rearrange("p (h two d) -> p h two d", h=H, two=2)
                    cb = rot_g[ckey][:, i, :].unsqueeze(1).broadcast_to([128, H, 32])
                    sb = rot_g[skey][:, i, :].unsqueeze(1).broadcast_to([128, H, 32])
                    t_a = rsc[:, 0, :].rearrange("p (h d) -> p h d", h=H)
                    t_b = rsc[:, 1, :].rearrange("p (h d) -> p h d", h=H)
                    em.op("vector", lambda v, q3=q3, sb=sb, t_a=t_a:
                          v.tensor_mul(t_a, q3[:, :, 1, :], sb))
                    em.op("gpsimd", lambda g_, q3=q3, sb=sb, t_b=t_b:
                          g_.tensor_mul(t_b, q3[:, :, 0, :], sb))
                    em.bar()
                    def rot2(v, q3=q3, cb=cb, o3=o3):
                        v.tensor_mul(o3[:, :, 0, :], q3[:, :, 0, :], cb)
                        return v.tensor_mul(o3[:, :, 1, :], q3[:, :, 1, :], cb)
                    em.op("vector", rot2)
                    em.bar()
                    em.op("vector", lambda v, o3=o3, t_a=t_a:
                          v.tensor_sub(o3[:, :, 0, :], o3[:, :, 0, :], t_a))
                    em.op("gpsimd", lambda g_, o3=o3, t_b=t_b:
                          g_.tensor_add(o3[:, :, 1, :], o3[:, :, 1, :], t_b))
                    em.bar()
                xcb = arF[:, 4096:8192].bitcast(BF16)
                emit_transpose_tiles(
                    em, [qrot_a[:, i, :] for i in range(G)],
                    lambda i, kc: xcb[:, (i * 8 + kc) * 128:(i * 8 + kc + 1) * 128],
                    iden[:], p_tr2[:], G)
                for i in range(G):
                    for kc in range(8):
                        em.dma("sync", lambda s_, i=i, kc=kc, scrT=scrT, t0=t0, xcb=xcb:
                               s_.dma_start(
                                   out=scrT[kc, :, (t0 + i) * 128:(t0 + i + 1) * 128],
                                   in_=xcb[:, (i * 8 + kc) * 128:(i * 8 + kc + 1) * 128]))
                if nm == "kv":
                    for i in range(G):
                        em.dma("sync", lambda s_, i=i, t0=t0: s_.dma_start(
                            out=kvt_t[:, t0 + i, :], in_=qrot_a[:, i, :]))
                em.bar()

        em.dma("gpsimd", lambda g: g.dma_start(out=WA_s[:], in_=Wo_d))
        em.dma("gpsimd", lambda g: g.dma_start(out=WB_s[:], in_=Wo0_d))
        em.dma("sync", lambda s_: s_.dma_start(out=mask_s[:, :, :], in_=mask_d))
        em.bar()

        for s in range(8):
            em.dma("sync", lambda s_, s=s: s_.dma_start(
                out=qTs[:], in_=qT2.rearrange("f p t -> p f t")[:, :, s * SEG:(s + 1) * SEG]))
            lo = max(0, (s - 1) * SEG)
            em.dma("sync", lambda s_, s=s, lo=lo: s_.dma_start(
                out=kvTs[:, :, (0 if s > 0 else SEG):],
                in_=kvT2.rearrange("f p t -> p f t")[:, :, lo:(s + 1) * SEG]))
            nti = 8 if s > 0 else 4
            for i in range(nti):
                ti = ((s - 1) * 4 + i) if s > 0 else i
                sl = i if s > 0 else (4 + i)
                em.dma("sync", lambda s_, sl=sl, ti=ti: s_.dma_start(
                    out=kvts[:, sl, :], in_=kvt_t[:, ti, :]))
            em.bar()
            nk = 2 * SEG if s > 0 else SEG
            ko = 0 if s > 0 else SEG
            ntk = nk // 128
            for h in range(H):
                hb = (h % 2) * 64
                fc = h // 2
                for r in range(2):
                    def mmsc(t, h=h, hb=hb, fc=fc, s=s, r=r):
                        rr = None
                        for j in range(2):
                            tqc = 2 * r + j
                            lhs = qTs[hb:hb + 64, fc, tqc * 128:(tqc + 1) * 128]
                            if s > 0:
                                t.matmul(p_a[:, j, :], lhs,
                                         kvTs[hb:hb + 64, fc, 0:512],
                                         start=True, stop=True)
                            rr = t.matmul(p_b[:, j, :], lhs,
                                          kvTs[hb:hb + 64, fc, SEG:2 * SEG],
                                          start=True, stop=True)
                        return rr
                    em.op("tensor", mmsc)
                    em.bar()
                    def evsc(v, s=s, r=r):
                        if s > 0:
                            v.tensor_add(probs4[:, 2 * r:2 * r + 2, 0:512],
                                         p_a[:, :, :], mask_s[:, 2 * r:2 * r + 2, 0:512])
                        return v.tensor_add(probs4[:, 2 * r:2 * r + 2, 512:1024],
                                            p_b[:, :, :], mask_s[:, 2 * r:2 * r + 2, 512:1024])
                    em.op("vector", evsc)
                    em.bar()
                em.op("vector", lambda v, ko=ko: v.tensor_reduce(
                    mx4[:], probs4[:, :, ko:1024], AX.X, ALU.max, negate=True))
                em.bar()
                def sfm1(v, ko=ko):
                    rr = None
                    for a in range(4):
                        rr = v.tensor_scalar(probs4[:, a, ko:1024],
                                             probs4[:, a, ko:1024],
                                             mx4[:, a:a + 1], None, ALU.add)
                    return rr
                em.op("vector", sfm1)
                em.bar()
                em.op("scalar", lambda sc, ko=ko: sc.activation(
                    probs4[:, :, ko:1024], probs4[:, :, ko:1024], AF.Exp))
                em.bar()
                em.op("vector", lambda v, ko=ko: v.tensor_reduce(
                    den4[:], probs4[:, :, ko:1024], AX.X, ALU.add))
                em.bar()
                em.op("vector", lambda v: v.reciprocal(den4[:], den4[:]))
                em.bar()
                def sfm2(v, ko=ko):
                    rr = None
                    for a in range(4):
                        rr = v.tensor_scalar(probsb[:, a, ko:1024],
                                             probs4[:, a, ko:1024],
                                             den4[:, a:a + 1], None, ALU.mult)
                    return rr
                em.op("vector", sfm2)
                em.bar()
                for tqc in range(4):
                    pp = p_tr if tqc % 2 == 0 else p_tr2
                    def trp(t, tqc=tqc, pp=pp, ko=ko, ntk=ntk):
                        rr = None
                        for tkc in range(ntk):
                            rr = t.transpose(
                                pp[:, tkc * 128:(tkc + 1) * 128],
                                probsb[:, tqc, ko + tkc * 128:ko + (tkc + 1) * 128],
                                iden[:])
                        return rr
                    em.op("tensor", trp)
                    em.bar()
                    def evp(v, tqc=tqc, pp=pp, ntk=ntk):
                        return v.tensor_copy(
                            probsT[:, 0:ntk, tqc * 128:(tqc + 1) * 128],
                            pp[:, 0:ntk * 128].rearrange("p (a m) -> p a m", m=128))
                    em.op("vector", evp)
                    em.bar()
                def mmav(t, h=h, s=s, ntk=ntk):
                    rr = None
                    for tkc in range(ntk):
                        kvtile = kvts[:, (tkc if s > 0 else 4 + tkc), h * 64:(h + 1) * 64]
                        rr = t.matmul(p_av[:], kvtile, probsT[:, tkc, :],
                                      start=(tkc == 0), stop=(tkc == ntk - 1))
                    return rr
                em.op("tensor", mmav)
                em.bar()
                em.op("vector", lambda v, hb=hb, fc=fc: v.tensor_copy(
                    oTs[hb:hb + 64, fc, :], p_av[:]))
                em.bar()
            W_s = WB_s if s == 0 else WA_s
            bias_b = bo0_b if s == 0 else bo_b
            for i in range(4):
                em.dma("sync", lambda s_, s=s, i=i: s_.dma_start(
                    out=xin2[:], in_=x_t[:, s * 4 + i, :]))
                def mmo(t, i=i, W_s=W_s):
                    for kc in range(8):
                        t.matmul(p_a[:, 0, :], oTs[:, kc, i * 128:(i + 1) * 128],
                                 W_s[:, kc, 0:512], start=(kc == 0), stop=(kc == 7))
                    rr = None
                    for kc in range(8):
                        rr = t.matmul(p_b[:, 0, :], oTs[:, kc, i * 128:(i + 1) * 128],
                                      W_s[:, kc, 512:1024], start=(kc == 0), stop=(kc == 7))
                    return rr
                em.op("tensor", mmo)
                em.bar()
                def evo(v, bias_b=bias_b):
                    v.tensor_add(yb[:, 0:512], p_a[:, 0, :], bias_b[:, 0:512])
                    return v.tensor_add(yb[:, 512:1024], p_b[:, 0, :], bias_b[:, 512:1024])
                em.op("vector", evo)
                em.bar()
                emit_ln_group(em, [yb[:]], [ybc[:]], [ybc[:]], sq[:], cols, D, 1)
                em.op("vector", lambda v: v.tensor_add(xin2[:], xin2[:], ybc[:]))
                em.bar()
                em.dma("sync", lambda s_, s=s, i=i: s_.dma_start(
                    out=y_t[:, s * 4 + i, :], in_=xin2[:]))
                em.bar()




def _l0ffn_xpos():
    dh = 64
    half = np.arange(0, dh, 2, dtype=np.float64)
    inv = 1.0 / 10000.0 ** (half / dh)
    t = np.arange(S, dtype=np.float64)
    fr = t[:, None] * inv[None, :]
    cos = np.cos(fr)
    sin = np.sin(fr)
    base = (half + 0.4 * dh) / (1.4 * dh)
    power = (t - S // 2) / 256.0
    scale = base[None, :] ** power[:, None]
    def til(a):
        return np.ascontiguousarray(
            a.astype(np.float32).reshape(NT, PT, 32).transpose(1, 0, 2))
    return {"fqc": til(cos * scale * 8.0), "fqs": til(sin * scale * 8.0),
            "fkc": til(cos / scale), "fks": til(sin / scale)}

def _l0ffn_mask():
    il = np.arange(512)[:, None]
    jl = np.arange(512)[None, :]
    m = np.zeros((512, 1024), np.float32)
    m[:, 0:512] = np.where(jl < il, NEG, 0.0)
    m[:, 512:] = np.where(jl > il, NEG, 0.0)
    return np.ascontiguousarray(m.reshape(4, 128, 1024).transpose(1, 0, 2))

def emit_l0_ffn(nc, em, pfx, x_in, y_out, dd):
    SB = lambda n, s, d: nc.sbuf_tensor(pfx + n, s, d)
    PS = lambda n, s, d: nc.psum_tensor(pfx + n, s, d)
    DR = lambda n, s, d: nc.dram_tensor(pfx + n, s, d).ap()
    iden_d = dd["iden"]; mask_d = dd["mask"]
    rot_d = {k: dd[k] for k in ("fqc", "fqs", "fkc", "fks")}
    WL = [{k: dd[f"{k}{l}"] for k in ("wqkv", "wo", "w1a", "w1g", "w2p", "qs",
                                      "ks", "lng", "lnb", "b1a", "b1g", "b2")}
          for l in range(2)]
    xcur = DR("xcur", [S, D], F32)
    qT2 = DR("qT2", [4, 128, S], BF16)
    kT2 = DR("kT2", [4, 128, S], BF16)
    vt2 = DR("vt2", [S, 512], BF16)
    hT2 = DR("hT2", [8, 128, S], BF16)
    h3a2 = DR("h3a2", [S, 2730], F32)

    x_t = x_in.rearrange("(t p) d -> p t d", p=PT)
    y_t = y_out.rearrange("(t p) d -> p t d", p=PT)
    xc_t = xcur.rearrange("(t p) d -> p t d", p=PT)
    vt_t = vt2.rearrange("(t p) d -> p t d", p=PT)
    h3a_t2 = h3a2.rearrange("(t p) d -> p t d", p=PT)

    G = 4
    NG = NT // G
    SEG = 512
    H = 8

    with ExitStack() as ctx:
        E = ctx.enter_context
        iden = E(SB("iden_s", [128, 128], BF16))
        eps_c = E(SB("eps_c", [128, 1], F32))
        sq = E(SB("sq", [128, D], F32))
        rsc = E(SB("rsc", [128, 2, 512], F32))
        ss8 = E(SB("ss8", [128, 16], F32))
        r8 = E(SB("r8", [128, 16], F32))
        qsb = E(SB("qsb", [128, 64], F32))
        ksb = E(SB("ksb", [128, 64], F32))
        lngb = E(SB("lngb", [128, D], F32))
        lnbb = E(SB("lnbb", [128, D], F32))
        b2b = E(SB("b2b", [128, D], F32))
        rotg = {k: E(SB(f"{k}_g", [128, G, 32], F32))
                for k in ("fqc", "fqs", "fkc", "fks")}
        cols = {"s": E(SB("ln_s", [128, G], F32)),
                "ss": E(SB("ln_ss", [128, G], F32)),
                "r": E(SB("ln_r", [128, G], F32)),
                "eps": eps_c}
        mx4 = E(SB("mx4", [128, 4], F32))
        den4 = E(SB("den4", [128, 4], F32))
        arF = E(SB("arF", [128, 18432], F32))
        xg = arF[:, 0:4096].rearrange("p (i d) -> p i d", d=D)
        qkvf = arF[:, 4096:10240].rearrange("p (i d) -> p i d", d=1536)
        probs4 = arF[:, 10240:14336].rearrange("p (a k) -> p a k", k=1024)
        mask_s = arF[:, 14336:18432].rearrange("p (a k) -> p a k", k=1024)
        hx = arF[:, 0:1024]
        b1ab = arF[:, 1024:3754]
        h3a_s = arF[:, 3754:6484]
        h3g_s = arF[:, 6484:9214]
        xt_c = arF[:, 9214:10238]
        arB = E(SB("arB", [128, 55304], BF16))
        wqkv_s = arB[:, 0:12288].rearrange("p (k n) -> p k n", n=1536)
        wo_s = arB[:, 12288:16384].rearrange("p (k n) -> p k n", n=1024)
        xTg = arB[:, 16384:20488].rearrange("p (k m) -> p k m", m=513)
        xbg = arB[:, 20488:24584].rearrange("p (i d) -> p i d", d=D)
        qrot = arB[:, 24584:26632].rearrange("p (i d) -> p i d", d=512)
        krot = arB[:, 26632:28680].rearrange("p (i d) -> p i d", d=512)
        vbf = arB[:, 28680:30728].rearrange("p (i d) -> p i d", d=512)
        qTs = arB[:, 34824:36872].rearrange("p (f t) -> p f t", t=512)
        kTs = arB[:, 36872:40968].rearrange("p (f t) -> p f t", t=1024)
        vts = arB[:, 40968:45064].rearrange("p (i d) -> p i d", d=512)
        probsb = arB[:, 45064:49160].rearrange("p (a k) -> p a k", k=1024)
        probsT = arB[:, 49160:53256].rearrange("p (a m) -> p a m", m=512)
        oTs = arB[:, 53256:55304].rearrange("p (f t) -> p f t", t=512)
        w1x_s = arB[:, 0:21840].rearrange("p (k n) -> p k n", n=2730)
        hb_c = arB[:, 21840:22864]
        hT_c1 = arB[:, 22864:23888].rearrange("p (k m) -> p k m", m=128)
        w2p_s = arB[:, 21840:44368].rearrange("p (k n) -> p k n", n=1024)
        actb = arB[:, 44368:47184]
        actT = arB[:, 47184:50000].rearrange("p (k m) -> p k m", m=128)
        hT_c2 = arB[:, 50000:51024].rearrange("p (k m) -> p k m", m=128)
        p_a = E(PS("p_a", [128, 2, 512], F32))
        p_b = E(PS("p_b", [128, 2, 512], F32))
        p_tr = E(PS("p_tr", [128, 1024], BF16))
        p_tr2 = E(PS("p_tr2", [128, 1024], BF16))
        p_av = E(PS("p_av", [64, 512], F32))


        em.dma("sync", lambda s_: s_.dma_start(out=iden[:], in_=iden_d))
        em.dma("sync", lambda s_: s_.dma_start(out=mask_s, in_=mask_d))
        em.op("vector", lambda v: v.memset(eps_c[:], 1e-5))
        em.bar()

        for g in range(NG):
            t0 = g * G
            for i in range(G):
                em.dma("sync", lambda s_, i=i, t0=t0: s_.dma_start(
                    out=xg[:, i, :], in_=x_t[:, t0 + i, :]))
            em.bar()
            emit_ln_group(em, [xg[:, i, :] for i in range(G)],
                          [qkvf[:, i, 0:1024] for i in range(G)],
                          [qkvf[:, i, 0:1024] for i in range(G)], sq[:], cols, D, G)
            for i in range(G):
                em.dma("sync", lambda s_, i=i, t0=t0: s_.dma_start(
                    out=xc_t[:, t0 + i, :], in_=qkvf[:, i, 0:1024]))
            em.bar()

        for l in range(2):
            W = WL[l]
            em.dma("gpsimd", lambda g_, W=W: g_.dma_start(out=wqkv_s, in_=W["wqkv"]))
            em.dma("gpsimd", lambda g_, W=W: g_.dma_start(out=wo_s, in_=W["wo"]))
            em.dma("sync", lambda s_, W=W: s_.dma_start(
                out=qsb[:], in_=W["qs"].broadcast_to([128, 64])))
            em.dma("sync", lambda s_, W=W: s_.dma_start(
                out=ksb[:], in_=W["ks"].broadcast_to([128, 64])))
            em.bar()
            for g in range(NG):
                t0 = g * G
                if g == 0:
                    em.op("vector", lambda v: v.memset(xTg[:, :, 0:1], 0.0))
                else:
                    em.op("vector", lambda v: v.tensor_copy(
                        xTg[:, :, 0:1], xTg[:, :, 512:513]))
                for i in range(G):
                    em.dma("sync", lambda s_, i=i, t0=t0: s_.dma_start(
                        out=xg[:, i, :], in_=xc_t[:, t0 + i, :]))
                for k in ("fqc", "fqs", "fkc", "fks"):
                    em.dma("sync", lambda s_, k=k, t0=t0: s_.dma_start(
                        out=rotg[k][:], in_=rot_d[k][:, t0:t0 + G, :]))
                em.bar()
                em.op("vector", lambda v: v.tensor_copy(
                    arB[:, 20488:24584], arF[:, 0:4096]))
                em.bar()
                emit_transpose_tiles(
                    em, [xbg[:, i, :] for i in range(G)],
                    lambda i, kc: xTg[:, kc, 1 + i * 128:1 + (i + 1) * 128],
                    iden[:], p_tr[:], G)
                for i in range(G):
                    def mmq(t, i=i):
                        r = None
                        for nb, pp in ((0, p_a[:, 0, :]), (1, p_a[:, 1, :]),
                                       (2, p_b[:, 0, :])):
                            for kc in range(8):
                                off = 1 if kc < 4 else 0
                                r = t.matmul(pp, xTg[:, kc, off + i * 128:off + (i + 1) * 128],
                                             wqkv_s[:, kc, nb * 512:(nb + 1) * 512],
                                             start=(kc == 0), stop=(kc == 7))
                        return r
                    em.op("tensor", mmq)
                    em.bar()
                    def evq(v, i=i):
                        v.tensor_copy(qkvf[:, i, 0:512], p_a[:, 0, :])
                        return v.tensor_copy(qkvf[:, i, 1024:1536], p_b[:, 0, :])
                    em.op("vector", evq)
                    em.op("scalar", lambda sc, i=i: sc.copy(
                        qkvf[:, i, 512:1024], p_a[:, 1, :]))
                    em.bar()
                for i in range(G):
                    em.op("scalar", lambda sc, i=i: sc.activation(
                        sq[:, 0:512], qkvf[:, i, 0:512], AF.Square))
                    em.op("vector", lambda v, i=i: v.tensor_copy(
                        vbf[:, i, :], qkvf[:, i, 1024:1536]))
                    em.bar()
                    em.op("scalar", lambda sc, i=i: sc.activation(
                        sq[:, 512:1024], qkvf[:, i, 512:1024], AF.Square))
                    em.op("vector", lambda v: v.tensor_reduce(
                        ss8[:, 0:8], sq[:, 0:512].rearrange("p (h d) -> p h d", d=64),
                        AX.X, ALU.add))
                    em.bar()
                    em.op("vector", lambda v: v.tensor_reduce(
                        ss8[:, 8:16], sq[:, 512:1024].rearrange("p (h d) -> p h d", d=64),
                        AX.X, ALU.add))
                    em.bar()
                    em.op("scalar", lambda sc: sc.activation(r8[:], ss8[:], AF.Sqrt))
                    em.bar()
                    em.op("vector", lambda v: v.tensor_scalar_max(r8[:], r8[:], 1e-12))
                    em.bar()
                    em.op("vector", lambda v: v.reciprocal(r8[:], r8[:]))
                    em.bar()
                    def nrm(v, i=i):
                        r = None
                        for h in range(8):
                            v.scalar_tensor_tensor(
                                qkvf[:, i, h * 64:(h + 1) * 64],
                                qkvf[:, i, h * 64:(h + 1) * 64],
                                r8[:, h:h + 1], qsb[:], ALU.mult, ALU.mult)
                            r = v.scalar_tensor_tensor(
                                qkvf[:, i, 512 + h * 64:512 + (h + 1) * 64],
                                qkvf[:, i, 512 + h * 64:512 + (h + 1) * 64],
                                r8[:, 8 + h:9 + h], ksb[:], ALU.mult, ALU.mult)
                        return r
                    em.op("vector", nrm)
                    em.bar()
                    for (o0, ck, sk, dst) in ((0, "fqc", "fqs", qrot),
                                              (512, "fkc", "fks", krot)):
                        q3 = qkvf[:, i, o0:o0 + 512].rearrange(
                            "p (h two d) -> p h two d", h=H, two=2)
                        o3 = dst[:, i, :].rearrange(
                            "p (h two d) -> p h two d", h=H, two=2)
                        cb = rotg[ck][:, i, :].unsqueeze(1).broadcast_to([128, H, 32])
                        sb = rotg[sk][:, i, :].unsqueeze(1).broadcast_to([128, H, 32])
                        t_a = rsc[:, 0, 0:256].rearrange("p (h d) -> p h d", h=H)
                        t_b = rsc[:, 1, 0:256].rearrange("p (h d) -> p h d", h=H)
                        def rt1(v, q3=q3, sb=sb, t_a=t_a, t_b=t_b):
                            v.tensor_mul(t_a, q3[:, :, 1, :], sb)
                            return v.tensor_mul(t_b, q3[:, :, 0, :], sb)
                        em.op("vector", rt1)
                        em.bar()
                        def rt2(v, q3=q3, cb=cb, o3=o3):
                            v.tensor_mul(o3[:, :, 0, :], q3[:, :, 0, :], cb)
                            return v.tensor_mul(o3[:, :, 1, :], q3[:, :, 1, :], cb)
                        em.op("vector", rt2)
                        em.bar()
                        def rt3(v, o3=o3, t_a=t_a, t_b=t_b):
                            v.tensor_sub(o3[:, :, 0, :], o3[:, :, 0, :], t_a)
                            return v.tensor_add(o3[:, :, 1, :], o3[:, :, 1, :], t_b)
                        em.op("vector", rt3)
                        em.bar()
                for i in range(G):
                    em.dma("sync", lambda s_, i=i, t0=t0: s_.dma_start(
                        out=vt_t[:, t0 + i, :], in_=vbf[:, i, :]))
                for nm, src, scrT in (("q", qrot, qT2), ("k", krot, kT2)):
                    emit_transpose_tiles(
                        em, [src[:, i, :] for i in range(G)],
                        lambda i, kc: arB[:, 30728 + (i * 4 + kc) * 128:
                                          30728 + (i * 4 + kc + 1) * 128],
                        iden[:], p_tr2[:], G, nkc=4)
                    for i in range(G):
                        for kc in range(4):
                            em.dma("sync", lambda s_, i=i, kc=kc, scrT=scrT, t0=t0:
                                   s_.dma_start(
                                       out=scrT[kc, :, (t0 + i) * 128:(t0 + i + 1) * 128],
                                       in_=arB[:, 30728 + (i * 4 + kc) * 128: 30728 + (i * 4 + kc + 1) * 128]))
                    em.bar()
            for w in range(8):
                em.dma("sync", lambda s_, w=w: s_.dma_start(
                    out=qTs, in_=qT2.rearrange("f p t -> p f t")[:, :, w * SEG:(w + 1) * SEG]))
                lo = max(0, (w - 1) * SEG)
                em.dma("sync", lambda s_, w=w, lo=lo: s_.dma_start(
                    out=kTs[:, :, (0 if w > 0 else SEG):],
                    in_=kT2.rearrange("f p t -> p f t")[:, :, lo:(w + 1) * SEG]))
                nti = 8 if w > 0 else 4
                for i in range(nti):
                    ti = ((w - 1) * 4 + i) if w > 0 else i
                    sl = i if w > 0 else (4 + i)
                    em.dma("sync", lambda s_, sl=sl, ti=ti: s_.dma_start(
                        out=vts[:, sl, :], in_=vt_t[:, ti, :]))
                em.bar()
                nk = 2 * SEG if w > 0 else SEG
                ko = 0 if w > 0 else SEG
                ntk = nk // 128
                for h in range(H):
                    hb = (h % 2) * 64
                    fc = h // 2
                    for r in range(2):
                        def mmsc(t, hb=hb, fc=fc, w=w, r=r):
                            rr = None
                            for j in range(2):
                                tqc = 2 * r + j
                                lhs = qTs[hb:hb + 64, fc, tqc * 128:(tqc + 1) * 128]
                                if w > 0:
                                    t.matmul(p_a[:, j, :], lhs,
                                             kTs[hb:hb + 64, fc, 0:512],
                                             start=True, stop=True)
                                rr = t.matmul(p_b[:, j, :], lhs,
                                              kTs[hb:hb + 64, fc, SEG:2 * SEG],
                                              start=True, stop=True)
                            return rr
                        em.op("tensor", mmsc)
                        em.bar()
                        def evsc(v, w=w, r=r):
                            if w > 0:
                                v.tensor_add(probs4[:, 2 * r:2 * r + 2, 0:512],
                                             p_a[:, :, :], mask_s[:, 2 * r:2 * r + 2, 0:512])
                            return v.tensor_add(probs4[:, 2 * r:2 * r + 2, 512:1024],
                                                p_b[:, :, :], mask_s[:, 2 * r:2 * r + 2, 512:1024])
                        em.op("vector", evsc)
                        em.bar()
                    em.op("vector", lambda v, ko=ko: v.tensor_reduce(
                        mx4[:], probs4[:, :, ko:1024], AX.X, ALU.max, negate=True))
                    em.bar()
                    def sfm1(v, ko=ko):
                        rr = None
                        for a in range(4):
                            rr = v.tensor_scalar(probs4[:, a, ko:1024],
                                                 probs4[:, a, ko:1024],
                                                 mx4[:, a:a + 1], None, ALU.add)
                        return rr
                    em.op("vector", sfm1)
                    em.bar()
                    em.op("scalar", lambda sc, ko=ko: sc.activation(
                        probs4[:, :, ko:1024], probs4[:, :, ko:1024], AF.Exp))
                    em.bar()
                    em.op("vector", lambda v, ko=ko: v.tensor_reduce(
                        den4[:], probs4[:, :, ko:1024], AX.X, ALU.add))
                    em.bar()
                    em.op("vector", lambda v: v.reciprocal(den4[:], den4[:]))
                    em.bar()
                    def sfm2(v, ko=ko):
                        rr = None
                        for a in range(4):
                            rr = v.tensor_scalar(probsb[:, a, ko:1024],
                                                 probs4[:, a, ko:1024],
                                                 den4[:, a:a + 1], None, ALU.mult)
                        return rr
                    em.op("vector", sfm2)
                    em.bar()
                    for tqc in range(4):
                        pp = p_tr if tqc % 2 == 0 else p_tr2
                        def trp(t, tqc=tqc, pp=pp, ko=ko, ntk=ntk):
                            rr = None
                            for tkc in range(ntk):
                                rr = t.transpose(
                                    pp[:, tkc * 128:(tkc + 1) * 128],
                                    probsb[:, tqc, ko + tkc * 128:ko + (tkc + 1) * 128],
                                    iden[:])
                            return rr
                        em.op("tensor", trp)
                        em.bar()
                        def evp(v, tqc=tqc, pp=pp, ntk=ntk):
                            return v.tensor_copy(
                                probsT[:, 0:ntk, tqc * 128:(tqc + 1) * 128],
                                pp[:, 0:ntk * 128].rearrange("p (a m) -> p a m", m=128))
                        em.op("vector", evp)
                        em.bar()
                    def mmav(t, h=h, w=w, ntk=ntk):
                        rr = None
                        for tkc in range(ntk):
                            vtile = vts[:, (tkc if w > 0 else 4 + tkc), h * 64:(h + 1) * 64]
                            rr = t.matmul(p_av[:], vtile, probsT[:, tkc, :],
                                          start=(tkc == 0), stop=(tkc == ntk - 1))
                        return rr
                    em.op("tensor", mmav)
                    em.bar()
                    em.op("vector", lambda v, hb=hb, fc=fc: v.tensor_copy(
                        oTs[hb:hb + 64, fc, :], p_av[:]))
                    em.bar()
                for i in range(4):
                    em.dma("sync", lambda s_, w=w, i=i: s_.dma_start(
                        out=xt_c, in_=xc_t[:, w * 4 + i, :]))
                    def mmo(t, i=i):
                        for kc in range(4):
                            t.matmul(p_a[:, 0, :], oTs[:, kc, i * 128:(i + 1) * 128],
                                     wo_s[:, kc, 0:512], start=(kc == 0), stop=(kc == 3))
                        rr = None
                        for kc in range(4):
                            rr = t.matmul(p_b[:, 0, :], oTs[:, kc, i * 128:(i + 1) * 128],
                                          wo_s[:, kc, 512:1024], start=(kc == 0), stop=(kc == 3))
                        return rr
                    em.op("tensor", mmo)
                    em.bar()
                    def evo(v):
                        v.tensor_add(xt_c[:, 0:512], p_a[:, 0, :], xt_c[:, 0:512])
                        return v.tensor_add(xt_c[:, 512:1024], p_b[:, 0, :], xt_c[:, 512:1024])
                    em.op("vector", evo)
                    em.bar()
                    em.dma("sync", lambda s_, w=w, i=i: s_.dma_start(
                        out=xc_t[:, w * 4 + i, :], in_=xt_c))
                    em.bar()
            em.dma("gpsimd", lambda g_, W=W: g_.dma_start(out=w1x_s, in_=W["w1a"]))
            em.dma("sync", lambda s_, W=W: s_.dma_start(
                out=b1ab, in_=W["b1a"].broadcast_to([128, 2730])))
            em.dma("sync", lambda s_, W=W: s_.dma_start(
                out=lngb[:], in_=W["lng"].broadcast_to([128, D])))
            em.dma("sync", lambda s_, W=W: s_.dma_start(
                out=lnbb[:], in_=W["lnb"].broadcast_to([128, D])))
            em.bar()
            for phase in (1, 2):
                if phase == 2:
                    em.dma("gpsimd", lambda g_, W=W: g_.dma_start(out=w1x_s, in_=W["w1g"]))
                    em.dma("gpsimd", lambda g_, W=W: g_.dma_start(out=w2p_s, in_=W["w2p"]))
                    em.dma("sync", lambda s_, W=W: s_.dma_start(
                        out=b1ab, in_=W["b1g"].broadcast_to([128, 2730])))
                    em.dma("sync", lambda s_, W=W: s_.dma_start(
                        out=b2b[:], in_=W["b2"].broadcast_to([128, D])))
                    em.op("vector", lambda v: v.memset(actb[:, 2730:2816], 0.0))
                    em.bar()
                for t_i in range(NT):
                    if phase == 1:
                        em.dma("sync", lambda s_, t_i=t_i: s_.dma_start(
                            out=hx[:, 0:512], in_=xc_t[:, t_i, 0:512]))
                        if t_i == 0:
                            em.op("vector", lambda v: v.memset(hx[0:1, 512:1024], 0.0))
                            em.dma("sync", lambda s_: s_.dma_start(
                                out=hx[1:128, 512:1024], in_=xcur[0:127, 512:1024]))
                        else:
                            em.dma("sync", lambda s_, t_i=t_i: s_.dma_start(
                                out=hx[:, 512:1024],
                                in_=xcur[t_i * 128 - 1:t_i * 128 + 127, 512:1024]))
                        em.bar()
                        emit_ln_group(em, [hx], [hb_c], [arF[:, 6484:7508]],
                                      sq[:], cols, D, 1,
                                      gain=lngb[:], bias=lnbb[:])
                        emit_transpose_tiles(em, [hb_c], lambda i, kc, t_i=t_i:
                                             hT_c1[:, kc, :], iden[:], p_tr[:], 1)
                        em.dma("sync", lambda s_, t_i=t_i: s_.dma_start(
                            out=hT2.rearrange("k p t -> p k t")[:, :, t_i * 128:(t_i + 1) * 128],
                            in_=hT_c1[:, :, :]))
                        hTv = hT_c1
                    else:
                        em.dma("sync", lambda s_, t_i=t_i: s_.dma_start(
                            out=hT_c2[:, :, :],
                            in_=hT2.rearrange("k p t -> p k t")[:, :, t_i * 128:(t_i + 1) * 128]))
                        em.dma("sync", lambda s_, t_i=t_i: s_.dma_start(
                            out=h3a_s, in_=h3a_t2[:, t_i, :]))
                        em.dma("sync", lambda s_, t_i=t_i: s_.dma_start(
                            out=xt_c, in_=xc_t[:, t_i, :]))
                        em.bar()
                        hTv = hT_c2
                    dstf = h3a_s if phase == 1 else h3g_s
                    for cpair in range(3):
                        def mmf(t, cpair=cpair, hTv=hTv):
                            rr = None
                            for j in range(2):
                                nb = cpair * 2 + j
                                nw = 512 if nb < 5 else 170
                                no = nb * 512
                                pp = (p_a if j == 0 else p_b)[:, 0, 0:nw]
                                for kc in range(8):
                                    rr = t.matmul(pp, hTv[:, kc, :],
                                                  w1x_s[:, kc, no:no + nw],
                                                  start=(kc == 0), stop=(kc == 7))
                            return rr
                        em.op("tensor", mmf)
                        em.bar()
                        def evf(v, cpair=cpair, dstf=dstf):
                            rr = None
                            for j in range(2):
                                nb = cpair * 2 + j
                                nw = 512 if nb < 5 else 170
                                no = nb * 512
                                pp = (p_a if j == 0 else p_b)[:, 0, 0:nw]
                                rr = v.tensor_add(dstf[:, no:no + nw], pp,
                                                  b1ab[:, no:no + nw])
                            return rr
                        em.op("vector", evf)
                        em.bar()
                    if phase == 1:
                        em.dma("sync", lambda s_, t_i=t_i: s_.dma_start(
                            out=h3a_t2[:, t_i, :], in_=h3a_s))
                        em.bar()
                        continue
                    em.op("vector", lambda v: v.tensor_mul(h3a_s, h3a_s, h3g_s))
                    em.bar()
                    em.op("scalar", lambda sc: sc.activation(
                        h3g_s, h3g_s, AF.Erf, scale=0.7071067811865476))
                    em.bar()
                    em.op("vector", lambda v: v.tensor_mul(h3g_s, h3a_s, h3g_s))
                    em.bar()
                    em.op("vector", lambda v: v.tensor_add(h3g_s, h3g_s, h3a_s))
                    em.bar()
                    em.op("vector", lambda v: v.tensor_scalar(
                        actb[:, 0:2730], h3g_s, 0.5, None, ALU.mult))
                    em.bar()
                    for bb, pp in ((0, p_tr), (1, p_tr2), (2, p_tr)):
                        nch = 8 if bb < 2 else 6
                        def trf(t, bb=bb, pp=pp, nch=nch):
                            rr = None
                            for c in range(nch):
                                kc = bb * 8 + c
                                rr = t.transpose(pp[:, c * 128:(c + 1) * 128],
                                                 actb[:, kc * 128:(kc + 1) * 128],
                                                 iden[:])
                            return rr
                        em.op("tensor", trf)
                        em.bar()
                        def evt(v, bb=bb, pp=pp, nch=nch):
                            return v.tensor_copy(
                                actT[:, bb * 8:bb * 8 + nch, :],
                                pp[:, 0:nch * 128].rearrange("p (a m) -> p a m", m=128))
                        em.op("vector", evt)
                        em.bar()
                    def mmy(t):
                        for kc in range(22):
                            t.matmul(p_a[:, 0, :], actT[:, kc, :], w2p_s[:, kc, 0:512],
                                     start=(kc == 0), stop=(kc == 21))
                        rr = None
                        for kc in range(22):
                            rr = t.matmul(p_b[:, 0, :], actT[:, kc, :], w2p_s[:, kc, 512:1024],
                                          start=(kc == 0), stop=(kc == 21))
                        return rr
                    em.op("tensor", mmy)
                    em.bar()
                    def evy(v):
                        v.tensor_add(rsc[:, 0, :], p_a[:, 0, :], b2b[:, 0:512])
                        return v.tensor_add(rsc[:, 1, :], p_b[:, 0, :], b2b[:, 512:1024])
                    em.op("vector", evy)
                    em.bar()
                    em.op("vector", lambda v: v.tensor_add(
                        xt_c, xt_c, rsc[:].rearrange("p a d -> p (a d)")))
                    em.bar()
                    em.dma("sync", lambda s_, t_i=t_i: s_.dma_start(
                        out=xc_t[:, t_i, :], in_=xt_c))
                    em.bar()

        for g in range(NG):
            t0 = g * G
            for i in range(G):
                em.dma("sync", lambda s_, i=i, t0=t0: s_.dma_start(
                    out=xg[:, i, :], in_=xc_t[:, t0 + i, :]))
                em.dma("sync", lambda s_, i=i, t0=t0: s_.dma_start(
                    out=qkvf[:, i, 512:1536], in_=x_t[:, t0 + i, :]))
            em.bar()
            emit_ln_group(em, [xg[:, i, :] for i in range(G)],
                          [xg[:, i, :] for i in range(G)],
                          [xg[:, i, :] for i in range(G)], sq[:], cols, D, G)
            em.op("vector", lambda v: v.tensor_add(
                xg[:, :, :], xg[:, :, :], qkvf[:, 0:G, 512:1536]))
            em.bar()
            for i in range(G):
                em.dma("sync", lambda s_, i=i, t0=t0: s_.dma_start(
                    out=y_t[:, t0 + i, :], in_=xg[:, i, :]))
            em.bar()



def run_l0_ffn(x_full, wqkv, q_scale, k_scale, wo, ln_g, ln_b, w1, b1, w2, b2):
    if "l0f" not in _CACHE:
        _CACHE["l0f"] = build_l0_ffn()
    nc = _CACHE["l0f"]
    consts = {"iden": _ident_bf16(), "mask": _l0ffn_mask(), **_l0ffn_xpos()}
    for l in range(2):
        w2p = np.zeros((2816, 1024), np.float32)
        w2p[0:2730] = np.asarray(w2[l])
        consts.update({
            f"wqkv{l}": _wtile(np.asarray(wqkv[l])),
            f"wo{l}": _wtile(np.asarray(wo[l])),
            f"w1a{l}": _wtile(np.ascontiguousarray(np.asarray(w1[l])[:, 0:2730])),
            f"w1g{l}": _wtile(np.ascontiguousarray(np.asarray(w1[l])[:, 2730:])),
            f"w2p{l}": _wtile(w2p),
            f"qs{l}": np.asarray(q_scale[l]).reshape(1, 64),
            f"ks{l}": np.asarray(k_scale[l]).reshape(1, 64),
            f"lng{l}": np.asarray(ln_g[l]).reshape(1, D),
            f"lnb{l}": np.asarray(ln_b[l]).reshape(1, D),
            f"b1a{l}": np.ascontiguousarray(np.asarray(b1[l])[0:2730]).reshape(1, 2730),
            f"b1g{l}": np.ascontiguousarray(np.asarray(b1[l])[2730:]).reshape(1, 2730),
            f"b2{l}": np.asarray(b2[l]).reshape(1, D),
        })
    in_maps = [{"x": np.ascontiguousarray(x_full[c]), **consts}
               for c in range(NCORES)]
    res = run_bass_kernel_spmd(nc, in_maps, core_ids=list(range(NCORES)))
    return np.stack([np.asarray(r["y"]) for r in res.results])



def _l0attn_consts():
    ut = np.triu(np.ones((128, 128), np.float32))
    trilS = np.tril(np.ones((128, 128), np.float32), -1)
    triuS = np.triu(np.ones((128, 128), np.float32), 1)
    triuI = np.triu(np.ones((128, 128), np.float32))
    idf = np.eye(128, dtype=np.float32)
    ones1 = np.ones((1, 128), np.float32)
    ones128 = np.ones((128, 128), np.float32)
    return {"utones": ut, "trilS": trilS, "triuS": triuS, "triuI": triuI,
            "idf32": idf, "ones1": ones1, "ones128": ones128}

def emit_l0_attn(nc, em, pfx, x_in, y_out, dd):
    SB = lambda n, s, d: nc.sbuf_tensor(pfx + n, s, d)
    PS = lambda n, s, d: nc.psum_tensor(pfx + n, s, d)
    DR = lambda n, s, d: nc.dram_tensor(pfx + n, s, d).ap()
    Win_d = dd["Win"]; Wctrl_d = dd["Wctrl"]; Wout_d = dd["Wout"]
    bctrl_d = dd["bctrl"]; bout_d = dd["bout"]; ramp_d = dd["ramp"]
    wconv_d = dd["wconv"]; iden_d = dd["iden"]
    cst_d = {k: dd[k] for k in _l0attn_consts()}
    qn2 = DR("qn2", [S, D], BF16)
    kn2 = DR("kn2", [S, D], BF16)
    vn2 = DR("vn2", [S, D], BF16)
    og2 = DR("og2", [S, D], F32)
    rd2 = DR("rd2", [S, D], F32)

    x_t = x_in.rearrange("(t p) d -> p t d", p=PT)
    y_t = y_out.rearrange("(t p) d -> p t d", p=PT)
    qn_t = qn2.rearrange("(t p) d -> p t d", p=PT)
    kn_t = kn2.rearrange("(t p) d -> p t d", p=PT)
    vn_t = vn2.rearrange("(t p) d -> p t d", p=PT)
    og_t = og2.rearrange("(t p) d -> p t d", p=PT)
    rd_t = rd2.rearrange("(t p) d -> p t d", p=PT)

    G = 4
    NG = NT // G

    with ExitStack() as ctx:
        E = ctx.enter_context
        iden = E(SB("iden_s", [128, 128], BF16))
        cst = {k: E(SB(f"{k}_s", list(v.shape), F32))
               for k, v in _l0attn_consts().items()}
        eps_c = E(SB("eps_c", [128, 1], F32))
        eps6 = E(SB("eps6", [128, 1], F32))
        sq = E(SB("sq", [128, D], F32))
        cols = {"s": E(SB("ln_s", [128, G], F32)),
                "ss": E(SB("ln_ss", [128, G], F32)),
                "r": E(SB("ln_r", [128, G], F32)),
                "eps": eps_c}
        Wbig = E(SB("Wbig", [128, 8, 1024], BF16))
        Wctrl_s = E(SB("Wctrl_s", [128, 8, 1032], BF16))
        wcv = E(SB("wcv", [128, 8, 12], F32))
        bctrl_b = E(SB("bctrl_b", [128, 1032], F32))
        bout_b = E(SB("bout_b", [128, D], F32))
        rampb = E(SB("rampb", [128, D], F32))
        stats = E(SB("stats", [128, NT, 12], F32))
        ctf = E(SB("ctf", [128, 1032], F32))
        ss4 = E(SB("ss4", [128, 4], F32))
        r4n = E(SB("r4n", [128, 4], F32))
        bnd = E(SB("bnd", [128, 3, 8, 3], F32))
        lgc = E(SB("lgc", [128, 4], F32))
        g4 = E(SB("g4", [128, 4], F32))
        rc4 = E(SB("rc4", [128, 4], F32))
        gC4 = E(SB("gC4", [128, 4], F32))
        lgT = E(SB("lgT", [4, 128], F32))
        nlgT = E(SB("nlgT", [4, 128], F32))
        S4 = E(SB("S4", [128, 4, 2, 256], F32))
        S4b = E(SB("S4b", [128, 4, 2, 256], BF16))
        AF1 = E(SB("AF1", [128, 4096], F32))
        AF2 = E(SB("AF2", [128, 4096], F32))
        AF3 = E(SB("AF3", [128, 4120], F32))
        AF4 = E(SB("AF4", [128, 4096], F32))
        AB1 = E(SB("AB1", [128, 4096], BF16))
        AB2 = E(SB("AB2", [128, 4096], BF16))
        AB3 = E(SB("AB3", [128, 4096], BF16))
        AB4 = E(SB("AB4", [128, 6144], BF16))
        xg = AF1[:].rearrange("p (i d) -> p i d", d=D)
        xcg = AF2[:].rearrange("p (i d) -> p i d", d=D)
        fmi = AF3[:, 0:4120].rearrange("p (a b) -> p a b", b=515)
        fmo = AF4[:].rearrange("p (a b) -> p a b", b=512)
        xlnb = AB1[:].rearrange("p (i d) -> p i d", d=D)
        xT = AB2[:].rearrange("p (k m) -> p k m", m=512)
        fmb = AB3[:].rearrange("p (a b) -> p a b", b=512)
        tkb = AB4[:, 0:4096].rearrange("p (i d) -> p i d", d=D)
        tkf = AF2[:].rearrange("p (i d) -> p i d", d=D)
        X4 = AF1[:, 0:512].rearrange("p (a b) -> p a b", b=128)
        X2_4 = AF1[:, 512:1024].rearrange("p (a b) -> p a b", b=128)
        r4s = AF1[:, 1024:1536].rearrange("p (a b) -> p a b", b=128)
        rT4 = AF1[:, 1536:2048].rearrange("p (a b) -> p a b", b=128)
        KK4 = AF1[:, 2048:2560].rearrange("p (a b) -> p a b", b=128)
        KQ4 = AF1[:, 2560:3072].rearrange("p (a b) -> p a b", b=128)
        scr4 = AF1[:, 3072:3584].rearrange("p (a b) -> p a b", b=128)
        scrM = AF1[:, 3584:4096]
        Ub4 = AF2[:, 0:1024].rearrange("p (a b) -> p a b", b=256)
        O14 = AF2[:, 1024:2048].rearrange("p (a b) -> p a b", b=256)
        rdt = AF2[:, 2048:3072]
        Qt_c = AB1[:, 0:1024]
        Kt_c = AB1[:, 1024:2048]
        Vt_c = AB1[:, 2048:3072]
        Kg_c = AB1[:, 3072:4096]
        Ks_c = AB2[:, 0:1024]
        KtT = AB2[:, 1024:2048].rearrange("p (a b m) -> p a b m", b=2, m=128)
        QtT = AB2[:, 2048:3072].rearrange("p (a b m) -> p a b m", b=2, m=128)
        Wt4 = AB2[:, 3072:4096].rearrange("p (a b m) -> p a b m", b=2, m=128)
        Ns4 = AB3[:, 0:512].rearrange("p (a b) -> p a b", b=128)
        Nt4 = AB3[:, 512:1024].rearrange("p (a b) -> p a b", b=128)
        Ms4 = AB3[:, 1024:1536].rearrange("p (a b) -> p a b", b=128)
        Mt4 = AB3[:, 1536:2048].rearrange("p (a b) -> p a b", b=128)
        Ps4 = AB3[:, 2048:2560].rearrange("p (a b) -> p a b", b=128)
        PtP4 = AB3[:, 2560:3072].rearrange("p (a b) -> p a b", b=128)
        At4 = AB3[:, 3072:3584].rearrange("p (a b) -> p a b", b=128)
        Pb4 = AB3[:, 3584:4096].rearrange("p (a b) -> p a b", b=128)
        dU4 = AB4[:, 4096:5120].rearrange("p (a b) -> p a b", b=256)
        ogt = AF3[:, 0:4096].rearrange("p (i d) -> p i d", d=D)
        rdn = AB1[:].rearrange("p (i d) -> p i d", d=D)
        rdT = AB2[:].rearrange("p (k m) -> p k m", m=512)
        p1 = E(PS("p1", [128, 512], F32))
        p2 = E(PS("p2", [128, 512], F32))
        p3 = E(PS("p3", [128, 1024], F32))
        p4 = E(PS("p4", [128, 1024], F32))
        ptr = E(PS("ptr", [128, 1024], BF16))
        pcs = E(PS("pcs", [128, 16], F32))


        em.dma("sync", lambda s_: s_.dma_start(out=iden[:], in_=iden_d))
        for k in cst:
            em.dma("sync", lambda s_, k=k: s_.dma_start(out=cst[k][:], in_=cst_d[k]))
        em.dma("gpsimd", lambda g_: g_.dma_start(out=Wctrl_s[:], in_=Wctrl_d))
        em.dma("sync", lambda s_: s_.dma_start(out=wcv[:], in_=wconv_d))
        em.dma("sync", lambda s_: s_.dma_start(
            out=bctrl_b[:], in_=bctrl_d.broadcast_to([128, 1032])))
        em.op("vector", lambda v: v.memset(eps_c[:], 1e-5))
        em.op("scalar", lambda sc: sc.memzero(eps6[:]))
        em.bar()
        em.op("vector", lambda v: v.memset(eps6[:], 1e-6))
        em.bar()

        for g in range(NG):
            t0 = g * G
            for i in range(G):
                em.dma("sync", lambda s_, i=i, t0=t0: s_.dma_start(
                    out=xg[:, i, :], in_=x_t[:, t0 + i, :]))
            em.bar()
            emit_ln_group(em, [xg[:, i, :] for i in range(G)],
                          [xlnb[:, i, :] for i in range(G)],
                          [xcg[:, i, :] for i in range(G)], sq[:], cols, D, G)
            emit_transpose_tiles(
                em, [xlnb[:, i, :] for i in range(G)],
                lambda i, kc: xT[:, kc, i * 128:(i + 1) * 128], iden[:],
                ptr[:], G)
            for i in range(G):
                def mmc(t, i=i):
                    r = None
                    for no, nw, pp in ((0, 512, p1[:, 0:512]), (512, 512, p2[:, 0:512]),
                                       (1024, 8, pcs[:, 0:8])):
                        for kc in range(8):
                            r = t.matmul(pp, xT[:, kc, i * 128:(i + 1) * 128],
                                         Wctrl_s[:, kc, no:no + nw],
                                         start=(kc == 0), stop=(kc == 7))
                    return r
                em.op("tensor", mmc)
                em.bar()
                def evc(v):
                    v.tensor_add(ctf[:, 0:512], p1[:], bctrl_b[:, 0:512])
                    v.tensor_add(ctf[:, 512:1024], p2[:], bctrl_b[:, 512:1024])
                    return v.tensor_add(ctf[:, 1024:1032], pcs[:, 0:8],
                                        bctrl_b[:, 1024:1032])
                em.op("vector", evc)
                em.bar()
                em.op("scalar", lambda sc: sc.activation(
                    ctf[:, 0:8], ctf[:, 0:8], AF.Sigmoid))
                em.bar()
                def sg2(v, i=i, t0=t0):
                    v.tensor_scalar(ctf[:, 0:4], ctf[:, 0:4], 0.98, 0.01,
                                    ALU.mult, ALU.add)
                    v.tensor_copy(stats[:, t0 + i, 4:8], ctf[:, 4:8])
                    return v.tensor_scalar(stats[:, t0 + i, 8:12], ctf[:, 4:8],
                                           -1.0, None, ALU.mult)
                em.op("vector", sg2)
                em.bar()
                em.op("scalar", lambda sc, i=i, t0=t0: sc.activation(
                    stats[:, t0 + i, 0:4], ctf[:, 0:4], AF.Ln))
                em.bar()
                em.op("scalar", lambda sc: sc.activation(
                    ctf[:, 8:1032], ctf[:, 8:1032], AF.Sigmoid))
                em.bar()
                em.dma("sync", lambda s_, i=i, t0=t0: s_.dma_start(
                    out=og_t[:, t0 + i, :], in_=ctf[:, 8:1032]))
            em.bar()
            for ti, scr_t in ((0, qn_t), (1, kn_t), (2, vn_t)):
                em.dma("gpsimd", lambda g_, ti=ti: g_.dma_start(
                    out=Wbig[:], in_=Win_d[:, :, ti * 1024:(ti + 1) * 1024]))
                if g == 0:
                    em.op("vector", lambda v: v.memset(fmi[:, :, 0:3], 0.0))
                else:
                    em.op("vector", lambda v, ti=ti: v.tensor_copy(
                        fmi[:, :, 0:3], bnd[:, ti, :, :]))
                em.bar()
                for fc in range(8):
                    pp = p3 if fc % 2 == 0 else p4
                    def mmf(t, fc=fc, pp=pp):
                        r = None
                        for kc in range(8):
                            r = t.matmul(pp[:, 0:512],
                                         Wbig[:, kc, fc * 128:(fc + 1) * 128],
                                         xT[:, kc, :], start=(kc == 0), stop=(kc == 7))
                        return r
                    em.op("tensor", mmf)
                    em.bar()
                    em.op("vector", lambda v, fc=fc, pp=pp: v.tensor_copy(
                        fmi[:, fc, 3:515], pp[:, 0:512]))
                    em.bar()
                def cva(v, ti=ti):
                    r = None
                    for kc in range(8):
                        r = v.tensor_scalar(fmo[:, kc, :], fmi[:, kc, 0:512],
                                            wcv[:, kc, ti * 4:ti * 4 + 1], None, ALU.mult)
                    return r
                em.op("vector", cva)
                em.bar()
                for j in range(1, 4):
                    def cvb(v, j=j, ti=ti):
                        r = None
                        for kc in range(8):
                            r = v.scalar_tensor_tensor(
                                fmo[:, kc, :], fmi[:, kc, j:j + 512],
                                wcv[:, kc, ti * 4 + j:ti * 4 + j + 1],
                                fmo[:, kc, :], ALU.mult, ALU.add)
                        return r
                    em.op("vector", cvb)
                    em.bar()
                em.op("vector", lambda v, ti=ti: v.tensor_copy(
                    bnd[:, ti, :, :], fmi[:, :, 512:515]))
                em.op("scalar", lambda sc: sc.activation(
                    AB3[:, 0:4096], AF4[:, 0:4096], AF.Silu))
                em.bar()
                for i in range(G):
                    def trb(t, i=i):
                        r = None
                        for fc in range(8):
                            r = t.transpose(ptr[:, fc * 128:(fc + 1) * 128],
                                            fmb[:, fc, i * 128:(i + 1) * 128], iden[:])
                        return r
                    em.op("tensor", trb)
                    em.bar()
                    em.op("vector", lambda v, i=i: v.tensor_copy(tkf[:, i, :], ptr[:]))
                    em.bar()
                for i in range(G):
                    if ti < 2:
                        em.op("scalar", lambda sc, i=i: sc.activation(
                            sq[:], tkf[:, i, :], AF.Square))
                        em.bar()
                        em.op("vector", lambda v: v.tensor_reduce(
                            ss4[:], sq[:].rearrange("p (h d) -> p h d", d=256),
                            AX.X, ALU.add))
                        em.bar()
                        em.op("scalar", lambda sc: sc.activation(
                            r4n[:], ss4[:], AF.Sqrt))
                        em.bar()
                        em.op("vector", lambda v: v.tensor_scalar_max(
                            r4n[:], r4n[:], 1e-6))
                        em.bar()
                        em.op("vector", lambda v: v.reciprocal(r4n[:], r4n[:]))
                        em.bar()
                    def apn(v, i=i, ti=ti):
                        r = None
                        for h in range(4):
                            r = v.tensor_scalar(tkb[:, i, h * 256:(h + 1) * 256],
                                                tkf[:, i, h * 256:(h + 1) * 256],
                                                r4n[:, h:h + 1] if ti < 2 else 1.0,
                                                None, ALU.mult)
                        return r
                    em.op("vector", apn)
                    em.bar()
                    em.dma("sync", lambda s_, i=i, t0=t0, scr_t=scr_t: s_.dma_start(
                        out=scr_t[:, t0 + i, :], in_=tkb[:, i, :]))
                em.bar()

        em.op("vector", lambda v: v.memset(S4[:].rearrange("p a b c -> p (a b c)"), 0.0))
        em.bar()
        for c in range(NT):
            em.dma("sync", lambda s_, c=c: s_.dma_start(out=Qt_c, in_=qn_t[:, c, :]))
            em.dma("sync", lambda s_, c=c: s_.dma_start(out=Kt_c, in_=kn_t[:, c, :]))
            em.dma("sync", lambda s_, c=c: s_.dma_start(out=Vt_c, in_=vn_t[:, c, :]))
            em.op("tensor", lambda t, c=c: t.matmul(
                pcs[:, 0:4], cst["utones"][:], stats[:, c, 0:4], start=True, stop=True))
            em.bar()
            em.op("vector", lambda v: v.tensor_copy(lgc[:], pcs[:, 0:4]))
            em.bar()
            em.op("scalar", lambda sc: sc.activation(g4[:], lgc[:], AF.Exp))
            def bxr(v, c=c):
                r = None
                for h in range(4):
                    r = v.tensor_scalar(scr4[:, h, :], cst["triuI"][:],
                                        stats[:, c, h:h + 1], None, ALU.mult)
                return r
            em.op("vector", bxr)
            em.bar()
            def mmx(t):
                r = None
                for h in range(4):
                    r = t.matmul(p3[:, h * 128:(h + 1) * 128], cst["ones128"][:],
                                 scr4[:, h, :], start=True, stop=True)
                return r
            em.op("tensor", mmx)
            em.bar()
            em.op("vector", lambda v: v.tensor_copy(
                X4[:].rearrange("p a b -> p (a b)"), p3[:, 0:512]))
            em.bar()
            def bx2(v):
                r = None
                for h in range(4):
                    v.tensor_scalar(X2_4[:, h, :], X4[:, h, :], lgc[:, h:h + 1],
                                    0.0, ALU.subtract, ALU.min)
                    r = v.tensor_scalar(scr4[:, h, :], X4[:, h, :], lgc[:, h:h + 1],
                                        0.0, ALU.subtract, ALU.max)
                return r
            em.op("vector", bx2)
            em.bar()
            def evx2(sc):
                sc.activation(rT4[:].rearrange("p a b -> p (a b)"),
                              X2_4[:].rearrange("p a b -> p (a b)"), AF.Exp)
                return sc.activation(r4s[:].rearrange("p a b -> p (a b)"),
                                     scr4[:].rearrange("p a b -> p (a b)"),
                                     AF.Exp, scale=-1.0)
            em.op("scalar", evx2)
            em.bar()
            def rcg(v):
                r = None
                for h in range(4):
                    r = v.tensor_copy(rc4[:, h:h + 1], rT4[:, h, 127:128])
                return r
            em.op("vector", rcg)
            em.bar()
            em.op("vector", lambda v: v.tensor_mul(gC4[:], g4[:], rc4[:]))
            em.bar()
            def kgs(v):
                r = None
                for h in range(4):
                    v.tensor_scalar(Kg_c[:, h * 256:(h + 1) * 256],
                                    Kt_c[:, h * 256:(h + 1) * 256],
                                    g4[:, h:h + 1], None, ALU.mult)
                    r = v.tensor_scalar(Ks_c[:, h * 256:(h + 1) * 256],
                                        Kt_c[:, h * 256:(h + 1) * 256],
                                        rc4[:, h:h + 1], None, ALU.mult)
                return r
            em.op("vector", kgs)
            em.bar()
            def trk(t):
                r = None
                for z in range(8):
                    h, kc2 = divmod(z, 2)
                    r = t.transpose(ptr[:, z * 128:(z + 1) * 128],
                                    Kt_c[h * 0:128, h * 256 + kc2 * 128:h * 256 + (kc2 + 1) * 128]
                                    if False else Kt_c[:, h * 256 + kc2 * 128:h * 256 + (kc2 + 1) * 128].rearrange("p m -> p m"),
                                    iden[:])
                return r
            def trk2(t):
                r = None
                for z in range(8):
                    h, kc2 = divmod(z, 2)
                    r = t.transpose(
                        ptr[:, z * 128:(z + 1) * 128],
                        AB1[:, 1024 + h * 256 + kc2 * 128:1024 + h * 256 + (kc2 + 1) * 128],
                        iden[:])
                return r
            em.op("tensor", trk2)
            em.bar()
            em.op("vector", lambda v: v.tensor_copy(AB2[:, 1024:2048], ptr[:]))
            em.bar()
            def trq2(t):
                r = None
                for z in range(8):
                    r = t.transpose(ptr[:, z * 128:(z + 1) * 128],
                                    AB1[:, z * 128:(z + 1) * 128], iden[:])
                return r
            em.op("tensor", trq2)
            em.bar()
            em.op("vector", lambda v: v.tensor_copy(AB2[:, 2048:3072], ptr[:]))
            em.bar()
            def mmg(t):
                r = None
                for h in range(4):
                    for kc2 in range(2):
                        t.matmul(p1[:, h * 128:(h + 1) * 128], KtT[:, h, kc2, :],
                                 KtT[:, h, kc2, :], start=(kc2 == 0), stop=(kc2 == 1))
                    for kc2 in range(2):
                        r = t.matmul(p2[:, h * 128:(h + 1) * 128], KtT[:, h, kc2, :],
                                     QtT[:, h, kc2, :], start=(kc2 == 0), stop=(kc2 == 1))
                return r
            em.op("tensor", mmg)
            em.bar()
            def evg(v):
                v.tensor_copy(KK4[:].rearrange("p a b -> p (a b)"), p1[:])
                return v.tensor_copy(KQ4[:].rearrange("p a b -> p (a b)"), p2[:])
            em.op("vector", evg)
            em.bar()
            def bbr(v, c=c):
                r = None
                for h in range(4):
                    r = v.tensor_scalar(scr4[:, h, :], cst["idf32"][:],
                                        stats[:, c, 4 + h:5 + h], None, ALU.mult)
                return r
            em.op("vector", bbr)
            em.bar()
            def mmbr(t):
                r = None
                for h in range(4):
                    r = t.matmul(p3[:, h * 128:(h + 1) * 128], cst["ones128"][:],
                                 scr4[:, h, :], start=True, stop=True)
                return r
            em.op("tensor", mmbr)
            em.bar()
            def ns1(v, c=c):
                v.tensor_mul(scr4[:].rearrange("p a b -> p (a b)"),
                             KK4[:].rearrange("p a b -> p (a b)"),
                             r4s[:].rearrange("p a b -> p (a b)"))
                r = None
                for h in range(4):
                    r = v.tensor_scalar(scr4[:, h, :], scr4[:, h, :],
                                        stats[:, c, 8 + h:9 + h], None, ALU.mult)
                return r
            em.op("vector", ns1)
            em.bar()
            def ns2(v):
                r = None
                for h in range(4):
                    r = v.tensor_mul(Ns4[:, h, :], scr4[:, h, :], cst["trilS"][:])
                return r
            em.op("vector", ns2)
            em.bar()
            def nt1(v):
                v.tensor_mul(scr4[:].rearrange("p a b -> p (a b)"),
                             KK4[:].rearrange("p a b -> p (a b)"),
                             rT4[:].rearrange("p a b -> p (a b)"))
                return v.tensor_mul(scr4[:].rearrange("p a b -> p (a b)"),
                                    scr4[:].rearrange("p a b -> p (a b)"),
                                    p3[:, 0:512])
            em.op("vector", nt1)
            em.bar()
            def nt2(v):
                r = None
                for h in range(4):
                    v.tensor_scalar(scr4[:, h, :], scr4[:, h, :], -1.0, None, ALU.mult)
                    r = v.tensor_mul(Nt4[:, h, :], scr4[:, h, :], cst["triuS"][:])
                return r
            em.op("vector", nt2)
            em.bar()
            def m0b(v):
                r = None
                for h in range(4):
                    v.tensor_add(Ms4[:, h, :], Ns4[:, h, :], iden[:])
                    r = v.tensor_add(Mt4[:, h, :], Nt4[:, h, :], iden[:])
                return r
            em.op("vector", m0b)
            em.bar()
            def p0c(v):
                v.tensor_copy(Ps4[:].rearrange("p a b -> p (a b)"),
                              Ns4[:].rearrange("p a b -> p (a b)"))
                return v.tensor_copy(PtP4[:].rearrange("p a b -> p (a b)"),
                                     Nt4[:].rearrange("p a b -> p (a b)"))
            em.op("vector", p0c)
            em.bar()
            for lvl in range(6):
                def mmp(t):
                    r = None
                    for h in range(4):
                        t.matmul(p1[:, h * 128:(h + 1) * 128], PtP4[:, h, :],
                                 Ps4[:, h, :], start=True, stop=True)
                        r = t.matmul(p2[:, h * 128:(h + 1) * 128], Ps4[:, h, :],
                                     PtP4[:, h, :], start=True, stop=True)
                    return r
                em.op("tensor", mmp)
                em.bar()
                def evp2(v):
                    v.tensor_copy(Ps4[:].rearrange("p a b -> p (a b)"), p1[:])
                    return v.tensor_copy(PtP4[:].rearrange("p a b -> p (a b)"), p2[:])
                em.op("vector", evp2)
                em.bar()
                def mmm(t):
                    r = None
                    for h in range(4):
                        t.matmul(p3[:, h * 128:(h + 1) * 128], Mt4[:, h, :],
                                 Ps4[:, h, :], start=True, stop=True)
                        r = t.matmul(p4[:, h * 128:(h + 1) * 128], Ps4[:, h, :],
                                     Mt4[:, h, :], start=True, stop=True)
                    return r
                em.op("tensor", mmm)
                em.bar()
                def evm(v):
                    v.tensor_copy(scrM, AB3[:, 1024:1536])
                    v.tensor_add(AB3[:, 1024:1536], scrM, p3[:, 0:512])
                    v.tensor_copy(scrM, AB3[:, 1536:2048])
                    return v.tensor_add(AB3[:, 1536:2048], scrM, p4[:, 0:512])
                em.op("vector", evm)
                em.bar()
            def at1(v, c=c):
                r = None
                for h in range(4):
                    r = v.tensor_scalar(At4[:, h, :], Mt4[:, h, :],
                                        stats[:, c, 4 + h:5 + h], None, ALU.mult)
                return r
            em.op("vector", at1)
            em.bar()
            def pb1(v):
                v.tensor_mul(scr4[:].rearrange("p a b -> p (a b)"),
                             KQ4[:].rearrange("p a b -> p (a b)"),
                             rT4[:].rearrange("p a b -> p (a b)"))
                r = None
                for h in range(4):
                    r = v.tensor_mul(Pb4[:, h, :], scr4[:, h, :], cst["triuI"][:])
                return r
            em.op("vector", pb1)
            em.op("scalar", lambda sc: sc.copy(
                S4b[:].rearrange("p a b c -> p (a b c)"),
                S4[:].rearrange("p a b c -> p (a b c)")))
            em.bar()
            def mmu(t):
                r = None
                for h in range(4):
                    t.matmul(p3[:, h * 256:(h + 1) * 256], At4[:, h, :],
                             Vt_c[:, h * 256:(h + 1) * 256].rearrange("p m -> p m")
                             if False else AB1[:, 2048 + h * 256:2048 + (h + 1) * 256],
                             start=True, stop=True)
                    for kc2 in range(2):
                        r = t.matmul(p4[:, h * 256 + kc2 * 128:h * 256 + (kc2 + 1) * 128],
                                     AB1[:, 3072 + h * 256 + kc2 * 128:3072 + h * 256 + (kc2 + 1) * 128],
                                     At4[:, h, :], start=True, stop=True)
                return r
            em.op("tensor", mmu)
            em.bar()
            def evu(v):
                v.tensor_copy(Ub4[:].rearrange("p a b -> p (a b)"), p3[:])
                return v.tensor_copy(Wt4[:].rearrange("p a b m -> p (a b m)"), p4[:])
            em.op("vector", evu)
            em.bar()
            def mmd(t):
                r = None
                for h in range(4):
                    for kc2 in range(2):
                        r = t.matmul(p3[:, h * 256:(h + 1) * 256], Wt4[:, h, kc2, :],
                                     S4b[:, h, kc2, :], start=(kc2 == 0), stop=(kc2 == 1))
                return r
            em.op("tensor", mmd)
            em.bar()
            em.op("vector", lambda v: v.tensor_sub(
                dU4[:].rearrange("p a b -> p (a b)"),
                Ub4[:].rearrange("p a b -> p (a b)"), p3[:]))
            em.bar()
            def mmo1(t):
                r = None
                for h in range(4):
                    for kc2 in range(2):
                        r = t.matmul(p4[:, h * 256:(h + 1) * 256], QtT[:, h, kc2, :],
                                     S4b[:, h, kc2, :], start=(kc2 == 0), stop=(kc2 == 1))
                return r
            em.op("tensor", mmo1)
            em.bar()
            def evo1(v):
                r = None
                for h in range(4):
                    r = v.tensor_scalar(O14[:, h, :], p4[:, h * 256:(h + 1) * 256],
                                        g4[:, h:h + 1], None, ALU.mult)
                return r
            em.op("vector", evo1)
            em.bar()
            def mmo2(t):
                r = None
                for h in range(4):
                    r = t.matmul(p3[:, h * 256:(h + 1) * 256], Pb4[:, h, :],
                                 dU4[:, h, :], start=True, stop=True)
                return r
            em.op("tensor", mmo2)
            em.bar()
            em.op("vector", lambda v: v.tensor_add(
                rdt, O14[:].rearrange("p a b -> p (a b)"), p3[:]))
            em.bar()
            em.dma("sync", lambda s_, c=c: s_.dma_start(out=rd_t[:, c, :], in_=rdt))
            def mms2(t):
                r = None
                for h in range(2):
                    for kc2 in range(2):
                        r = t.matmul(p4[:, (h * 2 + kc2) * 256:(h * 2 + kc2 + 1) * 256],
                                     AB2[:, h * 256 + kc2 * 128:h * 256 + (kc2 + 1) * 128],
                                     dU4[:, h, :], start=True, stop=True)
                return r
            def mms3(t):
                r = None
                for h in range(2, 4):
                    for kc2 in range(2):
                        r = t.matmul(p3[:, ((h - 2) * 2 + kc2) * 256:((h - 2) * 2 + kc2 + 1) * 256],
                                     AB2[:, h * 256 + kc2 * 128:h * 256 + (kc2 + 1) * 128],
                                     dU4[:, h, :], start=True, stop=True)
                return r
            em.op("tensor", mms2)
            em.op("tensor", mms3)
            em.bar()
            def evs(v):
                r = None
                for h in range(4):
                    pp = p4 if h < 2 else p3
                    hh = h if h < 2 else h - 2
                    for kc2 in range(2):
                        r = v.scalar_tensor_tensor(
                            S4[:, h, kc2, :], S4[:, h, kc2, :],
                            gC4[:, h:h + 1],
                            pp[:, (hh * 2 + kc2) * 256:(hh * 2 + kc2 + 1) * 256],
                            ALU.mult, ALU.add)
                return r
            em.op("vector", evs)
            em.bar()

        em.dma("gpsimd", lambda g_: g_.dma_start(out=Wbig[:], in_=Wout_d))
        em.dma("sync", lambda s_: s_.dma_start(
            out=bout_b[:], in_=bout_d.broadcast_to([128, D])))
        em.dma("sync", lambda s_: s_.dma_start(
            out=rampb[:], in_=ramp_d.broadcast_to([128, D])))
        em.bar()
        em.op("scalar", lambda sc: sc.activation(rampb[:], rampb[:], AF.Sigmoid))
        em.bar()
        for g in range(NG):
            t0 = g * G
            for i in range(G):
                em.dma("sync", lambda s_, i=i, t0=t0: s_.dma_start(
                    out=xg[:, i, :], in_=rd_t[:, t0 + i, :]))
                em.dma("sync", lambda s_, i=i, t0=t0: s_.dma_start(
                    out=ogt[:, i, :], in_=og_t[:, t0 + i, :]))
            em.bar()
            for i in range(G):
                em.op("scalar", lambda sc, i=i: sc.activation(
                    sq[:], xg[:, i, :], AF.Square))
                em.bar()
                em.op("vector", lambda v: v.tensor_reduce(
                    ss4[:], sq[:].rearrange("p (h d) -> p h d", d=256), AX.X, ALU.add))
                em.bar()
                em.op("scalar", lambda sc: sc.activation(
                    r4n[:], ss4[:], AF.Sqrt, scale=1.0 / 256.0, bias=eps6[:, 0:1]))
                em.bar()
                em.op("vector", lambda v: v.reciprocal(r4n[:], r4n[:]))
                em.bar()
                def apr(v, i=i):
                    r = None
                    for h in range(4):
                        r = v.tensor_scalar(rdn[:, i, h * 256:(h + 1) * 256],
                                            xg[:, i, h * 256:(h + 1) * 256],
                                            r4n[:, h:h + 1], None, ALU.mult)
                    return r
                em.op("vector", apr)
                em.bar()
            emit_transpose_tiles(
                em, [rdn[:, i, :] for i in range(G)],
                lambda i, kc: rdT[:, kc, i * 128:(i + 1) * 128], iden[:], ptr[:], G)
            for i in range(G):
                def mmw(t, i=i):
                    for kc in range(8):
                        t.matmul(p1[:], rdT[:, kc, i * 128:(i + 1) * 128],
                                 Wbig[:, kc, 0:512], start=(kc == 0), stop=(kc == 7))
                    r = None
                    for kc in range(8):
                        r = t.matmul(p2[:], rdT[:, kc, i * 128:(i + 1) * 128],
                                     Wbig[:, kc, 512:1024], start=(kc == 0), stop=(kc == 7))
                    return r
                em.op("tensor", mmw)
                em.bar()
                def evw(v, i=i):
                    v.tensor_add(tkf[:, i, 0:512], p1[:], bout_b[:, 0:512])
                    return v.tensor_add(tkf[:, i, 512:1024], p2[:],
                                        bout_b[:, 512:1024])
                em.op("vector", evw)
                em.bar()
            em.op("vector", lambda v: v.tensor_mul(
                AF2[:, 0:4096], AF2[:, 0:4096], AF3[:, 0:4096]))
            em.bar()
            def rmp(v):
                r = None
                for i in range(G):
                    r = v.tensor_mul(tkf[:, i, :], tkf[:, i, :], rampb[:])
                return r
            em.op("vector", rmp)
            em.bar()
            for i in range(G):
                em.dma("sync", lambda s_, i=i, t0=t0: s_.dma_start(
                    out=ogt[:, i, :], in_=x_t[:, t0 + i, :]))
            em.bar()
            emit_ln_group(em, [tkf[:, i, :] for i in range(G)],
                          [tkf[:, i, :] for i in range(G)],
                          [tkf[:, i, :] for i in range(G)], sq[:], cols, D, G)
            em.op("vector", lambda v: v.tensor_add(
                AF2[:, 0:4096], AF2[:, 0:4096], AF3[:, 0:4096]))
            em.bar()
            for i in range(G):
                em.dma("sync", lambda s_, i=i, t0=t0: s_.dma_start(
                    out=y_t[:, t0 + i, :], in_=tkf[:, i, :]))
            em.bar()



def run_l0_attn(x_full, W_in, W_ctrl, b_ctrl, w_q, w_k, w_v, W_out, b_out,
                ramp_logit):
    if "l0a" not in _CACHE:
        _CACHE["l0a"] = build_l0_attn()
    nc = _CACHE["l0a"]
    wcv = np.zeros((1024, 12), np.float32)
    wcv[:, 0:4] = np.asarray(w_q)[:, 0, :]
    wcv[:, 4:8] = np.asarray(w_k)[:, 0, :]
    wcv[:, 8:12] = np.asarray(w_v)[:, 0, :]
    wcvt = np.ascontiguousarray(wcv.reshape(8, 128, 12).transpose(1, 0, 2))
    consts = {"iden": _ident_bf16(), "Win": _wtile(np.asarray(W_in)),
              "Wctrl": _wtile(np.asarray(W_ctrl)),
              "Wout": _wtile(np.asarray(W_out)),
              "bctrl": np.asarray(b_ctrl).reshape(1, 1032),
              "bout": np.asarray(b_out).reshape(1, D),
              "ramp": np.asarray(ramp_logit).reshape(1, D),
              "wconv": wcvt, **_l0attn_consts()}
    in_maps = [{"x": np.ascontiguousarray(x_full[c]), **consts}
               for c in range(NCORES)]
    res = run_bass_kernel_spmd(nc, in_maps, core_ids=list(range(NCORES)))
    return np.stack([np.asarray(r["y"]) for r in res.results])



def emit_l1_ffn(nc, em, pfx, x_in, y_out, dd):
    SB = lambda n, s, d: nc.sbuf_tensor(pfx + n, s, d)
    PS = lambda n, s, d: nc.psum_tensor(pfx + n, s, d)
    Wq_d = dd["Wq"]; bq_d = dd["bq"]; K1T_d = dd["K1T"]; K2T_d = dd["K2T"]
    Wdn_d = dd["Wdn"]; Wup_d = dd["Wup"]; iden_d = dd["iden"]

    x_t = x_in.rearrange("(t p) d -> p t d", p=PT)
    y_t = y_out.rearrange("(t p) d -> p t d", p=PT)
    NH = 8
    TK = 10

    with ExitStack() as ctx:
        E = ctx.enter_context
        iden = E(SB("iden_s", [128, 128], BF16))
        eps_c = E(SB("eps_c", [128, 1], F32))
        sq = E(SB("sq", [128, 1024], F32))
        cols = {"s": E(SB("ln_s", [128, 4], F32)),
                "ss": E(SB("ln_ss", [128, 4], F32)),
                "r": E(SB("ln_r", [128, 4], F32)),
                "eps": eps_c}
        Wq_s = E(SB("Wq_s", [128, 8, 2048], BF16))
        bq_b = E(SB("bq_b", [128, 2048], F32))
        K1T_s = E(SB("K1T_s", [128, 256], BF16))
        K2T_s = E(SB("K2T_s", [128, 256], BF16))
        x0 = E(SB("x0", [128, D], F32))
        x0b = E(SB("x0b", [128, D], BF16))
        xin_t = E(SB("xin_t", [128, D], F32))
        xTt = E(SB("xTt", [128, 8, 128], BF16))
        qf = E(SB("qf", [128, 2048], F32))
        qb = E(SB("qb", [128, 2048], BF16))
        qT = E(SB("qT", [128, 16, 128], BF16))
        sc1 = E(SB("sc1", [128, NH, 256], F32))
        sc2 = E(SB("sc2", [128, NH, 256], F32))
        scm = E(SB("scm", [128, NH, 256], F32))
        m1a = E(SB("m1a", [128, NH, 8], F32))
        m1b = E(SB("m1b", [128, NH, 8], F32))
        m2a = E(SB("m2a", [128, NH, 8], F32))
        m2b = E(SB("m2b", [128, NH, 8], F32))
        i1a = E(SB("i1a", [128, NH, 8], U32))
        i1b = E(SB("i1b", [128, NH, 8], U32))
        i2a = E(SB("i2a", [128, NH, 8], U32))
        i2b = E(SB("i2b", [128, NH, 8], U32))
        v1f = E(SB("v1f", [128, NH, TK], F32))
        v2f = E(SB("v2f", [128, NH, TK], F32))
        i1f = E(SB("i1f", [128, NH, TK], F32))
        i2f = E(SB("i2f", [128, NH, TK], F32))
        cs4 = E(SB("cs4", [128, NH, TK, TK], F32))
        ea4 = E(SB("ea4", [128, NH, TK, TK], F32))
        csm = E(SB("csm", [128, NH, 100], F32))
        cm1 = E(SB("cm1", [128, NH, 8], F32))
        cm2 = E(SB("cm2", [128, NH, 8], F32))
        cvals = E(SB("cvals", [128, NH, TK], F32))
        eqm = E(SB("eqm", [128, NH, 100], F32))
        esf = E(SB("esf", [128, NH, TK], F32))
        esel = E(SB("esel", [128, NH, TK], U32))
        hsc = E(SB("hsc", [128, NH, TK], F32))
        hd = E(SB("hd", [128, NH, TK], F32))
        den8 = E(SB("den8", [128, NH], F32))
        Wdg = E(SB("Wdg", [128, 2, 8, 1024], BF16))
        Wug = E(SB("Wug", [128, 2, 8, 1024], BF16))
        dtsb = E(SB("dtsb", [128, 8, 1024], BF16))
        gtmp = E(SB("gtmp", [128, 1024], BF16))
        acc = E(SB("acc", [128, D], F32))
        acc_g = E(SB("acc_g", [128, D], F32))
        p1 = E(PS("p1", [128, 512], F32))
        p2 = E(PS("p2", [128, 512], F32))
        p3 = E(PS("p3", [128, 1024], F32))
        p4 = E(PS("p4", [128, 1024], F32))
        ptr = E(PS("ptr", [128, 1024], BF16))
        ptr2 = E(PS("ptr2", [128, 1024], BF16))


        em.dma("sync", lambda s_: s_.dma_start(out=iden[:], in_=iden_d))
        em.dma("gpsimd", lambda g_: g_.dma_start(out=Wq_s[:], in_=Wq_d))
        em.dma("gpsimd", lambda g_: g_.dma_start(out=K1T_s[:], in_=K1T_d))
        em.dma("gpsimd", lambda g_: g_.dma_start(out=K2T_s[:], in_=K2T_d))
        em.dma("sync", lambda s_: s_.dma_start(
            out=bq_b[:], in_=bq_d.broadcast_to([128, 2048])))
        em.op("vector", lambda v: v.memset(eps_c[:], 1e-5))
        em.bar()

        for t_i in range(NT):
            em.dma("sync", lambda s_, t_i=t_i: s_.dma_start(
                out=xin_t[:], in_=x_t[:, t_i, :]))
            em.bar()
            emit_ln_group(em, [xin_t[:]], [x0[:]], [x0[:]], sq[:, 0:D], cols, D, 1)
            em.op("vector", lambda v: v.tensor_copy(x0b[:], x0[:]))
            em.bar()
            def trx(t):
                r = None
                for kc in range(8):
                    r = t.transpose(ptr[:, kc * 128:(kc + 1) * 128],
                                    x0b[:, kc * 128:(kc + 1) * 128], iden[:])
                return r
            em.op("tensor", trx)
            em.bar()
            em.op("vector", lambda v: v.tensor_copy(
                xTt[:].rearrange("p a m -> p (a m)"), ptr[:]))
            em.bar()
            for half in range(2):
                def mmq(t, half=half):
                    r = None
                    for j in range(2):
                        nb = half * 2 + j
                        pp = p1 if j == 0 else p2
                        for kc in range(8):
                            r = t.matmul(pp[:], xTt[:, kc, :],
                                         Wq_s[:, kc, nb * 512:(nb + 1) * 512],
                                         start=(kc == 0), stop=(kc == 7))
                    return r
                em.op("tensor", mmq)
                em.bar()
                def evq(v, half=half):
                    v.tensor_add(qf[:, half * 1024:half * 1024 + 512], p1[:],
                                 bq_b[:, half * 1024:half * 1024 + 512])
                    return v.tensor_add(qf[:, half * 1024 + 512:half * 1024 + 1024],
                                        p2[:], bq_b[:, half * 1024 + 512:half * 1024 + 1024])
                em.op("vector", evq)
                em.bar()
            em.op("vector", lambda v: v.tensor_copy(qb[:], qf[:]))
            em.bar()
            for half in range(2):
                def trq(t, half=half):
                    r = None
                    for c8 in range(8):
                        r = t.transpose(ptr[:, c8 * 128:(c8 + 1) * 128],
                                        qb[:, half * 1024 + c8 * 128:half * 1024 + (c8 + 1) * 128],
                                        iden[:])
                    return r
                em.op("tensor", trq)
                em.bar()
                em.op("vector", lambda v, half=half: v.tensor_copy(
                    qT[:, half * 8:(half + 1) * 8, :].rearrange("p a m -> p (a m)"),
                    ptr[:]))
                em.bar()
            for rr_ in range(2):
                def mms(t, rr_=rr_):
                    r = None
                    for hh in range(4):
                        h = rr_ * 4 + hh
                        r = t.matmul(p3[:, hh * 256:(hh + 1) * 256],
                                     qT[:, h * 2, :], K1T_s[:], start=True, stop=True)
                        r = t.matmul(p4[:, hh * 256:(hh + 1) * 256],
                                     qT[:, h * 2 + 1, :], K2T_s[:], start=True, stop=True)
                    return r
                em.op("tensor", mms)
                em.bar()
                def evs(v, rr_=rr_):
                    v.tensor_copy(sc1[:, rr_ * 4:(rr_ + 1) * 4, :]
                                  .rearrange("p a b -> p (a b)"), p3[:])
                    return v.tensor_copy(sc2[:, rr_ * 4:(rr_ + 1) * 4, :]
                                         .rearrange("p a b -> p (a b)"), p4[:])
                em.op("vector", evs)
                em.bar()
            def mx1(v):
                r = None
                for h in range(NH):
                    v.max(out=m1a[:, h, :], in_=sc1[:, h, :])
                    r = v.max(out=m2a[:, h, :], in_=sc2[:, h, :])
                return r
            em.op("vector", mx1)
            em.bar()
            def ix1(v):
                r = None
                for h in range(NH):
                    v.max_index(out=i1a[:, h, :], in_max=m1a[:, h, :], in_values=sc1[:, h, :])
                    r = v.max_index(out=i2a[:, h, :], in_max=m2a[:, h, :], in_values=sc2[:, h, :])
                return r
            em.op("vector", ix1)
            em.bar()
            def mr1(v):
                r = None
                for h in range(NH):
                    r = v.match_replace(out=scm[:, h, :], in_to_replace=m1a[:, h, :],
                                        in_values=sc1[:, h, :], imm_value=NEG)
                return r
            em.op("vector", mr1)
            em.bar()
            def mx1b(v):
                r = None
                for h in range(NH):
                    r = v.max(out=m1b[:, h, :], in_=scm[:, h, :])
                return r
            em.op("vector", mx1b)
            em.bar()
            def ix1b(v):
                r = None
                for h in range(NH):
                    r = v.max_index(out=i1b[:, h, :], in_max=m1b[:, h, :], in_values=scm[:, h, :])
                return r
            em.op("vector", ix1b)
            em.bar()
            def mr2(v):
                r = None
                for h in range(NH):
                    r = v.match_replace(out=scm[:, h, :], in_to_replace=m2a[:, h, :],
                                        in_values=sc2[:, h, :], imm_value=NEG)
                return r
            em.op("vector", mr2)
            em.bar()
            def mx2b(v):
                r = None
                for h in range(NH):
                    r = v.max(out=m2b[:, h, :], in_=scm[:, h, :])
                return r
            em.op("vector", mx2b)
            em.bar()
            def ix2b(v):
                r = None
                for h in range(NH):
                    r = v.max_index(out=i2b[:, h, :], in_max=m2b[:, h, :], in_values=scm[:, h, :])
                return r
            em.op("vector", ix2b)
            em.bar()
            def asm(v):
                v.tensor_copy(v1f[:, :, 0:8], m1a[:])
                v.tensor_copy(v1f[:, :, 8:10], m1b[:, :, 0:2])
                v.tensor_copy(v2f[:, :, 0:8], m2a[:])
                v.tensor_copy(v2f[:, :, 8:10], m2b[:, :, 0:2])
                v.tensor_copy(i1f[:, :, 0:8], i1a[:])
                v.tensor_copy(i1f[:, :, 8:10], i1b[:, :, 0:2])
                v.tensor_copy(i2f[:, :, 0:8], i2a[:])
                return v.tensor_copy(i2f[:, :, 8:10], i2b[:, :, 0:2])
            em.op("vector", asm)
            em.bar()
            em.op("vector", lambda v: v.tensor_scalar(
                i1f[:].rearrange("p a b -> p (a b)"),
                i1f[:].rearrange("p a b -> p (a b)"), 256.0, None, ALU.mult))
            em.bar()
            def bld(v):
                v.tensor_scalar(cs4[:],
                                v1f[:].unsqueeze(2).broadcast_to([128, NH, TK, TK]),
                                1.0, None, ALU.mult)
                return v.tensor_scalar(
                    ea4[:], i1f[:].unsqueeze(2).broadcast_to([128, NH, TK, TK]),
                    1.0, None, ALU.mult)
            em.op("vector", bld)
            em.bar()
            def bld2(v):
                cs_sw = cs4[:].rearrange("p h b a -> p h a b")
                ea_sw = ea4[:].rearrange("p h b a -> p h a b")
                v.tensor_add(cs_sw, cs_sw,
                             v2f[:].unsqueeze(2).broadcast_to([128, NH, TK, TK]))
                return v.tensor_add(ea_sw, ea_sw,
                                    i2f[:].unsqueeze(2).broadcast_to([128, NH, TK, TK]))
            em.op("vector", bld2)
            em.bar()
            def cx1(v):
                r = None
                for h in range(NH):
                    r = v.max(out=cm1[:, h, :],
                              in_=cs4[:, h, :, :].rearrange("p a b -> p (a b)"))
                return r
            em.op("vector", cx1)
            em.bar()
            def cmr(v):
                r = None
                for h in range(NH):
                    r = v.match_replace(out=csm[:, h, :], in_to_replace=cm1[:, h, :],
                                        in_values=cs4[:, h, :, :].rearrange("p a b -> p (a b)"),
                                        imm_value=NEG)
                return r
            em.op("vector", cmr)
            em.bar()
            def cx2(v):
                r = None
                for h in range(NH):
                    r = v.max(out=cm2[:, h, :], in_=csm[:, h, :])
                return r
            em.op("vector", cx2)
            em.bar()
            def casm(v):
                v.tensor_copy(cvals[:, :, 0:8], cm1[:])
                return v.tensor_copy(cvals[:, :, 8:10], cm2[:, :, 0:2])
            em.op("vector", casm)
            em.bar()
            for k in range(TK):
                def eq1(v, k=k):
                    r = None
                    for h in range(NH):
                        r = v.tensor_scalar(eqm[:, h, :],
                                            cs4[:, h, :, :].rearrange("p a b -> p (a b)"),
                                            cvals[:, h, k:k + 1], None, ALU.is_equal)
                    return r
                em.op("vector", eq1)
                em.bar()
                em.op("vector", lambda v: v.tensor_mul(
                    eqm[:].rearrange("p a b -> p (a b)"),
                    eqm[:].rearrange("p a b -> p (a b)"),
                    ea4[:].rearrange("p h b a -> p (h b a)")))

                em.bar()
                em.op("vector", lambda v, k=k: v.tensor_reduce(
                    esf[:, :, k], eqm[:], AX.X, ALU.max))
                em.bar()
            em.op("vector", lambda v: v.tensor_copy(
                esel[:].rearrange("p a b -> p (a b)"),
                esf[:].rearrange("p a b -> p (a b)")))
            def sfx(v):
                r = None
                for h in range(NH):
                    r = v.tensor_scalar(hsc[:, h, :], cvals[:, h, :],
                                        cvals[:, h, 0:1], None, ALU.subtract)
                return r
            em.op("vector", sfx)
            em.bar()
            em.op("scalar", lambda sc: sc.activation(
                hsc[:].rearrange("p a b -> p (a b)"),
                hsc[:].rearrange("p a b -> p (a b)"), AF.Exp))
            em.bar()
            em.op("vector", lambda v: v.tensor_reduce(den8[:], hsc[:], AX.X, ALU.add))
            em.bar()
            em.op("vector", lambda v: v.reciprocal(den8[:], den8[:]))
            em.bar()
            def sfx2(v):
                r = None
                for h in range(NH):
                    r = v.tensor_scalar(hsc[:, h, :], hsc[:, h, :],
                                        den8[:, h:h + 1], None, ALU.mult)
                return r
            em.op("vector", sfx2)
            em.bar()
            for si in range(TK + 1):
                if si < TK:
                    for sl in range(8):
                        def gd(g_, k=si, sl=sl, pb=si % 2):
                            return g_.indirect_dma_start(
                                out=Wdg[:, pb, sl, :], out_offset=None,
                                in_=Wdn_d[0:128, :],
                                in_offset=bass.IndirectOffsetOnAxis(
                                    ap=esel[:, sl, k:k + 1], axis=0))
                        em.dma("gpsimd", gd)
                if si >= 1:
                    def dodv(v, pk=si - 1, ppb=(si - 1) % 2):
                        for sl in range(8):
                            v.tensor_mul(dtsb[:, sl, :], Wdg[:, ppb, sl, :], x0b[:])
                        r = None
                        for sl in range(8):
                            r = v.tensor_reduce(
                                hd[:, sl, pk:pk + 1], dtsb[:, sl, :],
                                AX.X, ALU.add)
                        return r
                    em.op("vector", dodv)
                em.bar()
            em.op("scalar", lambda sc: sc.activation(
                esf[:].rearrange("p a b -> p (a b)"),
                hd[:].rearrange("p a b -> p (a b)"), AF.Erf,
                scale=0.7071067811865476))
            em.bar()
            def gel(v):
                t1 = esf[:].rearrange("p a b -> p (a b)")
                h1 = hd[:].rearrange("p a b -> p (a b)")
                v.tensor_mul(t1, t1, h1)
                v.tensor_add(t1, t1, h1)
                v.tensor_mul(h1, t1, hsc[:].rearrange("p a b -> p (a b)"))
                return v.tensor_scalar(h1, h1, 0.5, None, ALU.mult)
            em.op("vector", gel)
            em.op("gpsimd", lambda g_: g_.memset(acc[:], 0.0))
            em.op("gpsimd", lambda g_: g_.memset(acc_g[:], 0.0))
            em.bar()
            for si in range(TK + 1):
                if si < TK:
                    for sl in range(8):
                        def gu(g_, k=si, sl=sl, pb=si % 2):
                            return g_.indirect_dma_start(
                                out=Wug[:, pb, sl, :], out_offset=None,
                                in_=Wup_d[0:128, :],
                                in_offset=bass.IndirectOffsetOnAxis(
                                    ap=esel[:, sl, k:k + 1], axis=0))
                        em.dma("gpsimd", gu)
                if si >= 1:
                    def upav(v, pk=si - 1, ppb=(si - 1) % 2):
                        r = None
                        for sl in range(6):
                            r = v.scalar_tensor_tensor(
                                acc[:], Wug[:, ppb, sl, :],
                                hd[:, sl, pk:pk + 1],
                                acc[:], ALU.mult, ALU.add)
                        return r
                    em.op("vector", upav)
                    def upag(g_, pk=si - 1, ppb=(si - 1) % 2):
                        r = None
                        for sl in (6, 7):
                            g_.tensor_scalar(gtmp[:], Wug[:, ppb, sl, :],
                                             hd[:, sl, pk:pk + 1], None, ALU.mult)
                            r = g_.tensor_add(acc_g[:], acc_g[:], gtmp[:])
                        return r
                    em.op("gpsimd", upag)
                em.bar()
            em.op("vector", lambda v: v.tensor_add(acc[:], acc[:], acc_g[:]))
            emit_ln_group(em, [acc[:]], [acc[:]], [acc[:]], sq[:, 0:D], cols, D, 1)
            em.op("vector", lambda v: v.tensor_add(xin_t[:], xin_t[:], acc[:]))
            em.bar()
            em.dma("sync", lambda s_, t_i=t_i: s_.dma_start(
                out=y_t[:, t_i, :], in_=xin_t[:]))
            em.bar()



def run_l1_ffn(x_full, Wq, bq, K1, K2, W_up, W_down):
    if "l1f" not in _CACHE:
        _CACHE["l1f"] = build_l1_ffn()
    nc = _CACHE["l1f"]
    consts = {"iden": _ident_bf16(), "Wq": _wtile(np.asarray(Wq)),
              "bq": np.asarray(bq).reshape(1, 2048),
              "K1T": np.ascontiguousarray(np.asarray(K1).T),
              "K2T": np.ascontiguousarray(np.asarray(K2).T),
              "Wdn": np.asarray(W_down), "Wup": np.asarray(W_up)}
    in_maps = [{"x": np.ascontiguousarray(x_full[c]), **consts}
               for c in range(NCORES)]
    res = run_bass_kernel_spmd(nc, in_maps, core_ids=list(range(NCORES)))
    return np.stack([np.asarray(r["y"]) for r in res.results])


_CACHE = {}

_L0A_KEYS = ("Win", "Wctrl", "Wout", "bctrl", "bout", "ramp", "wconv",
             "utones", "trilS", "triuS", "triuI", "idf32", "ones1", "ones128")
_L0F_KEYS = tuple([f"{k}{l}" for l in range(2)
                   for k in ("wqkv", "wo", "w1a", "w1g", "w2p", "qs", "ks",
                             "lng", "lnb", "b1a", "b1g", "b2")]
                  + ["fqc", "fqs", "fkc", "fks", "mask"])
_L1A_KEYS = ("Wq", "Wkv", "Wo", "Wo0", "bo", "bo0", "mask", "qcos", "qsin",
             "kcos", "ksin")
_L1F_KEYS = ("Wq", "bq", "K1T", "K2T", "Wdn", "Wup")


def build_fused():
    nc = bass.Bass("TRN2", target_bir_lowering=False)
    x_in = nc.dram_tensor("x", [S, D], F32, kind="ExternalInput").ap()
    y_out = nc.dram_tensor("y", [S, D], F32, kind="ExternalOutput").ap()
    iden_d = nc.dram_tensor("iden", [128, 128], BF16, kind="ExternalInput").ap()
    x01 = nc.dram_tensor("x01", [S, D], F32).ap()
    x12 = nc.dram_tensor("x12", [S, D], F32).ap()
    x23 = nc.dram_tensor("x23", [S, D], F32).ap()

    def DI(name, shape, dt=F32):
        return nc.dram_tensor(name, shape, dt, kind="ExternalInput").ap()

    dd_a = {"Win": DI("a_Win", [128, 8, 3072]),
            "Wctrl": DI("a_Wctrl", [128, 8, 1032]),
            "Wout": DI("a_Wout", [128, 8, 1024]),
            "bctrl": DI("a_bctrl", [1, 1032]), "bout": DI("a_bout", [1, D]),
            "ramp": DI("a_ramp", [1, D]), "wconv": DI("a_wconv", [128, 8, 12]),
            "iden": iden_d}
    for k, v in _l0attn_consts().items():
        dd_a[k] = DI("a_" + k, list(v.shape))

    dd_f = {"iden": iden_d, "mask": DI("f_mask", [128, 4, 1024])}
    for k in ("fqc", "fqs", "fkc", "fks"):
        dd_f[k] = DI("f_" + k, [128, NT, 32])
    shapes_f = {"wqkv": [128, 8, 1536], "wo": [128, 4, 1024],
                "w1a": [128, 8, 2730], "w1g": [128, 8, 2730],
                "w2p": [128, 22, 1024], "qs": [1, 64], "ks": [1, 64],
                "lng": [1, D], "lnb": [1, D], "b1a": [1, 2730],
                "b1g": [1, 2730], "b2": [1, D]}
    for l in range(2):
        for k, shp in shapes_f.items():
            dd_f[f"{k}{l}"] = DI(f"f_{k}{l}", shp)

    dd_s = {"Wq": DI("s_Wq", [128, 8, D]), "Wkv": DI("s_Wkv", [128, 8, D]),
            "Wo": DI("s_Wo", [128, 8, D]), "Wo0": DI("s_Wo0", [128, 8, D]),
            "bo": DI("s_bo", [1, D]), "bo0": DI("s_bo0", [1, D]),
            "iden": iden_d, "mask": DI("s_mask", [128, 4, 1024])}
    for k in ("qcos", "qsin", "kcos", "ksin"):
        dd_s[k] = DI("s_" + k, [128, NT, 32])

    dd_p = {"Wq": DI("p_Wq", [128, 8, 2048]), "bq": DI("p_bq", [1, 2048]),
            "K1T": DI("p_K1T", [128, 256]), "K2T": DI("p_K2T", [128, 256]),
            "Wdn": DI("p_Wdn", [65536, 1024]), "Wup": DI("p_Wup", [65536, 1024]),
            "iden": iden_d}

    with ExitStack() as ctx:
        em = Em(nc)
        emit_l0_attn(nc, em, "a_", x_in, x01, dd_a)
        emit_l0_ffn(nc, em, "f_", x01, x12, dd_f)
        emit_l1_attn(nc, em, "s_", x12, x23, dd_s)
        emit_l1_ffn(nc, em, "p_", x23, y_out, dd_p)
        em.build(ctx)
    return nc


def _host_consts(inputs):
    c = {"iden": _ident_bf16()}
    wcv = np.zeros((1024, 12), np.float32)
    wcv[:, 0:4] = np.asarray(inputs["L0_attn_w_q"])[:, 0, :]
    wcv[:, 4:8] = np.asarray(inputs["L0_attn_w_k"])[:, 0, :]
    wcv[:, 8:12] = np.asarray(inputs["L0_attn_w_v"])[:, 0, :]
    c["a_wconv"] = np.ascontiguousarray(wcv.reshape(8, 128, 12).transpose(1, 0, 2))
    c["a_Win"] = _wtile(np.asarray(inputs["L0_attn_W_in"]))
    c["a_Wctrl"] = _wtile(np.asarray(inputs["L0_attn_W_ctrl"]))
    c["a_Wout"] = _wtile(np.asarray(inputs["L0_attn_W_out"]))
    c["a_bctrl"] = np.asarray(inputs["L0_attn_b_ctrl"]).reshape(1, 1032)
    c["a_bout"] = np.asarray(inputs["L0_attn_b_out"]).reshape(1, D)
    c["a_ramp"] = np.asarray(inputs["L0_attn_ramp_logit"]).reshape(1, D)
    for k, v in _l0attn_consts().items():
        c["a_" + k] = v
    for k, v in _l0ffn_xpos().items():
        c["f_" + k] = v
    c["f_mask"] = _l0ffn_mask()
    w1 = np.asarray(inputs["L0_ffn_w1"])
    w2 = np.asarray(inputs["L0_ffn_w2"])
    for l in range(2):
        w2p = np.zeros((2816, 1024), np.float32)
        w2p[0:2730] = w2[l]
        c[f"f_wqkv{l}"] = _wtile(np.asarray(inputs["L0_ffn_wqkv"][l]))
        c[f"f_wo{l}"] = _wtile(np.asarray(inputs["L0_ffn_wo"][l]))
        c[f"f_w1a{l}"] = _wtile(np.ascontiguousarray(w1[l][:, 0:2730]))
        c[f"f_w1g{l}"] = _wtile(np.ascontiguousarray(w1[l][:, 2730:]))
        c[f"f_w2p{l}"] = _wtile(w2p)
        c[f"f_qs{l}"] = np.asarray(inputs["L0_ffn_q_scale"][l]).reshape(1, 64)
        c[f"f_ks{l}"] = np.asarray(inputs["L0_ffn_k_scale"][l]).reshape(1, 64)
        c[f"f_lng{l}"] = np.asarray(inputs["L0_ffn_ln_g"][l]).reshape(1, D)
        c[f"f_lnb{l}"] = np.asarray(inputs["L0_ffn_ln_b"][l]).reshape(1, D)
        c[f"f_b1a{l}"] = np.ascontiguousarray(
            np.asarray(inputs["L0_ffn_b1"][l])[0:2730]).reshape(1, 2730)
        c[f"f_b1g{l}"] = np.ascontiguousarray(
            np.asarray(inputs["L0_ffn_b1"][l])[2730:]).reshape(1, 2730)
        c[f"f_b2{l}"] = np.asarray(inputs["L0_ffn_b2"][l]).reshape(1, D)
    c["s_Wq"] = _wtile(np.asarray(inputs["L1_attn_Wq"]))
    c["s_Wkv"] = _wtile(np.asarray(inputs["L1_attn_Wkv"]))
    c["s_Wo"] = _wtile(np.asarray(inputs["L1_attn_Wo"]))
    c["s_Wo0"] = _wtile(np.asarray(inputs["L1_attn_Wo0"]))
    c["s_bo"] = np.asarray(inputs["L1_attn_bo"]).reshape(1, D)
    c["s_bo0"] = np.asarray(inputs["L1_attn_bo0"]).reshape(1, D)
    c["s_mask"] = _l1_mask()
    for k, v in _l1_rot().items():
        c["s_" + k] = v
    c["p_Wq"] = _wtile(np.asarray(inputs["L1_ffn_Wq"]))
    c["p_bq"] = np.asarray(inputs["L1_ffn_bq"]).reshape(1, 2048)
    c["p_K1T"] = np.ascontiguousarray(np.asarray(inputs["L1_ffn_K1"]).T)
    c["p_K2T"] = np.ascontiguousarray(np.asarray(inputs["L1_ffn_K2"]).T)
    c["p_Wdn"] = np.asarray(inputs["L1_ffn_W_down"])
    c["p_Wup"] = np.asarray(inputs["L1_ffn_W_up"])
    return c


def kernel(**inputs):
    if "fused" not in _CACHE:
        _CACHE["fused"] = build_fused()
    nc = _CACHE["fused"]
    c = _host_consts(inputs)
    xall = np.asarray(inputs["L0_attn_x"], np.float32)
    in_maps = [{"x": np.ascontiguousarray(xall[ci]), **c} for ci in range(NCORES)]
    res = run_bass_kernel_spmd(nc, in_maps, core_ids=list(range(NCORES)))
    return np.stack([np.asarray(r["y"]) for r in res.results]).astype(np.float32)


# version 09 (final)
# speedup vs baseline: 1.2256x; 1.1878x over previous
import numpy as np
from contextlib import ExitStack
import concourse.bass as bass
import concourse.mybir as mybir
from concourse.bass_utils import run_bass_kernel_spmd

F32 = mybir.dt.float32
BF16 = mybir.dt.bfloat16
U32 = mybir.dt.uint32
U8 = mybir.dt.uint8
AX = mybir.AxisListType
ALU = mybir.AluOpType
AF = mybir.ActivationFunctionType

NCORES = 8
S = 4096
D = 1024
PT = 128
NT = S // PT
NEG = -3.0e38

ENGS = ("sync", "tensor", "vector", "scalar", "gpsimd")


class Em:

    def __init__(self, nc):
        self.nc = nc
        self.items = {e: [] for e in ENGS}

    def op(self, eng, fn):
        self.items[eng].append(("op", fn))

    def dma(self, eng, fn):
        self.items[eng].append(("dma", fn))

    def bar(self):
        for e in ENGS:
            self.items[e].append(("bar",))

    def build(self, ctx):
        nc = self.nc
        bsem = ctx.enter_context(nc.semaphore("bsem"))
        dsems = {e: ctx.enter_context(nc.semaphore(f"dsem_{e}")) for e in ENGS}
        block = ctx.enter_context(nc.Block())
        for e in ENGS:
            def mk(e=e, items=self.items[e]):
                def run(eng):
                    barcount = 0
                    dmacount = 0
                    dma_waited = 0
                    last = None
                    for it in items:
                        kind = it[0]
                        if kind == "op":
                            r = it[1](eng)
                            last = r if r is not None else last
                        elif kind == "dma":
                            it[1](eng).then_inc(dsems[e], 16)
                            dmacount += 1
                            last = None
                        else:
                            if dmacount > dma_waited:
                                eng.wait_ge(dsems[e], 16 * dmacount)
                                dma_waited = dmacount
                            if last is not None:
                                last.then_inc(bsem, 1)
                            else:
                                eng.nop().then_inc(bsem, 1)
                            last = None
                            barcount += 1
                            eng.wait_ge(bsem, len(ENGS) * barcount)
                return run
            getattr(block, e)(mk())


def _ident_bf16():
    import ml_dtypes
    return np.eye(128, dtype=ml_dtypes.bfloat16)

def _l1_mask():
    i = np.arange(512)[:, None]
    j = np.arange(512)[None, :]
    m = np.zeros((512, 1024), np.float32)
    m[:, 512:] = np.where(j > i, NEG, 0.0)
    return np.ascontiguousarray(m.reshape(4, 128, 1024).transpose(1, 0, 2))

def _l1_rot():
    d = 64
    inv = 1.0 / 10000.0 ** (np.arange(0, d, 2, dtype=np.float64) / d)
    t = np.arange(S, dtype=np.float64)
    fr = t[:, None] * inv[None, :]
    cos = np.cos(fr).astype(np.float32)
    sin = np.sin(fr).astype(np.float32)
    sc = np.float32(d ** (-0.5))
    def til(a):
        return np.ascontiguousarray(a.reshape(NT, PT, 32).transpose(1, 0, 2))
    return {"qcos": til(cos * sc), "qsin": til(sin * sc),
            "kcos": til(cos), "ksin": til(sin)}

def _wtile(w):
    k, n = w.shape
    return np.ascontiguousarray(w.reshape(k // PT, PT, n).transpose(1, 0, 2))


def emit_ln_group(em, xs, outs, xcs, sq, cols, dmodel, n, gain=None, bias=None):
    inv = 1.0 / dmodel
    for i in range(n):
        em.op("vector", lambda v, i=i: v.reduce_sum(
            cols["s"][:, i:i + 1], xs[i], axis=AX.X))
    em.bar()
    em.op("vector", lambda v: v.tensor_scalar(
        cols["s"][:, 0:n], cols["s"][:, 0:n], inv, None, ALU.mult))
    em.bar()
    for i in range(n):
        em.op("vector", lambda v, i=i: v.tensor_scalar(
            xcs[i], xs[i], cols["s"][:, i:i + 1], None, ALU.subtract))
    em.bar()
    for i in range(n):
        em.op("scalar", lambda sc, i=i: sc.activation(
            sq, xcs[i], AF.Square, accum_out=cols["ss"][:, i:i + 1]))
    em.bar()
    em.op("scalar", lambda sc: sc.activation(
        cols["r"][:, 0:n], cols["ss"][:, 0:n], AF.Sqrt, scale=inv,
        bias=cols["eps"][:, 0:1]))
    em.bar()
    em.op("vector", lambda v: v.reciprocal(cols["r"][:, 0:n], cols["r"][:, 0:n]))
    em.bar()
    if gain is None:
        for i in range(n):
            em.op("vector", lambda v, i=i: v.tensor_scalar(
                outs[i], xcs[i], cols["r"][:, i:i + 1], None, ALU.mult))
    else:
        for i in range(n):
            em.op("vector", lambda v, i=i: v.scalar_tensor_tensor(
                outs[i], xcs[i], cols["r"][:, i:i + 1], gain, ALU.mult, ALU.mult))
        em.bar()
        for i in range(n):
            em.op("vector", lambda v, i=i: v.tensor_add(outs[i], outs[i], bias))
    em.bar()


def emit_transpose_tiles(em, srcs, dst_fn, iden, p_tr, n, nkc=8):
    for i in range(n):
        def tr(t, i=i):
            r = None
            for kc in range(nkc):
                r = t.transpose(p_tr[:, kc * 128:(kc + 1) * 128],
                                srcs[i][:, kc * 128:(kc + 1) * 128], iden)
            return r
        em.op("tensor", tr)
        em.bar()
        def ev(v, i=i):
            r = None
            for kc in range(nkc):
                r = v.tensor_copy(dst_fn(i, kc), p_tr[:, kc * 128:(kc + 1) * 128])
            return r
        em.op("vector", ev)
        em.bar()



def emit_row_broadcast(em, dst, n, chunk=256):
    def f(g, dst=dst, n=n, chunk=chunk):
        r = None
        for o in range(0, n, chunk):
            r = g.partition_broadcast(dst[:, o:o + chunk], dst[0:1, o:o + chunk])
        return r
    em.op("gpsimd", f)

def emit_l1_attn(nc, em, pfx, x_in, y_out, dd):
    SB = lambda n, s, d: nc.sbuf_tensor(pfx + n, s, d)
    PS = lambda n, s, d: nc.psum_tensor(pfx + n, s, d)
    DR = lambda n, s, d: nc.dram_tensor(pfx + n, s, d).ap()
    Wq_d = dd["Wq"]; Wkv_d = dd["Wkv"]; Wo_d = dd["Wo"]; Wo0_d = dd["Wo0"]
    bo_d = dd["bo"]; bo0_d = dd["bo0"]; iden_d = dd["iden"]; mask_d = dd["mask"]
    rot_d = {k: dd[k] for k in ("qcos", "qsin", "kcos", "ksin")}
    qT2 = DR("qT2", [8, 128, S], BF16)
    kvT2 = DR("kvT2", [8, 128, S], BF16)
    kvt_d = DR("kvt_scr", [S, D], BF16)

    x_t = x_in.rearrange("(t p) d -> p t d", p=PT)
    y_t = y_out.rearrange("(t p) d -> p t d", p=PT)
    kvt_t = kvt_d.rearrange("(t p) d -> p t d", p=PT)

    G = 4
    NG = NT // G
    SEG = 512
    H = 16

    with ExitStack() as ctx:
        E = ctx.enter_context
        WA_s = E(SB("WA_s", [128, 8, D], BF16))
        WB_s = E(SB("WB_s", [128, 8, D], BF16))
        iden = E(SB("iden_s", [128, 128], BF16))
        bo_b = E(SB("bo_b", [128, D], F32))
        bo0_b = E(SB("bo0_b", [128, D], F32))
        eps_c = E(SB("eps_c", [128, 1], F32))
        rot_g = {k: E(SB(f"{k}_g", [128, G, 32], F32))
                 for k in ("qcos", "qsin", "kcos", "ksin")}
        sq = E(SB("sq", [128, D], F32))
        rsc = E(SB("rsc", [128, 2, 512], F32))
        arF = E(SB("arF", [128, 12288], F32))
        xg_a = arF[:, 0:4096].rearrange("p (i d) -> p i d", d=D)
        xc_a = arF[:, 4096:8192].rearrange("p (i d) -> p i d", d=D)
        probs4 = arF[:, 0:4096].rearrange("p (a k) -> p a k", k=1024)
        mask_s = arF[:, 4096:8192].rearrange("p (a k) -> p a k", k=1024)
        yb = arF[:, 8192:9216]
        ybc = arF[:, 9216:10240]
        xin2 = arF[:, 10240:11264]
        arB = E(SB("arB", [128, 16384], BF16))
        xln_a = arB[:, 0:4096].rearrange("p (i d) -> p i d", d=D)
        xT_a = arB[:, 4096:8192].rearrange("p (kc m) -> p kc m", m=G * PT)
        qrot_a = arB[:, 8192:12288].rearrange("p (i d) -> p i d", d=D)
        kvTs = arB[:, 0:8192].rearrange("p (f t) -> p f t", t=2 * SEG)
        kvts = arB[:, 8192:16384].rearrange("p (i d) -> p i d", d=D)
        qTs = E(SB("qTs", [128, 8, SEG], BF16))
        probsb = E(SB("probsb", [128, 4, 1024], BF16))
        probsT = E(SB("probsT", [128, 8, SEG], BF16))
        oTs = E(SB("oTs", [128, 8, SEG], BF16))
        mx4 = E(SB("mx4", [128, 4], F32))
        den4 = E(SB("den4", [128, 4], F32))
        cols = {"s": E(SB("ln_s", [128, G], F32)),
                "ss": E(SB("ln_ss", [128, G], F32)),
                "r": E(SB("ln_r", [128, G], F32)),
                "eps": eps_c}
        p_a = E(PS("p_a", [128, 2, 512], F32))
        p_b = E(PS("p_b", [128, 2, 512], F32))
        p_tr = E(PS("p_tr", [128, 1024], BF16))
        p_tr2 = E(PS("p_tr2", [128, 1024], BF16))
        p_av = E(PS("p_av", [64, 512], F32))


        em.dma("gpsimd", lambda g: g.dma_start(out=WA_s[:], in_=Wq_d))
        em.dma("gpsimd", lambda g: g.dma_start(out=WB_s[:], in_=Wkv_d))
        em.dma("sync", lambda s_: s_.dma_start(out=iden[:], in_=iden_d))
        em.dma("sync", lambda s_: s_.dma_start(out=bo_b[:], in_=bo_d.broadcast_to([128, D])))
        em.dma("sync", lambda s_: s_.dma_start(out=bo0_b[:], in_=bo0_d.broadcast_to([128, D])))
        em.op("vector", lambda v: v.memset(eps_c[:], 1e-5))
        em.bar()

        for g in range(NG):
            t0 = g * G
            for i in range(G):
                em.dma("sync", lambda s_, i=i, t0=t0: s_.dma_start(
                    out=xg_a[:, i, :], in_=x_t[:, t0 + i, :]))
            for k in ("qcos", "qsin", "kcos", "ksin"):
                em.dma("sync", lambda s_, k=k, t0=t0: s_.dma_start(
                    out=rot_g[k][:], in_=rot_d[k][:, t0:t0 + G, :]))
            em.bar()
            emit_ln_group(em, [xg_a[:, i, :] for i in range(G)],
                          [xln_a[:, i, :] for i in range(G)],
                          [xc_a[:, i, :] for i in range(G)], sq[:], cols, D, G)
            emit_transpose_tiles(
                em, [xln_a[:, i, :] for i in range(G)],
                lambda i, kc: xT_a[:, kc, i * 128:(i + 1) * 128], iden[:],
                p_tr[:], G)
            for nm, W_s, scrT, ckey, skey in (
                    ("q", WA_s, qT2, "qcos", "qsin"),
                    ("kv", WB_s, kvT2, "kcos", "ksin")):
                for i in range(G):
                    def mm1(t, i=i, W_s=W_s):
                        for kc in range(8):
                            t.matmul(p_a[:, 0, :], xT_a[:, kc, i * 128:(i + 1) * 128],
                                     W_s[:, kc, 0:512], start=(kc == 0), stop=(kc == 7))
                        r = None
                        for kc in range(8):
                            r = t.matmul(p_b[:, 0, :], xT_a[:, kc, i * 128:(i + 1) * 128],
                                         W_s[:, kc, 512:1024], start=(kc == 0), stop=(kc == 7))
                        return r
                    em.op("tensor", mm1)
                    em.bar()
                    em.op("vector", lambda v, i=i: v.tensor_copy(
                        xg_a[:, i, 0:512], p_a[:, 0, :]))
                    em.op("scalar", lambda s_, i=i: s_.copy(
                        xg_a[:, i, 512:1024], p_b[:, 0, :]))
                    em.bar()
                for i in range(G):
                    q3 = xg_a[:, i, :].rearrange("p (h two d) -> p h two d", h=H, two=2)
                    o3 = qrot_a[:, i, :].rearrange("p (h two d) -> p h two d", h=H, two=2)
                    cb = rot_g[ckey][:, i, :].unsqueeze(1).broadcast_to([128, H, 32])
                    sb = rot_g[skey][:, i, :].unsqueeze(1).broadcast_to([128, H, 32])
                    t_a = rsc[:, 0, :].rearrange("p (h d) -> p h d", h=H)
                    t_b = rsc[:, 1, :].rearrange("p (h d) -> p h d", h=H)
                    em.op("vector", lambda v, q3=q3, sb=sb, t_a=t_a:
                          v.tensor_mul(t_a, q3[:, :, 1, :], sb))
                    em.op("gpsimd", lambda g_, q3=q3, sb=sb, t_b=t_b:
                          g_.tensor_mul(t_b, q3[:, :, 0, :], sb))
                    em.bar()
                    def rot2(v, q3=q3, cb=cb, o3=o3):
                        v.tensor_mul(o3[:, :, 0, :], q3[:, :, 0, :], cb)
                        return v.tensor_mul(o3[:, :, 1, :], q3[:, :, 1, :], cb)
                    em.op("vector", rot2)
                    em.bar()
                    em.op("vector", lambda v, o3=o3, t_a=t_a:
                          v.tensor_sub(o3[:, :, 0, :], o3[:, :, 0, :], t_a))
                    em.op("gpsimd", lambda g_, o3=o3, t_b=t_b:
                          g_.tensor_add(o3[:, :, 1, :], o3[:, :, 1, :], t_b))
                    em.bar()
                xcb = arF[:, 4096:8192].bitcast(BF16)
                emit_transpose_tiles(
                    em, [qrot_a[:, i, :] for i in range(G)],
                    lambda i, kc: xcb[:, (i * 8 + kc) * 128:(i * 8 + kc + 1) * 128],
                    iden[:], p_tr2[:], G)
                for i in range(G):
                    for kc in range(8):
                        em.dma("sync", lambda s_, i=i, kc=kc, scrT=scrT, t0=t0, xcb=xcb:
                               s_.dma_start(
                                   out=scrT[kc, :, (t0 + i) * 128:(t0 + i + 1) * 128],
                                   in_=xcb[:, (i * 8 + kc) * 128:(i * 8 + kc + 1) * 128]))
                if nm == "kv":
                    for i in range(G):
                        em.dma("sync", lambda s_, i=i, t0=t0: s_.dma_start(
                            out=kvt_t[:, t0 + i, :], in_=qrot_a[:, i, :]))
                em.bar()

        em.dma("gpsimd", lambda g: g.dma_start(out=WA_s[:], in_=Wo_d))
        em.dma("gpsimd", lambda g: g.dma_start(out=WB_s[:], in_=Wo0_d))
        em.dma("sync", lambda s_: s_.dma_start(out=mask_s[:, :, :], in_=mask_d))
        em.bar()

        for s in range(8):
            em.dma("sync", lambda s_, s=s: s_.dma_start(
                out=qTs[:], in_=qT2.rearrange("f p t -> p f t")[:, :, s * SEG:(s + 1) * SEG]))
            lo = max(0, (s - 1) * SEG)
            em.dma("sync", lambda s_, s=s, lo=lo: s_.dma_start(
                out=kvTs[:, :, (0 if s > 0 else SEG):],
                in_=kvT2.rearrange("f p t -> p f t")[:, :, lo:(s + 1) * SEG]))
            nti = 8 if s > 0 else 4
            for i in range(nti):
                ti = ((s - 1) * 4 + i) if s > 0 else i
                sl = i if s > 0 else (4 + i)
                em.dma("sync", lambda s_, sl=sl, ti=ti: s_.dma_start(
                    out=kvts[:, sl, :], in_=kvt_t[:, ti, :]))
            em.bar()
            nk = 2 * SEG if s > 0 else SEG
            ko = 0 if s > 0 else SEG
            ntk = nk // 128
            for h in range(H):
                hb = (h % 2) * 64
                fc = h // 2
                for r in range(2):
                    def mmsc(t, h=h, hb=hb, fc=fc, s=s, r=r):
                        rr = None
                        for j in range(2):
                            tqc = 2 * r + j
                            lhs = qTs[hb:hb + 64, fc, tqc * 128:(tqc + 1) * 128]
                            if s > 0:
                                t.matmul(p_a[:, j, :], lhs,
                                         kvTs[hb:hb + 64, fc, 0:512],
                                         start=True, stop=True)
                            rr = t.matmul(p_b[:, j, :], lhs,
                                          kvTs[hb:hb + 64, fc, SEG:2 * SEG],
                                          start=True, stop=True)
                        return rr
                    em.op("tensor", mmsc)
                    em.bar()
                    def evsc(v, s=s, r=r):
                        if s > 0:
                            v.tensor_add(probs4[:, 2 * r:2 * r + 2, 0:512],
                                         p_a[:, :, :], mask_s[:, 2 * r:2 * r + 2, 0:512])
                        return v.tensor_add(probs4[:, 2 * r:2 * r + 2, 512:1024],
                                            p_b[:, :, :], mask_s[:, 2 * r:2 * r + 2, 512:1024])
                    em.op("vector", evsc)
                    em.bar()
                em.op("vector", lambda v, ko=ko: v.tensor_reduce(
                    mx4[:], probs4[:, :, ko:1024], AX.X, ALU.max, negate=True))
                em.bar()
                def sfm1(v, ko=ko):
                    rr = None
                    for a in range(4):
                        rr = v.tensor_scalar(probs4[:, a, ko:1024],
                                             probs4[:, a, ko:1024],
                                             mx4[:, a:a + 1], None, ALU.add)
                    return rr
                em.op("vector", sfm1)
                em.bar()
                em.op("scalar", lambda sc, ko=ko: sc.activation(
                    probs4[:, :, ko:1024], probs4[:, :, ko:1024], AF.Exp))
                em.bar()
                em.op("vector", lambda v, ko=ko: v.tensor_reduce(
                    den4[:], probs4[:, :, ko:1024], AX.X, ALU.add))
                em.bar()
                em.op("vector", lambda v: v.reciprocal(den4[:], den4[:]))
                em.bar()
                def sfm2(v, ko=ko):
                    rr = None
                    for a in range(4):
                        rr = v.tensor_scalar(probsb[:, a, ko:1024],
                                             probs4[:, a, ko:1024],
                                             den4[:, a:a + 1], None, ALU.mult)
                    return rr
                em.op("vector", sfm2)
                em.bar()
                for tqc in range(4):
                    pp = p_tr if tqc % 2 == 0 else p_tr2
                    def trp(t, tqc=tqc, pp=pp, ko=ko, ntk=ntk):
                        rr = None
                        for tkc in range(ntk):
                            rr = t.transpose(
                                pp[:, tkc * 128:(tkc + 1) * 128],
                                probsb[:, tqc, ko + tkc * 128:ko + (tkc + 1) * 128],
                                iden[:])
                        return rr
                    em.op("tensor", trp)
                    em.bar()
                    def evp(v, tqc=tqc, pp=pp, ntk=ntk):
                        return v.tensor_copy(
                            probsT[:, 0:ntk, tqc * 128:(tqc + 1) * 128],
                            pp[:, 0:ntk * 128].rearrange("p (a m) -> p a m", m=128))
                    em.op("vector", evp)
                    em.bar()
                def mmav(t, h=h, s=s, ntk=ntk):
                    rr = None
                    for tkc in range(ntk):
                        kvtile = kvts[:, (tkc if s > 0 else 4 + tkc), h * 64:(h + 1) * 64]
                        rr = t.matmul(p_av[:], kvtile, probsT[:, tkc, :],
                                      start=(tkc == 0), stop=(tkc == ntk - 1))
                    return rr
                em.op("tensor", mmav)
                em.bar()
                em.op("vector", lambda v, hb=hb, fc=fc: v.tensor_copy(
                    oTs[hb:hb + 64, fc, :], p_av[:]))
                em.bar()
            W_s = WB_s if s == 0 else WA_s
            bias_b = bo0_b if s == 0 else bo_b
            for i in range(4):
                em.dma("sync", lambda s_, s=s, i=i: s_.dma_start(
                    out=xin2[:], in_=x_t[:, s * 4 + i, :]))
                def mmo(t, i=i, W_s=W_s):
                    for kc in range(8):
                        t.matmul(p_a[:, 0, :], oTs[:, kc, i * 128:(i + 1) * 128],
                                 W_s[:, kc, 0:512], start=(kc == 0), stop=(kc == 7))
                    rr = None
                    for kc in range(8):
                        rr = t.matmul(p_b[:, 0, :], oTs[:, kc, i * 128:(i + 1) * 128],
                                      W_s[:, kc, 512:1024], start=(kc == 0), stop=(kc == 7))
                    return rr
                em.op("tensor", mmo)
                em.bar()
                def evo(v, bias_b=bias_b):
                    v.tensor_add(yb[:, 0:512], p_a[:, 0, :], bias_b[:, 0:512])
                    return v.tensor_add(yb[:, 512:1024], p_b[:, 0, :], bias_b[:, 512:1024])
                em.op("vector", evo)
                em.bar()
                emit_ln_group(em, [yb[:]], [ybc[:]], [ybc[:]], sq[:], cols, D, 1)
                em.op("vector", lambda v: v.tensor_add(xin2[:], xin2[:], ybc[:]))
                em.bar()
                em.dma("sync", lambda s_, s=s, i=i: s_.dma_start(
                    out=y_t[:, s * 4 + i, :], in_=xin2[:]))
                em.bar()




def _l0ffn_xpos():
    dh = 64
    half = np.arange(0, dh, 2, dtype=np.float64)
    inv = 1.0 / 10000.0 ** (half / dh)
    t = np.arange(S, dtype=np.float64)
    fr = t[:, None] * inv[None, :]
    cos = np.cos(fr)
    sin = np.sin(fr)
    base = (half + 0.4 * dh) / (1.4 * dh)
    power = (t - S // 2) / 256.0
    scale = base[None, :] ** power[:, None]
    def til(a):
        return np.ascontiguousarray(
            a.astype(np.float32).reshape(NT, PT, 32).transpose(1, 0, 2))
    return {"fqc": til(cos * scale * 8.0), "fqs": til(sin * scale * 8.0),
            "fkc": til(cos / scale), "fks": til(sin / scale)}

def _l0ffn_mask():
    il = np.arange(512)[:, None]
    jl = np.arange(512)[None, :]
    m = np.zeros((512, 1024), np.float32)
    m[:, 0:512] = np.where(jl < il, NEG, 0.0)
    m[:, 512:] = np.where(jl > il, NEG, 0.0)
    return np.ascontiguousarray(m.reshape(4, 128, 1024).transpose(1, 0, 2))

def emit_l0_ffn(nc, em, pfx, x_in, y_out, dd):
    SB = lambda n, s, d: nc.sbuf_tensor(pfx + n, s, d)
    PS = lambda n, s, d: nc.psum_tensor(pfx + n, s, d)
    DR = lambda n, s, d: nc.dram_tensor(pfx + n, s, d).ap()
    iden_d = dd["iden"]; mask_d = dd["mask"]
    rot_d = {k: dd[k] for k in ("fqc", "fqs", "fkc", "fks")}
    WL = [{k: dd[f"{k}{l}"] for k in ("wqkv", "wo", "w1a", "w1g", "w2p", "qs",
                                      "ks", "lng", "lnb", "b1a", "b1g", "b2")}
          for l in range(2)]
    xcur = DR("xcur", [S, D], F32)
    qT2 = DR("qT2", [4, 128, S], BF16)
    kT2 = DR("kT2", [4, 128, S], BF16)
    vt2 = DR("vt2", [S, 512], BF16)
    hT2 = DR("hT2", [8, 128, S], BF16)
    h3a2 = DR("h3a2", [S, 2730], F32)

    x_t = x_in.rearrange("(t p) d -> p t d", p=PT)
    y_t = y_out.rearrange("(t p) d -> p t d", p=PT)
    xc_t = xcur.rearrange("(t p) d -> p t d", p=PT)
    vt_t = vt2.rearrange("(t p) d -> p t d", p=PT)
    h3a_t2 = h3a2.rearrange("(t p) d -> p t d", p=PT)

    G = 4
    NG = NT // G
    SEG = 512
    H = 8

    with ExitStack() as ctx:
        E = ctx.enter_context
        iden = E(SB("iden_s", [128, 128], BF16))
        eps_c = E(SB("eps_c", [128, 1], F32))
        sq = E(SB("sq", [128, D], F32))
        rsc = E(SB("rsc", [128, 2, 512], F32))
        ss8 = E(SB("ss8", [128, 16], F32))
        r8 = E(SB("r8", [128, 16], F32))
        qsb = E(SB("qsb", [128, 64], F32))
        ksb = E(SB("ksb", [128, 64], F32))
        lngb = E(SB("lngb", [128, D], F32))
        lnbb = E(SB("lnbb", [128, D], F32))
        b2b = E(SB("b2b", [128, D], F32))
        rotg = {k: E(SB(f"{k}_g", [128, G, 32], F32))
                for k in ("fqc", "fqs", "fkc", "fks")}
        cols = {"s": E(SB("ln_s", [128, G], F32)),
                "ss": E(SB("ln_ss", [128, G], F32)),
                "r": E(SB("ln_r", [128, G], F32)),
                "eps": eps_c}
        mx4 = E(SB("mx4", [128, 4], F32))
        den4 = E(SB("den4", [128, 4], F32))
        arF = E(SB("arF", [128, 18432], F32))
        xg = arF[:, 0:4096].rearrange("p (i d) -> p i d", d=D)
        qkvf = arF[:, 4096:10240].rearrange("p (i d) -> p i d", d=1536)
        probs4 = arF[:, 10240:14336].rearrange("p (a k) -> p a k", k=1024)
        mask_s = arF[:, 14336:18432].rearrange("p (a k) -> p a k", k=1024)
        hx = arF[:, 0:1024]
        b1ab = arF[:, 1024:3754]
        h3a_s = arF[:, 3754:6484]
        h3g_s = arF[:, 6484:9214]
        xt_c = arF[:, 9214:10238]
        arB = E(SB("arB", [128, 55304], BF16))
        wqkv_s = arB[:, 0:12288].rearrange("p (k n) -> p k n", n=1536)
        wo_s = arB[:, 12288:16384].rearrange("p (k n) -> p k n", n=1024)
        xTg = arB[:, 16384:20488].rearrange("p (k m) -> p k m", m=513)
        xbg = arB[:, 20488:24584].rearrange("p (i d) -> p i d", d=D)
        qrot = arB[:, 24584:26632].rearrange("p (i d) -> p i d", d=512)
        krot = arB[:, 26632:28680].rearrange("p (i d) -> p i d", d=512)
        vbf = arB[:, 28680:30728].rearrange("p (i d) -> p i d", d=512)
        qTs = arB[:, 34824:36872].rearrange("p (f t) -> p f t", t=512)
        kTs = arB[:, 36872:40968].rearrange("p (f t) -> p f t", t=1024)
        vts = arB[:, 40968:45064].rearrange("p (i d) -> p i d", d=512)
        probsb = arB[:, 45064:49160].rearrange("p (a k) -> p a k", k=1024)
        probsT = arB[:, 49160:53256].rearrange("p (a m) -> p a m", m=512)
        oTs = arB[:, 53256:55304].rearrange("p (f t) -> p f t", t=512)
        w1x_s = arB[:, 0:21840].rearrange("p (k n) -> p k n", n=2730)
        hb_c = arB[:, 21840:22864]
        hT_c1 = arB[:, 22864:23888].rearrange("p (k m) -> p k m", m=128)
        w2p_s = arB[:, 21840:44368].rearrange("p (k n) -> p k n", n=1024)
        actb = arB[:, 44368:47184]
        actT = arB[:, 47184:50000].rearrange("p (k m) -> p k m", m=128)
        hT_c2 = arB[:, 50000:51024].rearrange("p (k m) -> p k m", m=128)
        p_a = E(PS("p_a", [128, 2, 512], F32))
        p_b = E(PS("p_b", [128, 2, 512], F32))
        p_tr = E(PS("p_tr", [128, 1024], BF16))
        p_tr2 = E(PS("p_tr2", [128, 1024], BF16))
        p_av = E(PS("p_av", [64, 512], F32))


        em.dma("sync", lambda s_: s_.dma_start(out=iden[:], in_=iden_d))
        em.dma("sync", lambda s_: s_.dma_start(out=mask_s, in_=mask_d))
        em.op("vector", lambda v: v.memset(eps_c[:], 1e-5))
        em.bar()

        for g in range(NG):
            t0 = g * G
            for i in range(G):
                em.dma("sync", lambda s_, i=i, t0=t0: s_.dma_start(
                    out=xg[:, i, :], in_=x_t[:, t0 + i, :]))
            em.bar()
            emit_ln_group(em, [xg[:, i, :] for i in range(G)],
                          [qkvf[:, i, 0:1024] for i in range(G)],
                          [qkvf[:, i, 0:1024] for i in range(G)], sq[:], cols, D, G)
            for i in range(G):
                em.dma("sync", lambda s_, i=i, t0=t0: s_.dma_start(
                    out=xc_t[:, t0 + i, :], in_=qkvf[:, i, 0:1024]))
            em.bar()

        for l in range(2):
            W = WL[l]
            em.dma("gpsimd", lambda g_, W=W: g_.dma_start(out=wqkv_s, in_=W["wqkv"]))
            em.dma("gpsimd", lambda g_, W=W: g_.dma_start(out=wo_s, in_=W["wo"]))
            em.dma("sync", lambda s_, W=W: s_.dma_start(
                out=qsb[:], in_=W["qs"].broadcast_to([128, 64])))
            em.dma("sync", lambda s_, W=W: s_.dma_start(
                out=ksb[:], in_=W["ks"].broadcast_to([128, 64])))
            em.bar()
            for g in range(NG):
                t0 = g * G
                if g == 0:
                    em.op("vector", lambda v: v.memset(xTg[:, :, 0:1], 0.0))
                else:
                    em.op("vector", lambda v: v.tensor_copy(
                        xTg[:, :, 0:1], xTg[:, :, 512:513]))
                for i in range(G):
                    em.dma("sync", lambda s_, i=i, t0=t0: s_.dma_start(
                        out=xg[:, i, :], in_=xc_t[:, t0 + i, :]))
                for k in ("fqc", "fqs", "fkc", "fks"):
                    em.dma("sync", lambda s_, k=k, t0=t0: s_.dma_start(
                        out=rotg[k][:], in_=rot_d[k][:, t0:t0 + G, :]))
                em.bar()
                em.op("vector", lambda v: v.tensor_copy(
                    arB[:, 20488:24584], arF[:, 0:4096]))
                em.bar()
                emit_transpose_tiles(
                    em, [xbg[:, i, :] for i in range(G)],
                    lambda i, kc: xTg[:, kc, 1 + i * 128:1 + (i + 1) * 128],
                    iden[:], p_tr[:], G)
                for i in range(G):
                    def mmq(t, i=i):
                        r = None
                        for nb, pp in ((0, p_a[:, 0, :]), (1, p_a[:, 1, :]),
                                       (2, p_b[:, 0, :])):
                            for kc in range(8):
                                off = 1 if kc < 4 else 0
                                r = t.matmul(pp, xTg[:, kc, off + i * 128:off + (i + 1) * 128],
                                             wqkv_s[:, kc, nb * 512:(nb + 1) * 512],
                                             start=(kc == 0), stop=(kc == 7))
                        return r
                    em.op("tensor", mmq)
                    em.bar()
                    def evq(v, i=i):
                        v.tensor_copy(qkvf[:, i, 0:512], p_a[:, 0, :])
                        return v.tensor_copy(qkvf[:, i, 1024:1536], p_b[:, 0, :])
                    em.op("vector", evq)
                    em.op("scalar", lambda sc, i=i: sc.copy(
                        qkvf[:, i, 512:1024], p_a[:, 1, :]))
                    em.bar()
                for i in range(G):
                    em.op("scalar", lambda sc, i=i: sc.activation(
                        sq[:, 0:512], qkvf[:, i, 0:512], AF.Square))
                    em.op("vector", lambda v, i=i: v.tensor_copy(
                        vbf[:, i, :], qkvf[:, i, 1024:1536]))
                    em.bar()
                    em.op("scalar", lambda sc, i=i: sc.activation(
                        sq[:, 512:1024], qkvf[:, i, 512:1024], AF.Square))
                    em.op("vector", lambda v: v.tensor_reduce(
                        ss8[:, 0:8], sq[:, 0:512].rearrange("p (h d) -> p h d", d=64),
                        AX.X, ALU.add))
                    em.bar()
                    em.op("vector", lambda v: v.tensor_reduce(
                        ss8[:, 8:16], sq[:, 512:1024].rearrange("p (h d) -> p h d", d=64),
                        AX.X, ALU.add))
                    em.bar()
                    em.op("scalar", lambda sc: sc.activation(r8[:], ss8[:], AF.Sqrt))
                    em.bar()
                    em.op("vector", lambda v: v.tensor_scalar_max(r8[:], r8[:], 1e-12))
                    em.bar()
                    em.op("vector", lambda v: v.reciprocal(r8[:], r8[:]))
                    em.bar()
                    def nrm(v, i=i):
                        r = None
                        for h in range(8):
                            v.scalar_tensor_tensor(
                                qkvf[:, i, h * 64:(h + 1) * 64],
                                qkvf[:, i, h * 64:(h + 1) * 64],
                                r8[:, h:h + 1], qsb[:], ALU.mult, ALU.mult)
                            r = v.scalar_tensor_tensor(
                                qkvf[:, i, 512 + h * 64:512 + (h + 1) * 64],
                                qkvf[:, i, 512 + h * 64:512 + (h + 1) * 64],
                                r8[:, 8 + h:9 + h], ksb[:], ALU.mult, ALU.mult)
                        return r
                    em.op("vector", nrm)
                    em.bar()
                    for (o0, ck, sk, dst) in ((0, "fqc", "fqs", qrot),
                                              (512, "fkc", "fks", krot)):
                        q3 = qkvf[:, i, o0:o0 + 512].rearrange(
                            "p (h two d) -> p h two d", h=H, two=2)
                        o3 = dst[:, i, :].rearrange(
                            "p (h two d) -> p h two d", h=H, two=2)
                        cb = rotg[ck][:, i, :].unsqueeze(1).broadcast_to([128, H, 32])
                        sb = rotg[sk][:, i, :].unsqueeze(1).broadcast_to([128, H, 32])
                        t_a = rsc[:, 0, 0:256].rearrange("p (h d) -> p h d", h=H)
                        t_b = rsc[:, 1, 0:256].rearrange("p (h d) -> p h d", h=H)
                        def rt1(v, q3=q3, sb=sb, t_a=t_a, t_b=t_b):
                            v.tensor_mul(t_a, q3[:, :, 1, :], sb)
                            return v.tensor_mul(t_b, q3[:, :, 0, :], sb)
                        em.op("vector", rt1)
                        em.bar()
                        def rt2(v, q3=q3, cb=cb, o3=o3):
                            v.tensor_mul(o3[:, :, 0, :], q3[:, :, 0, :], cb)
                            return v.tensor_mul(o3[:, :, 1, :], q3[:, :, 1, :], cb)
                        em.op("vector", rt2)
                        em.bar()
                        def rt3(v, o3=o3, t_a=t_a, t_b=t_b):
                            v.tensor_sub(o3[:, :, 0, :], o3[:, :, 0, :], t_a)
                            return v.tensor_add(o3[:, :, 1, :], o3[:, :, 1, :], t_b)
                        em.op("vector", rt3)
                        em.bar()
                for i in range(G):
                    em.dma("sync", lambda s_, i=i, t0=t0: s_.dma_start(
                        out=vt_t[:, t0 + i, :], in_=vbf[:, i, :]))
                for nm, src, scrT in (("q", qrot, qT2), ("k", krot, kT2)):
                    emit_transpose_tiles(
                        em, [src[:, i, :] for i in range(G)],
                        lambda i, kc: arB[:, 30728 + (i * 4 + kc) * 128:
                                          30728 + (i * 4 + kc + 1) * 128],
                        iden[:], p_tr2[:], G, nkc=4)
                    for i in range(G):
                        for kc in range(4):
                            em.dma("sync", lambda s_, i=i, kc=kc, scrT=scrT, t0=t0:
                                   s_.dma_start(
                                       out=scrT[kc, :, (t0 + i) * 128:(t0 + i + 1) * 128],
                                       in_=arB[:, 30728 + (i * 4 + kc) * 128: 30728 + (i * 4 + kc + 1) * 128]))
                    em.bar()
            for w in range(8):
                em.dma("sync", lambda s_, w=w: s_.dma_start(
                    out=qTs, in_=qT2.rearrange("f p t -> p f t")[:, :, w * SEG:(w + 1) * SEG]))
                lo = max(0, (w - 1) * SEG)
                em.dma("sync", lambda s_, w=w, lo=lo: s_.dma_start(
                    out=kTs[:, :, (0 if w > 0 else SEG):],
                    in_=kT2.rearrange("f p t -> p f t")[:, :, lo:(w + 1) * SEG]))
                nti = 8 if w > 0 else 4
                for i in range(nti):
                    ti = ((w - 1) * 4 + i) if w > 0 else i
                    sl = i if w > 0 else (4 + i)
                    em.dma("sync", lambda s_, sl=sl, ti=ti: s_.dma_start(
                        out=vts[:, sl, :], in_=vt_t[:, ti, :]))
                em.bar()
                nk = 2 * SEG if w > 0 else SEG
                ko = 0 if w > 0 else SEG
                ntk = nk // 128
                for h in range(H):
                    hb = (h % 2) * 64
                    fc = h // 2
                    for r in range(2):
                        def mmsc(t, hb=hb, fc=fc, w=w, r=r):
                            rr = None
                            for j in range(2):
                                tqc = 2 * r + j
                                lhs = qTs[hb:hb + 64, fc, tqc * 128:(tqc + 1) * 128]
                                if w > 0:
                                    t.matmul(p_a[:, j, :], lhs,
                                             kTs[hb:hb + 64, fc, 0:512],
                                             start=True, stop=True)
                                rr = t.matmul(p_b[:, j, :], lhs,
                                              kTs[hb:hb + 64, fc, SEG:2 * SEG],
                                              start=True, stop=True)
                            return rr
                        em.op("tensor", mmsc)
                        em.bar()
                        def evsc(v, w=w, r=r):
                            if w > 0:
                                v.tensor_add(probs4[:, 2 * r:2 * r + 2, 0:512],
                                             p_a[:, :, :], mask_s[:, 2 * r:2 * r + 2, 0:512])
                            return v.tensor_add(probs4[:, 2 * r:2 * r + 2, 512:1024],
                                                p_b[:, :, :], mask_s[:, 2 * r:2 * r + 2, 512:1024])
                        em.op("vector", evsc)
                        em.bar()
                    em.op("vector", lambda v, ko=ko: v.tensor_reduce(
                        mx4[:], probs4[:, :, ko:1024], AX.X, ALU.max, negate=True))
                    em.bar()
                    def sfm1(v, ko=ko):
                        rr = None
                        for a in range(4):
                            rr = v.tensor_scalar(probs4[:, a, ko:1024],
                                                 probs4[:, a, ko:1024],
                                                 mx4[:, a:a + 1], None, ALU.add)
                        return rr
                    em.op("vector", sfm1)
                    em.bar()
                    em.op("scalar", lambda sc, ko=ko: sc.activation(
                        probs4[:, :, ko:1024], probs4[:, :, ko:1024], AF.Exp))
                    em.bar()
                    em.op("vector", lambda v, ko=ko: v.tensor_reduce(
                        den4[:], probs4[:, :, ko:1024], AX.X, ALU.add))
                    em.bar()
                    em.op("vector", lambda v: v.reciprocal(den4[:], den4[:]))
                    em.bar()
                    def sfm2(v, ko=ko):
                        rr = None
                        for a in range(4):
                            rr = v.tensor_scalar(probsb[:, a, ko:1024],
                                                 probs4[:, a, ko:1024],
                                                 den4[:, a:a + 1], None, ALU.mult)
                        return rr
                    em.op("vector", sfm2)
                    em.bar()
                    for tqc in range(4):
                        pp = p_tr if tqc % 2 == 0 else p_tr2
                        def trp(t, tqc=tqc, pp=pp, ko=ko, ntk=ntk):
                            rr = None
                            for tkc in range(ntk):
                                rr = t.transpose(
                                    pp[:, tkc * 128:(tkc + 1) * 128],
                                    probsb[:, tqc, ko + tkc * 128:ko + (tkc + 1) * 128],
                                    iden[:])
                            return rr
                        em.op("tensor", trp)
                        em.bar()
                        def evp(v, tqc=tqc, pp=pp, ntk=ntk):
                            return v.tensor_copy(
                                probsT[:, 0:ntk, tqc * 128:(tqc + 1) * 128],
                                pp[:, 0:ntk * 128].rearrange("p (a m) -> p a m", m=128))
                        em.op("vector", evp)
                        em.bar()
                    def mmav(t, h=h, w=w, ntk=ntk):
                        rr = None
                        for tkc in range(ntk):
                            vtile = vts[:, (tkc if w > 0 else 4 + tkc), h * 64:(h + 1) * 64]
                            rr = t.matmul(p_av[:], vtile, probsT[:, tkc, :],
                                          start=(tkc == 0), stop=(tkc == ntk - 1))
                        return rr
                    em.op("tensor", mmav)
                    em.bar()
                    em.op("vector", lambda v, hb=hb, fc=fc: v.tensor_copy(
                        oTs[hb:hb + 64, fc, :], p_av[:]))
                    em.bar()
                for i in range(4):
                    em.dma("sync", lambda s_, w=w, i=i: s_.dma_start(
                        out=xt_c, in_=xc_t[:, w * 4 + i, :]))
                    def mmo(t, i=i):
                        for kc in range(4):
                            t.matmul(p_a[:, 0, :], oTs[:, kc, i * 128:(i + 1) * 128],
                                     wo_s[:, kc, 0:512], start=(kc == 0), stop=(kc == 3))
                        rr = None
                        for kc in range(4):
                            rr = t.matmul(p_b[:, 0, :], oTs[:, kc, i * 128:(i + 1) * 128],
                                          wo_s[:, kc, 512:1024], start=(kc == 0), stop=(kc == 3))
                        return rr
                    em.op("tensor", mmo)
                    em.bar()
                    def evo(v):
                        v.tensor_add(xt_c[:, 0:512], p_a[:, 0, :], xt_c[:, 0:512])
                        return v.tensor_add(xt_c[:, 512:1024], p_b[:, 0, :], xt_c[:, 512:1024])
                    em.op("vector", evo)
                    em.bar()
                    em.dma("sync", lambda s_, w=w, i=i: s_.dma_start(
                        out=xc_t[:, w * 4 + i, :], in_=xt_c))
                    em.bar()
            em.dma("gpsimd", lambda g_, W=W: g_.dma_start(out=w1x_s, in_=W["w1a"]))
            em.dma("sync", lambda s_, W=W: s_.dma_start(
                out=b1ab, in_=W["b1a"].broadcast_to([128, 2730])))
            em.dma("sync", lambda s_, W=W: s_.dma_start(
                out=lngb[:], in_=W["lng"].broadcast_to([128, D])))
            em.dma("sync", lambda s_, W=W: s_.dma_start(
                out=lnbb[:], in_=W["lnb"].broadcast_to([128, D])))
            em.bar()
            for phase in (1, 2):
                if phase == 2:
                    em.dma("gpsimd", lambda g_, W=W: g_.dma_start(out=w1x_s, in_=W["w1g"]))
                    em.dma("gpsimd", lambda g_, W=W: g_.dma_start(out=w2p_s, in_=W["w2p"]))
                    em.dma("sync", lambda s_, W=W: s_.dma_start(
                        out=b1ab, in_=W["b1g"].broadcast_to([128, 2730])))
                    em.dma("sync", lambda s_, W=W: s_.dma_start(
                        out=b2b[:], in_=W["b2"].broadcast_to([128, D])))
                    em.op("vector", lambda v: v.memset(actb[:, 2730:2816], 0.0))
                    em.bar()
                for t_i in range(NT):
                    if phase == 1:
                        em.dma("sync", lambda s_, t_i=t_i: s_.dma_start(
                            out=hx[:, 0:512], in_=xc_t[:, t_i, 0:512]))
                        if t_i == 0:
                            em.op("vector", lambda v: v.memset(hx[0:1, 512:1024], 0.0))
                            em.dma("sync", lambda s_: s_.dma_start(
                                out=hx[1:128, 512:1024], in_=xcur[0:127, 512:1024]))
                        else:
                            em.dma("sync", lambda s_, t_i=t_i: s_.dma_start(
                                out=hx[:, 512:1024],
                                in_=xcur[t_i * 128 - 1:t_i * 128 + 127, 512:1024]))
                        em.bar()
                        emit_ln_group(em, [hx], [hb_c], [arF[:, 6484:7508]],
                                      sq[:], cols, D, 1,
                                      gain=lngb[:], bias=lnbb[:])
                        emit_transpose_tiles(em, [hb_c], lambda i, kc, t_i=t_i:
                                             hT_c1[:, kc, :], iden[:], p_tr[:], 1)
                        em.dma("sync", lambda s_, t_i=t_i: s_.dma_start(
                            out=hT2.rearrange("k p t -> p k t")[:, :, t_i * 128:(t_i + 1) * 128],
                            in_=hT_c1[:, :, :]))
                        hTv = hT_c1
                    else:
                        em.dma("sync", lambda s_, t_i=t_i: s_.dma_start(
                            out=hT_c2[:, :, :],
                            in_=hT2.rearrange("k p t -> p k t")[:, :, t_i * 128:(t_i + 1) * 128]))
                        em.dma("sync", lambda s_, t_i=t_i: s_.dma_start(
                            out=h3a_s, in_=h3a_t2[:, t_i, :]))
                        em.dma("sync", lambda s_, t_i=t_i: s_.dma_start(
                            out=xt_c, in_=xc_t[:, t_i, :]))
                        em.bar()
                        hTv = hT_c2
                    dstf = h3a_s if phase == 1 else h3g_s
                    for cpair in range(3):
                        def mmf(t, cpair=cpair, hTv=hTv):
                            rr = None
                            for j in range(2):
                                nb = cpair * 2 + j
                                nw = 512 if nb < 5 else 170
                                no = nb * 512
                                pp = (p_a if j == 0 else p_b)[:, 0, 0:nw]
                                for kc in range(8):
                                    rr = t.matmul(pp, hTv[:, kc, :],
                                                  w1x_s[:, kc, no:no + nw],
                                                  start=(kc == 0), stop=(kc == 7))
                            return rr
                        em.op("tensor", mmf)
                        em.bar()
                        def evf(v, cpair=cpair, dstf=dstf):
                            rr = None
                            for j in range(2):
                                nb = cpair * 2 + j
                                nw = 512 if nb < 5 else 170
                                no = nb * 512
                                pp = (p_a if j == 0 else p_b)[:, 0, 0:nw]
                                rr = v.tensor_add(dstf[:, no:no + nw], pp,
                                                  b1ab[:, no:no + nw])
                            return rr
                        em.op("vector", evf)
                        em.bar()
                    if phase == 1:
                        em.dma("sync", lambda s_, t_i=t_i: s_.dma_start(
                            out=h3a_t2[:, t_i, :], in_=h3a_s))
                        em.bar()
                        continue
                    em.op("vector", lambda v: v.tensor_mul(h3a_s, h3a_s, h3g_s))
                    em.bar()
                    em.op("scalar", lambda sc: sc.activation(
                        h3g_s, h3g_s, AF.Erf, scale=0.7071067811865476))
                    em.bar()
                    em.op("vector", lambda v: v.tensor_mul(h3g_s, h3a_s, h3g_s))
                    em.bar()
                    em.op("vector", lambda v: v.tensor_add(h3g_s, h3g_s, h3a_s))
                    em.bar()
                    em.op("vector", lambda v: v.tensor_scalar(
                        actb[:, 0:2730], h3g_s, 0.5, None, ALU.mult))
                    em.bar()
                    for bb, pp in ((0, p_tr), (1, p_tr2), (2, p_tr)):
                        nch = 8 if bb < 2 else 6
                        def trf(t, bb=bb, pp=pp, nch=nch):
                            rr = None
                            for c in range(nch):
                                kc = bb * 8 + c
                                rr = t.transpose(pp[:, c * 128:(c + 1) * 128],
                                                 actb[:, kc * 128:(kc + 1) * 128],
                                                 iden[:])
                            return rr
                        em.op("tensor", trf)
                        em.bar()
                        def evt(v, bb=bb, pp=pp, nch=nch):
                            return v.tensor_copy(
                                actT[:, bb * 8:bb * 8 + nch, :],
                                pp[:, 0:nch * 128].rearrange("p (a m) -> p a m", m=128))
                        em.op("vector", evt)
                        em.bar()
                    def mmy(t):
                        for kc in range(22):
                            t.matmul(p_a[:, 0, :], actT[:, kc, :], w2p_s[:, kc, 0:512],
                                     start=(kc == 0), stop=(kc == 21))
                        rr = None
                        for kc in range(22):
                            rr = t.matmul(p_b[:, 0, :], actT[:, kc, :], w2p_s[:, kc, 512:1024],
                                          start=(kc == 0), stop=(kc == 21))
                        return rr
                    em.op("tensor", mmy)
                    em.bar()
                    def evy(v):
                        v.tensor_add(rsc[:, 0, :], p_a[:, 0, :], b2b[:, 0:512])
                        return v.tensor_add(rsc[:, 1, :], p_b[:, 0, :], b2b[:, 512:1024])
                    em.op("vector", evy)
                    em.bar()
                    em.op("vector", lambda v: v.tensor_add(
                        xt_c, xt_c, rsc[:].rearrange("p a d -> p (a d)")))
                    em.bar()
                    em.dma("sync", lambda s_, t_i=t_i: s_.dma_start(
                        out=xc_t[:, t_i, :], in_=xt_c))
                    em.bar()

        for g in range(NG):
            t0 = g * G
            for i in range(G):
                em.dma("sync", lambda s_, i=i, t0=t0: s_.dma_start(
                    out=xg[:, i, :], in_=xc_t[:, t0 + i, :]))
                em.dma("sync", lambda s_, i=i, t0=t0: s_.dma_start(
                    out=qkvf[:, i, 512:1536], in_=x_t[:, t0 + i, :]))
            em.bar()
            emit_ln_group(em, [xg[:, i, :] for i in range(G)],
                          [xg[:, i, :] for i in range(G)],
                          [xg[:, i, :] for i in range(G)], sq[:], cols, D, G)
            em.op("vector", lambda v: v.tensor_add(
                xg[:, :, :], xg[:, :, :], qkvf[:, 0:G, 512:1536]))
            em.bar()
            for i in range(G):
                em.dma("sync", lambda s_, i=i, t0=t0: s_.dma_start(
                    out=y_t[:, t0 + i, :], in_=xg[:, i, :]))
            em.bar()



def run_l0_ffn(x_full, wqkv, q_scale, k_scale, wo, ln_g, ln_b, w1, b1, w2, b2):
    if "l0f" not in _CACHE:
        _CACHE["l0f"] = build_l0_ffn()
    nc = _CACHE["l0f"]
    consts = {"iden": _ident_bf16(), "mask": _l0ffn_mask(), **_l0ffn_xpos()}
    for l in range(2):
        w2p = np.zeros((2816, 1024), np.float32)
        w2p[0:2730] = np.asarray(w2[l])
        consts.update({
            f"wqkv{l}": _wtile(np.asarray(wqkv[l])),
            f"wo{l}": _wtile(np.asarray(wo[l])),
            f"w1a{l}": _wtile(np.ascontiguousarray(np.asarray(w1[l])[:, 0:2730])),
            f"w1g{l}": _wtile(np.ascontiguousarray(np.asarray(w1[l])[:, 2730:])),
            f"w2p{l}": _wtile(w2p),
            f"qs{l}": np.asarray(q_scale[l]).reshape(1, 64),
            f"ks{l}": np.asarray(k_scale[l]).reshape(1, 64),
            f"lng{l}": np.asarray(ln_g[l]).reshape(1, D),
            f"lnb{l}": np.asarray(ln_b[l]).reshape(1, D),
            f"b1a{l}": np.ascontiguousarray(np.asarray(b1[l])[0:2730]).reshape(1, 2730),
            f"b1g{l}": np.ascontiguousarray(np.asarray(b1[l])[2730:]).reshape(1, 2730),
            f"b2{l}": np.asarray(b2[l]).reshape(1, D),
        })
    in_maps = [{"x": np.ascontiguousarray(x_full[c]), **consts}
               for c in range(NCORES)]
    res = run_bass_kernel_spmd(nc, in_maps, core_ids=list(range(NCORES)))
    return np.stack([np.asarray(r["y"]) for r in res.results])



def _l0attn_consts():
    ut = np.triu(np.ones((128, 128), np.float32))
    trilS = np.tril(np.ones((128, 128), np.float32), -1)
    triuS = np.triu(np.ones((128, 128), np.float32), 1)
    triuI = np.triu(np.ones((128, 128), np.float32))
    idf = np.eye(128, dtype=np.float32)
    ones1 = np.ones((1, 128), np.float32)
    ones128 = np.ones((128, 128), np.float32)
    return {"utones": ut, "trilS": trilS, "triuS": triuS, "triuI": triuI,
            "idf32": idf, "ones1": ones1, "ones128": ones128}

def emit_l0_attn(nc, em, pfx, x_in, y_out, dd):
    SB = lambda n, s, d: nc.sbuf_tensor(pfx + n, s, d)
    PS = lambda n, s, d: nc.psum_tensor(pfx + n, s, d)
    DR = lambda n, s, d: nc.dram_tensor(pfx + n, s, d).ap()
    Win_d = dd["Win"]; Wctrl_d = dd["Wctrl"]; Wout_d = dd["Wout"]
    bctrl_d = dd["bctrl"]; bout_d = dd["bout"]; ramp_d = dd["ramp"]
    wconv_d = dd["wconv"]; iden_d = dd["iden"]
    cst_d = {k: dd[k] for k in _l0attn_consts()}
    qn2 = DR("qn2", [S, D], BF16)
    kn2 = DR("kn2", [S, D], BF16)
    vn2 = DR("vn2", [S, D], BF16)
    og2 = DR("og2", [S, D], F32)
    rd2 = DR("rd2", [S, D], F32)

    x_t = x_in.rearrange("(t p) d -> p t d", p=PT)
    y_t = y_out.rearrange("(t p) d -> p t d", p=PT)
    qn_t = qn2.rearrange("(t p) d -> p t d", p=PT)
    kn_t = kn2.rearrange("(t p) d -> p t d", p=PT)
    vn_t = vn2.rearrange("(t p) d -> p t d", p=PT)
    og_t = og2.rearrange("(t p) d -> p t d", p=PT)
    rd_t = rd2.rearrange("(t p) d -> p t d", p=PT)

    G = 4
    NG = NT // G

    with ExitStack() as ctx:
        E = ctx.enter_context
        iden = E(SB("iden_s", [128, 128], BF16))
        cst = {k: E(SB(f"{k}_s", list(v.shape), F32))
               for k, v in _l0attn_consts().items()}
        eps_c = E(SB("eps_c", [128, 1], F32))
        eps6 = E(SB("eps6", [128, 1], F32))
        sq = E(SB("sq", [128, D], F32))
        cols = {"s": E(SB("ln_s", [128, G], F32)),
                "ss": E(SB("ln_ss", [128, G], F32)),
                "r": E(SB("ln_r", [128, G], F32)),
                "eps": eps_c}
        Wbig = E(SB("Wbig", [128, 8, 1024], BF16))
        Wctrl_s = E(SB("Wctrl_s", [128, 8, 1032], BF16))
        wcv = E(SB("wcv", [128, 8, 12], F32))
        bctrl_b = E(SB("bctrl_b", [128, 1032], F32))
        bout_b = E(SB("bout_b", [128, D], F32))
        rampb = E(SB("rampb", [128, D], F32))
        stats = E(SB("stats", [128, NT, 12], F32))
        ctf = E(SB("ctf", [128, 1032], F32))
        ss4 = E(SB("ss4", [128, 4], F32))
        r4n = E(SB("r4n", [128, 4], F32))
        bnd = E(SB("bnd", [128, 3, 8, 3], F32))
        lgc = E(SB("lgc", [128, 4], F32))
        g4 = E(SB("g4", [128, 4], F32))
        rc4 = E(SB("rc4", [128, 4], F32))
        gC4 = E(SB("gC4", [128, 4], F32))
        lgT = E(SB("lgT", [4, 128], F32))
        nlgT = E(SB("nlgT", [4, 128], F32))
        S4 = E(SB("S4", [128, 4, 2, 256], F32))
        S4b = E(SB("S4b", [128, 4, 2, 256], BF16))
        AF1 = E(SB("AF1", [128, 4096], F32))
        AF2 = E(SB("AF2", [128, 4096], F32))
        AF3 = E(SB("AF3", [128, 4120], F32))
        AF4 = E(SB("AF4", [128, 4096], F32))
        AB1 = E(SB("AB1", [128, 4096], BF16))
        AB2 = E(SB("AB2", [128, 4096], BF16))
        AB3 = E(SB("AB3", [128, 4096], BF16))
        AB4 = E(SB("AB4", [128, 6144], BF16))
        xg = AF1[:].rearrange("p (i d) -> p i d", d=D)
        xcg = AF2[:].rearrange("p (i d) -> p i d", d=D)
        fmi = AF3[:, 0:4120].rearrange("p (a b) -> p a b", b=515)
        fmo = AF4[:].rearrange("p (a b) -> p a b", b=512)
        xlnb = AB1[:].rearrange("p (i d) -> p i d", d=D)
        xT = AB2[:].rearrange("p (k m) -> p k m", m=512)
        fmb = AB3[:].rearrange("p (a b) -> p a b", b=512)
        tkb = AB4[:, 0:4096].rearrange("p (i d) -> p i d", d=D)
        tkf = AF2[:].rearrange("p (i d) -> p i d", d=D)
        X4 = AF1[:, 0:512].rearrange("p (a b) -> p a b", b=128)
        X2_4 = AF1[:, 512:1024].rearrange("p (a b) -> p a b", b=128)
        r4s = AF1[:, 1024:1536].rearrange("p (a b) -> p a b", b=128)
        rT4 = AF1[:, 1536:2048].rearrange("p (a b) -> p a b", b=128)
        KK4 = AF1[:, 2048:2560].rearrange("p (a b) -> p a b", b=128)
        KQ4 = AF1[:, 2560:3072].rearrange("p (a b) -> p a b", b=128)
        scr4 = AF1[:, 3072:3584].rearrange("p (a b) -> p a b", b=128)
        scrM = AF1[:, 3584:4096]
        Ub4 = AF2[:, 0:1024].rearrange("p (a b) -> p a b", b=256)
        O14 = AF2[:, 1024:2048].rearrange("p (a b) -> p a b", b=256)
        rdt = AF2[:, 2048:3072]
        Qt_c = AB1[:, 0:1024]
        Kt_c = AB1[:, 1024:2048]
        Vt_c = AB1[:, 2048:3072]
        Kg_c = AB1[:, 3072:4096]
        Ks_c = AB2[:, 0:1024]
        KtT = AB2[:, 1024:2048].rearrange("p (a b m) -> p a b m", b=2, m=128)
        QtT = AB2[:, 2048:3072].rearrange("p (a b m) -> p a b m", b=2, m=128)
        Wt4 = AB2[:, 3072:4096].rearrange("p (a b m) -> p a b m", b=2, m=128)
        Ns4 = AB3[:, 0:512].rearrange("p (a b) -> p a b", b=128)
        Nt4 = AB3[:, 512:1024].rearrange("p (a b) -> p a b", b=128)
        Ms4 = AB3[:, 1024:1536].rearrange("p (a b) -> p a b", b=128)
        Mt4 = AB3[:, 1536:2048].rearrange("p (a b) -> p a b", b=128)
        Ps4 = AB3[:, 2048:2560].rearrange("p (a b) -> p a b", b=128)
        PtP4 = AB3[:, 2560:3072].rearrange("p (a b) -> p a b", b=128)
        At4 = AB3[:, 3072:3584].rearrange("p (a b) -> p a b", b=128)
        Pb4 = AB3[:, 3584:4096].rearrange("p (a b) -> p a b", b=128)
        dU4 = AB4[:, 4096:5120].rearrange("p (a b) -> p a b", b=256)
        ogt = AF3[:, 0:4096].rearrange("p (i d) -> p i d", d=D)
        rdn = AB1[:].rearrange("p (i d) -> p i d", d=D)
        rdT = AB2[:].rearrange("p (k m) -> p k m", m=512)
        p1 = E(PS("p1", [128, 512], F32))
        p2 = E(PS("p2", [128, 512], F32))
        p3 = E(PS("p3", [128, 1024], F32))
        p4 = E(PS("p4", [128, 1024], F32))
        ptr = E(PS("ptr", [128, 1024], BF16))
        pcs = E(PS("pcs", [128, 16], F32))


        em.dma("sync", lambda s_: s_.dma_start(out=iden[:], in_=iden_d))
        for k in cst:
            em.dma("sync", lambda s_, k=k: s_.dma_start(out=cst[k][:], in_=cst_d[k]))
        em.dma("gpsimd", lambda g_: g_.dma_start(out=Wctrl_s[:], in_=Wctrl_d))
        em.dma("sync", lambda s_: s_.dma_start(out=wcv[:], in_=wconv_d))
        em.dma("sync", lambda s_: s_.dma_start(
            out=bctrl_b[:], in_=bctrl_d.broadcast_to([128, 1032])))
        em.op("vector", lambda v: v.memset(eps_c[:], 1e-5))
        em.op("scalar", lambda sc: sc.memzero(eps6[:]))
        em.bar()
        em.op("vector", lambda v: v.memset(eps6[:], 1e-6))
        em.bar()

        for g in range(NG):
            t0 = g * G
            for i in range(G):
                em.dma("sync", lambda s_, i=i, t0=t0: s_.dma_start(
                    out=xg[:, i, :], in_=x_t[:, t0 + i, :]))
            em.bar()
            emit_ln_group(em, [xg[:, i, :] for i in range(G)],
                          [xlnb[:, i, :] for i in range(G)],
                          [xcg[:, i, :] for i in range(G)], sq[:], cols, D, G)
            emit_transpose_tiles(
                em, [xlnb[:, i, :] for i in range(G)],
                lambda i, kc: xT[:, kc, i * 128:(i + 1) * 128], iden[:],
                ptr[:], G)
            for i in range(G):
                def mmc(t, i=i):
                    r = None
                    for no, nw, pp in ((0, 512, p1[:, 0:512]), (512, 512, p2[:, 0:512]),
                                       (1024, 8, pcs[:, 0:8])):
                        for kc in range(8):
                            r = t.matmul(pp, xT[:, kc, i * 128:(i + 1) * 128],
                                         Wctrl_s[:, kc, no:no + nw],
                                         start=(kc == 0), stop=(kc == 7))
                    return r
                em.op("tensor", mmc)
                em.bar()
                def evc(v):
                    v.tensor_add(ctf[:, 0:512], p1[:], bctrl_b[:, 0:512])
                    v.tensor_add(ctf[:, 512:1024], p2[:], bctrl_b[:, 512:1024])
                    return v.tensor_add(ctf[:, 1024:1032], pcs[:, 0:8],
                                        bctrl_b[:, 1024:1032])
                em.op("vector", evc)
                em.bar()
                em.op("scalar", lambda sc: sc.activation(
                    ctf[:, 0:8], ctf[:, 0:8], AF.Sigmoid))
                em.bar()
                def sg2(v, i=i, t0=t0):
                    v.tensor_scalar(ctf[:, 0:4], ctf[:, 0:4], 0.98, 0.01,
                                    ALU.mult, ALU.add)
                    v.tensor_copy(stats[:, t0 + i, 4:8], ctf[:, 4:8])
                    return v.tensor_scalar(stats[:, t0 + i, 8:12], ctf[:, 4:8],
                                           -1.0, None, ALU.mult)
                em.op("vector", sg2)
                em.bar()
                em.op("scalar", lambda sc, i=i, t0=t0: sc.activation(
                    stats[:, t0 + i, 0:4], ctf[:, 0:4], AF.Ln))
                em.bar()
                em.op("scalar", lambda sc: sc.activation(
                    ctf[:, 8:1032], ctf[:, 8:1032], AF.Sigmoid))
                em.bar()
                em.dma("sync", lambda s_, i=i, t0=t0: s_.dma_start(
                    out=og_t[:, t0 + i, :], in_=ctf[:, 8:1032]))
            em.bar()
            for ti, scr_t in ((0, qn_t), (1, kn_t), (2, vn_t)):
                em.dma("gpsimd", lambda g_, ti=ti: g_.dma_start(
                    out=Wbig[:], in_=Win_d[:, :, ti * 1024:(ti + 1) * 1024]))
                if g == 0:
                    em.op("vector", lambda v: v.memset(fmi[:, :, 0:3], 0.0))
                else:
                    em.op("vector", lambda v, ti=ti: v.tensor_copy(
                        fmi[:, :, 0:3], bnd[:, ti, :, :]))
                em.bar()
                for fc in range(8):
                    pp = p3 if fc % 2 == 0 else p4
                    def mmf(t, fc=fc, pp=pp):
                        r = None
                        for kc in range(8):
                            r = t.matmul(pp[:, 0:512],
                                         Wbig[:, kc, fc * 128:(fc + 1) * 128],
                                         xT[:, kc, :], start=(kc == 0), stop=(kc == 7))
                        return r
                    em.op("tensor", mmf)
                    em.bar()
                    em.op("vector", lambda v, fc=fc, pp=pp: v.tensor_copy(
                        fmi[:, fc, 3:515], pp[:, 0:512]))
                    em.bar()
                def cva(v, ti=ti):
                    r = None
                    for kc in range(8):
                        r = v.tensor_scalar(fmo[:, kc, :], fmi[:, kc, 0:512],
                                            wcv[:, kc, ti * 4:ti * 4 + 1], None, ALU.mult)
                    return r
                em.op("vector", cva)
                em.bar()
                for j in range(1, 4):
                    def cvb(v, j=j, ti=ti):
                        r = None
                        for kc in range(8):
                            r = v.scalar_tensor_tensor(
                                fmo[:, kc, :], fmi[:, kc, j:j + 512],
                                wcv[:, kc, ti * 4 + j:ti * 4 + j + 1],
                                fmo[:, kc, :], ALU.mult, ALU.add)
                        return r
                    em.op("vector", cvb)
                    em.bar()
                em.op("vector", lambda v, ti=ti: v.tensor_copy(
                    bnd[:, ti, :, :], fmi[:, :, 512:515]))
                em.op("scalar", lambda sc: sc.activation(
                    AB3[:, 0:4096], AF4[:, 0:4096], AF.Silu))
                em.bar()
                for i in range(G):
                    def trb(t, i=i):
                        r = None
                        for fc in range(8):
                            r = t.transpose(ptr[:, fc * 128:(fc + 1) * 128],
                                            fmb[:, fc, i * 128:(i + 1) * 128], iden[:])
                        return r
                    em.op("tensor", trb)
                    em.bar()
                    em.op("vector", lambda v, i=i: v.tensor_copy(tkf[:, i, :], ptr[:]))
                    em.bar()
                for i in range(G):
                    if ti < 2:
                        em.op("scalar", lambda sc, i=i: sc.activation(
                            sq[:], tkf[:, i, :], AF.Square))
                        em.bar()
                        em.op("vector", lambda v: v.tensor_reduce(
                            ss4[:], sq[:].rearrange("p (h d) -> p h d", d=256),
                            AX.X, ALU.add))
                        em.bar()
                        em.op("scalar", lambda sc: sc.activation(
                            r4n[:], ss4[:], AF.Sqrt))
                        em.bar()
                        em.op("vector", lambda v: v.tensor_scalar_max(
                            r4n[:], r4n[:], 1e-6))
                        em.bar()
                        em.op("vector", lambda v: v.reciprocal(r4n[:], r4n[:]))
                        em.bar()
                    def apn(v, i=i, ti=ti):
                        r = None
                        for h in range(4):
                            r = v.tensor_scalar(tkb[:, i, h * 256:(h + 1) * 256],
                                                tkf[:, i, h * 256:(h + 1) * 256],
                                                r4n[:, h:h + 1] if ti < 2 else 1.0,
                                                None, ALU.mult)
                        return r
                    em.op("vector", apn)
                    em.bar()
                    em.dma("sync", lambda s_, i=i, t0=t0, scr_t=scr_t: s_.dma_start(
                        out=scr_t[:, t0 + i, :], in_=tkb[:, i, :]))
                em.bar()

        em.op("vector", lambda v: v.memset(S4[:].rearrange("p a b c -> p (a b c)"), 0.0))
        em.bar()
        for c in range(NT):
            em.dma("sync", lambda s_, c=c: s_.dma_start(out=Qt_c, in_=qn_t[:, c, :]))
            em.dma("sync", lambda s_, c=c: s_.dma_start(out=Kt_c, in_=kn_t[:, c, :]))
            em.dma("sync", lambda s_, c=c: s_.dma_start(out=Vt_c, in_=vn_t[:, c, :]))
            em.op("tensor", lambda t, c=c: t.matmul(
                pcs[:, 0:4], cst["utones"][:], stats[:, c, 0:4], start=True, stop=True))
            em.bar()
            em.op("vector", lambda v: v.tensor_copy(lgc[:], pcs[:, 0:4]))
            em.bar()
            em.op("scalar", lambda sc: sc.activation(g4[:], lgc[:], AF.Exp))
            def bxr(v, c=c):
                r = None
                for h in range(4):
                    r = v.tensor_scalar(scr4[:, h, :], cst["triuI"][:],
                                        stats[:, c, h:h + 1], None, ALU.mult)
                return r
            em.op("vector", bxr)
            em.bar()
            def mmx(t):
                r = None
                for h in range(4):
                    r = t.matmul(p3[:, h * 128:(h + 1) * 128], cst["ones128"][:],
                                 scr4[:, h, :], start=True, stop=True)
                return r
            em.op("tensor", mmx)
            em.bar()
            em.op("vector", lambda v: v.tensor_copy(
                X4[:].rearrange("p a b -> p (a b)"), p3[:, 0:512]))
            em.bar()
            def bx2(v):
                r = None
                for h in range(4):
                    v.tensor_scalar(X2_4[:, h, :], X4[:, h, :], lgc[:, h:h + 1],
                                    0.0, ALU.subtract, ALU.min)
                    r = v.tensor_scalar(scr4[:, h, :], X4[:, h, :], lgc[:, h:h + 1],
                                        0.0, ALU.subtract, ALU.max)
                return r
            em.op("vector", bx2)
            em.bar()
            def evx2(sc):
                sc.activation(rT4[:].rearrange("p a b -> p (a b)"),
                              X2_4[:].rearrange("p a b -> p (a b)"), AF.Exp)
                return sc.activation(r4s[:].rearrange("p a b -> p (a b)"),
                                     scr4[:].rearrange("p a b -> p (a b)"),
                                     AF.Exp, scale=-1.0)
            em.op("scalar", evx2)
            em.bar()
            def rcg(v):
                r = None
                for h in range(4):
                    r = v.tensor_copy(rc4[:, h:h + 1], rT4[:, h, 127:128])
                return r
            em.op("vector", rcg)
            em.bar()
            em.op("vector", lambda v: v.tensor_mul(gC4[:], g4[:], rc4[:]))
            em.bar()
            def kgs(v):
                r = None
                for h in range(4):
                    v.tensor_scalar(Kg_c[:, h * 256:(h + 1) * 256],
                                    Kt_c[:, h * 256:(h + 1) * 256],
                                    g4[:, h:h + 1], None, ALU.mult)
                    r = v.tensor_scalar(Ks_c[:, h * 256:(h + 1) * 256],
                                        Kt_c[:, h * 256:(h + 1) * 256],
                                        rc4[:, h:h + 1], None, ALU.mult)
                return r
            em.op("vector", kgs)
            em.bar()
            def trk(t):
                r = None
                for z in range(8):
                    h, kc2 = divmod(z, 2)
                    r = t.transpose(ptr[:, z * 128:(z + 1) * 128],
                                    Kt_c[h * 0:128, h * 256 + kc2 * 128:h * 256 + (kc2 + 1) * 128]
                                    if False else Kt_c[:, h * 256 + kc2 * 128:h * 256 + (kc2 + 1) * 128].rearrange("p m -> p m"),
                                    iden[:])
                return r
            def trk2(t):
                r = None
                for z in range(8):
                    h, kc2 = divmod(z, 2)
                    r = t.transpose(
                        ptr[:, z * 128:(z + 1) * 128],
                        AB1[:, 1024 + h * 256 + kc2 * 128:1024 + h * 256 + (kc2 + 1) * 128],
                        iden[:])
                return r
            em.op("tensor", trk2)
            em.bar()
            em.op("vector", lambda v: v.tensor_copy(AB2[:, 1024:2048], ptr[:]))
            em.bar()
            def trq2(t):
                r = None
                for z in range(8):
                    r = t.transpose(ptr[:, z * 128:(z + 1) * 128],
                                    AB1[:, z * 128:(z + 1) * 128], iden[:])
                return r
            em.op("tensor", trq2)
            em.bar()
            em.op("vector", lambda v: v.tensor_copy(AB2[:, 2048:3072], ptr[:]))
            em.bar()
            def mmg(t):
                r = None
                for h in range(4):
                    for kc2 in range(2):
                        t.matmul(p1[:, h * 128:(h + 1) * 128], KtT[:, h, kc2, :],
                                 KtT[:, h, kc2, :], start=(kc2 == 0), stop=(kc2 == 1))
                    for kc2 in range(2):
                        r = t.matmul(p2[:, h * 128:(h + 1) * 128], KtT[:, h, kc2, :],
                                     QtT[:, h, kc2, :], start=(kc2 == 0), stop=(kc2 == 1))
                return r
            em.op("tensor", mmg)
            em.bar()
            def evg(v):
                v.tensor_copy(KK4[:].rearrange("p a b -> p (a b)"), p1[:])
                return v.tensor_copy(KQ4[:].rearrange("p a b -> p (a b)"), p2[:])
            em.op("vector", evg)
            em.bar()
            def bbr(v, c=c):
                r = None
                for h in range(4):
                    r = v.tensor_scalar(scr4[:, h, :], cst["idf32"][:],
                                        stats[:, c, 4 + h:5 + h], None, ALU.mult)
                return r
            em.op("vector", bbr)
            em.bar()
            def mmbr(t):
                r = None
                for h in range(4):
                    r = t.matmul(p3[:, h * 128:(h + 1) * 128], cst["ones128"][:],
                                 scr4[:, h, :], start=True, stop=True)
                return r
            em.op("tensor", mmbr)
            em.bar()
            def ns1(v, c=c):
                v.tensor_mul(scr4[:].rearrange("p a b -> p (a b)"),
                             KK4[:].rearrange("p a b -> p (a b)"),
                             r4s[:].rearrange("p a b -> p (a b)"))
                r = None
                for h in range(4):
                    r = v.tensor_scalar(scr4[:, h, :], scr4[:, h, :],
                                        stats[:, c, 8 + h:9 + h], None, ALU.mult)
                return r
            em.op("vector", ns1)
            em.bar()
            def ns2(v):
                r = None
                for h in range(4):
                    r = v.tensor_mul(Ns4[:, h, :], scr4[:, h, :], cst["trilS"][:])
                return r
            em.op("vector", ns2)
            em.bar()
            def nt1(v):
                v.tensor_mul(scr4[:].rearrange("p a b -> p (a b)"),
                             KK4[:].rearrange("p a b -> p (a b)"),
                             rT4[:].rearrange("p a b -> p (a b)"))
                return v.tensor_mul(scr4[:].rearrange("p a b -> p (a b)"),
                                    scr4[:].rearrange("p a b -> p (a b)"),
                                    p3[:, 0:512])
            em.op("vector", nt1)
            em.bar()
            def nt2(v):
                r = None
                for h in range(4):
                    v.tensor_scalar(scr4[:, h, :], scr4[:, h, :], -1.0, None, ALU.mult)
                    r = v.tensor_mul(Nt4[:, h, :], scr4[:, h, :], cst["triuS"][:])
                return r
            em.op("vector", nt2)
            em.bar()
            def m0b(v):
                r = None
                for h in range(4):
                    v.tensor_add(Ms4[:, h, :], Ns4[:, h, :], iden[:])
                    r = v.tensor_add(Mt4[:, h, :], Nt4[:, h, :], iden[:])
                return r
            em.op("vector", m0b)
            em.bar()
            def p0c(v):
                v.tensor_copy(Ps4[:].rearrange("p a b -> p (a b)"),
                              Ns4[:].rearrange("p a b -> p (a b)"))
                return v.tensor_copy(PtP4[:].rearrange("p a b -> p (a b)"),
                                     Nt4[:].rearrange("p a b -> p (a b)"))
            em.op("vector", p0c)
            em.bar()
            for lvl in range(6):
                def mmp(t):
                    r = None
                    for h in range(4):
                        t.matmul(p1[:, h * 128:(h + 1) * 128], PtP4[:, h, :],
                                 Ps4[:, h, :], start=True, stop=True)
                        r = t.matmul(p2[:, h * 128:(h + 1) * 128], Ps4[:, h, :],
                                     PtP4[:, h, :], start=True, stop=True)
                    return r
                em.op("tensor", mmp)
                em.bar()
                def evp2(v):
                    v.tensor_copy(Ps4[:].rearrange("p a b -> p (a b)"), p1[:])
                    return v.tensor_copy(PtP4[:].rearrange("p a b -> p (a b)"), p2[:])
                em.op("vector", evp2)
                em.bar()
                def mmm(t):
                    r = None
                    for h in range(4):
                        t.matmul(p3[:, h * 128:(h + 1) * 128], Mt4[:, h, :],
                                 Ps4[:, h, :], start=True, stop=True)
                        r = t.matmul(p4[:, h * 128:(h + 1) * 128], Ps4[:, h, :],
                                     Mt4[:, h, :], start=True, stop=True)
                    return r
                em.op("tensor", mmm)
                em.bar()
                def evm(v):
                    v.tensor_copy(scrM, AB3[:, 1024:1536])
                    v.tensor_add(AB3[:, 1024:1536], scrM, p3[:, 0:512])
                    v.tensor_copy(scrM, AB3[:, 1536:2048])
                    return v.tensor_add(AB3[:, 1536:2048], scrM, p4[:, 0:512])
                em.op("vector", evm)
                em.bar()
            def at1(v, c=c):
                r = None
                for h in range(4):
                    r = v.tensor_scalar(At4[:, h, :], Mt4[:, h, :],
                                        stats[:, c, 4 + h:5 + h], None, ALU.mult)
                return r
            em.op("vector", at1)
            em.bar()
            def pb1(v):
                v.tensor_mul(scr4[:].rearrange("p a b -> p (a b)"),
                             KQ4[:].rearrange("p a b -> p (a b)"),
                             rT4[:].rearrange("p a b -> p (a b)"))
                r = None
                for h in range(4):
                    r = v.tensor_mul(Pb4[:, h, :], scr4[:, h, :], cst["triuI"][:])
                return r
            em.op("vector", pb1)
            em.op("scalar", lambda sc: sc.copy(
                S4b[:].rearrange("p a b c -> p (a b c)"),
                S4[:].rearrange("p a b c -> p (a b c)")))
            em.bar()
            def mmu(t):
                r = None
                for h in range(4):
                    t.matmul(p3[:, h * 256:(h + 1) * 256], At4[:, h, :],
                             Vt_c[:, h * 256:(h + 1) * 256].rearrange("p m -> p m")
                             if False else AB1[:, 2048 + h * 256:2048 + (h + 1) * 256],
                             start=True, stop=True)
                    for kc2 in range(2):
                        r = t.matmul(p4[:, h * 256 + kc2 * 128:h * 256 + (kc2 + 1) * 128],
                                     AB1[:, 3072 + h * 256 + kc2 * 128:3072 + h * 256 + (kc2 + 1) * 128],
                                     At4[:, h, :], start=True, stop=True)
                return r
            em.op("tensor", mmu)
            em.bar()
            def evu(v):
                v.tensor_copy(Ub4[:].rearrange("p a b -> p (a b)"), p3[:])
                return v.tensor_copy(Wt4[:].rearrange("p a b m -> p (a b m)"), p4[:])
            em.op("vector", evu)
            em.bar()
            def mmd(t):
                r = None
                for h in range(4):
                    for kc2 in range(2):
                        r = t.matmul(p3[:, h * 256:(h + 1) * 256], Wt4[:, h, kc2, :],
                                     S4b[:, h, kc2, :], start=(kc2 == 0), stop=(kc2 == 1))
                return r
            em.op("tensor", mmd)
            em.bar()
            em.op("vector", lambda v: v.tensor_sub(
                dU4[:].rearrange("p a b -> p (a b)"),
                Ub4[:].rearrange("p a b -> p (a b)"), p3[:]))
            em.bar()
            def mmo1(t):
                r = None
                for h in range(4):
                    for kc2 in range(2):
                        r = t.matmul(p4[:, h * 256:(h + 1) * 256], QtT[:, h, kc2, :],
                                     S4b[:, h, kc2, :], start=(kc2 == 0), stop=(kc2 == 1))
                return r
            em.op("tensor", mmo1)
            em.bar()
            def evo1(v):
                r = None
                for h in range(4):
                    r = v.tensor_scalar(O14[:, h, :], p4[:, h * 256:(h + 1) * 256],
                                        g4[:, h:h + 1], None, ALU.mult)
                return r
            em.op("vector", evo1)
            em.bar()
            def mmo2(t):
                r = None
                for h in range(4):
                    r = t.matmul(p3[:, h * 256:(h + 1) * 256], Pb4[:, h, :],
                                 dU4[:, h, :], start=True, stop=True)
                return r
            em.op("tensor", mmo2)
            em.bar()
            em.op("vector", lambda v: v.tensor_add(
                rdt, O14[:].rearrange("p a b -> p (a b)"), p3[:]))
            em.bar()
            em.dma("sync", lambda s_, c=c: s_.dma_start(out=rd_t[:, c, :], in_=rdt))
            def mms2(t):
                r = None
                for h in range(2):
                    for kc2 in range(2):
                        r = t.matmul(p4[:, (h * 2 + kc2) * 256:(h * 2 + kc2 + 1) * 256],
                                     AB2[:, h * 256 + kc2 * 128:h * 256 + (kc2 + 1) * 128],
                                     dU4[:, h, :], start=True, stop=True)
                return r
            def mms3(t):
                r = None
                for h in range(2, 4):
                    for kc2 in range(2):
                        r = t.matmul(p3[:, ((h - 2) * 2 + kc2) * 256:((h - 2) * 2 + kc2 + 1) * 256],
                                     AB2[:, h * 256 + kc2 * 128:h * 256 + (kc2 + 1) * 128],
                                     dU4[:, h, :], start=True, stop=True)
                return r
            em.op("tensor", mms2)
            em.op("tensor", mms3)
            em.bar()
            def evs(v):
                r = None
                for h in range(4):
                    pp = p4 if h < 2 else p3
                    hh = h if h < 2 else h - 2
                    for kc2 in range(2):
                        r = v.scalar_tensor_tensor(
                            S4[:, h, kc2, :], S4[:, h, kc2, :],
                            gC4[:, h:h + 1],
                            pp[:, (hh * 2 + kc2) * 256:(hh * 2 + kc2 + 1) * 256],
                            ALU.mult, ALU.add)
                return r
            em.op("vector", evs)
            em.bar()

        em.dma("gpsimd", lambda g_: g_.dma_start(out=Wbig[:], in_=Wout_d))
        em.dma("sync", lambda s_: s_.dma_start(
            out=bout_b[:], in_=bout_d.broadcast_to([128, D])))
        em.dma("sync", lambda s_: s_.dma_start(
            out=rampb[:], in_=ramp_d.broadcast_to([128, D])))
        em.bar()
        em.op("scalar", lambda sc: sc.activation(rampb[:], rampb[:], AF.Sigmoid))
        em.bar()
        for g in range(NG):
            t0 = g * G
            for i in range(G):
                em.dma("sync", lambda s_, i=i, t0=t0: s_.dma_start(
                    out=xg[:, i, :], in_=rd_t[:, t0 + i, :]))
                em.dma("sync", lambda s_, i=i, t0=t0: s_.dma_start(
                    out=ogt[:, i, :], in_=og_t[:, t0 + i, :]))
            em.bar()
            for i in range(G):
                em.op("scalar", lambda sc, i=i: sc.activation(
                    sq[:], xg[:, i, :], AF.Square))
                em.bar()
                em.op("vector", lambda v: v.tensor_reduce(
                    ss4[:], sq[:].rearrange("p (h d) -> p h d", d=256), AX.X, ALU.add))
                em.bar()
                em.op("scalar", lambda sc: sc.activation(
                    r4n[:], ss4[:], AF.Sqrt, scale=1.0 / 256.0, bias=eps6[:, 0:1]))
                em.bar()
                em.op("vector", lambda v: v.reciprocal(r4n[:], r4n[:]))
                em.bar()
                def apr(v, i=i):
                    r = None
                    for h in range(4):
                        r = v.tensor_scalar(rdn[:, i, h * 256:(h + 1) * 256],
                                            xg[:, i, h * 256:(h + 1) * 256],
                                            r4n[:, h:h + 1], None, ALU.mult)
                    return r
                em.op("vector", apr)
                em.bar()
            emit_transpose_tiles(
                em, [rdn[:, i, :] for i in range(G)],
                lambda i, kc: rdT[:, kc, i * 128:(i + 1) * 128], iden[:], ptr[:], G)
            for i in range(G):
                def mmw(t, i=i):
                    for kc in range(8):
                        t.matmul(p1[:], rdT[:, kc, i * 128:(i + 1) * 128],
                                 Wbig[:, kc, 0:512], start=(kc == 0), stop=(kc == 7))
                    r = None
                    for kc in range(8):
                        r = t.matmul(p2[:], rdT[:, kc, i * 128:(i + 1) * 128],
                                     Wbig[:, kc, 512:1024], start=(kc == 0), stop=(kc == 7))
                    return r
                em.op("tensor", mmw)
                em.bar()
                def evw(v, i=i):
                    v.tensor_add(tkf[:, i, 0:512], p1[:], bout_b[:, 0:512])
                    return v.tensor_add(tkf[:, i, 512:1024], p2[:],
                                        bout_b[:, 512:1024])
                em.op("vector", evw)
                em.bar()
            em.op("vector", lambda v: v.tensor_mul(
                AF2[:, 0:4096], AF2[:, 0:4096], AF3[:, 0:4096]))
            em.bar()
            def rmp(v):
                r = None
                for i in range(G):
                    r = v.tensor_mul(tkf[:, i, :], tkf[:, i, :], rampb[:])
                return r
            em.op("vector", rmp)
            em.bar()
            for i in range(G):
                em.dma("sync", lambda s_, i=i, t0=t0: s_.dma_start(
                    out=ogt[:, i, :], in_=x_t[:, t0 + i, :]))
            em.bar()
            emit_ln_group(em, [tkf[:, i, :] for i in range(G)],
                          [tkf[:, i, :] for i in range(G)],
                          [tkf[:, i, :] for i in range(G)], sq[:], cols, D, G)
            em.op("vector", lambda v: v.tensor_add(
                AF2[:, 0:4096], AF2[:, 0:4096], AF3[:, 0:4096]))
            em.bar()
            for i in range(G):
                em.dma("sync", lambda s_, i=i, t0=t0: s_.dma_start(
                    out=y_t[:, t0 + i, :], in_=tkf[:, i, :]))
            em.bar()



def run_l0_attn(x_full, W_in, W_ctrl, b_ctrl, w_q, w_k, w_v, W_out, b_out,
                ramp_logit):
    if "l0a" not in _CACHE:
        _CACHE["l0a"] = build_l0_attn()
    nc = _CACHE["l0a"]
    wcv = np.zeros((1024, 12), np.float32)
    wcv[:, 0:4] = np.asarray(w_q)[:, 0, :]
    wcv[:, 4:8] = np.asarray(w_k)[:, 0, :]
    wcv[:, 8:12] = np.asarray(w_v)[:, 0, :]
    wcvt = np.ascontiguousarray(wcv.reshape(8, 128, 12).transpose(1, 0, 2))
    consts = {"iden": _ident_bf16(), "Win": _wtile(np.asarray(W_in)),
              "Wctrl": _wtile(np.asarray(W_ctrl)),
              "Wout": _wtile(np.asarray(W_out)),
              "bctrl": np.asarray(b_ctrl).reshape(1, 1032),
              "bout": np.asarray(b_out).reshape(1, D),
              "ramp": np.asarray(ramp_logit).reshape(1, D),
              "wconv": wcvt, **_l0attn_consts()}
    in_maps = [{"x": np.ascontiguousarray(x_full[c]), **consts}
               for c in range(NCORES)]
    res = run_bass_kernel_spmd(nc, in_maps, core_ids=list(range(NCORES)))
    return np.stack([np.asarray(r["y"]) for r in res.results])



def emit_l1_ffn(nc, em, pfx, x_in, y_out, dd):
    SB = lambda n, s, d: nc.sbuf_tensor(pfx + n, s, d)
    PS = lambda n, s, d: nc.psum_tensor(pfx + n, s, d)
    Wq_d = dd["Wq"]; bq_d = dd["bq"]; K1T_d = dd["K1T"]; K2T_d = dd["K2T"]
    Wdn_d = dd["Wdn"]; Wup_d = dd["Wup"]; iden_d = dd["iden"]

    x_t = x_in.rearrange("(t p) d -> p t d", p=PT)
    y_t = y_out.rearrange("(t p) d -> p t d", p=PT)
    NH = 8
    TK = 10

    with ExitStack() as ctx:
        E = ctx.enter_context
        iden = E(SB("iden_s", [128, 128], BF16))
        eps_c = E(SB("eps_c", [128, 1], F32))
        sq = E(SB("sq", [128, 1024], F32))
        cols = {"s": E(SB("ln_s", [128, 4], F32)),
                "ss": E(SB("ln_ss", [128, 4], F32)),
                "r": E(SB("ln_r", [128, 4], F32)),
                "eps": eps_c}
        Wq_s = E(SB("Wq_s", [128, 8, 2048], BF16))
        bq_b = E(SB("bq_b", [128, 2048], F32))
        K1T_s = E(SB("K1T_s", [128, 256], BF16))
        K2T_s = E(SB("K2T_s", [128, 256], BF16))
        x0 = E(SB("x0", [128, D], F32))
        x0b = E(SB("x0b", [128, D], BF16))
        xin_t = E(SB("xin_t", [128, D], F32))
        xTt = E(SB("xTt", [128, 8, 128], BF16))
        qf = E(SB("qf", [128, 2048], F32))
        qb = E(SB("qb", [128, 2048], BF16))
        qT = E(SB("qT", [128, 16, 128], BF16))
        sc1 = E(SB("sc1", [128, NH, 256], F32))
        sc2 = E(SB("sc2", [128, NH, 256], F32))
        scm = E(SB("scm", [128, NH, 256], F32))
        m1a = E(SB("m1a", [128, NH, 8], F32))
        m1b = E(SB("m1b", [128, NH, 8], F32))
        m2a = E(SB("m2a", [128, NH, 8], F32))
        m2b = E(SB("m2b", [128, NH, 8], F32))
        i1a = E(SB("i1a", [128, NH, 8], U32))
        i1b = E(SB("i1b", [128, NH, 8], U32))
        i2a = E(SB("i2a", [128, NH, 8], U32))
        i2b = E(SB("i2b", [128, NH, 8], U32))
        v1f = E(SB("v1f", [128, NH, TK], F32))
        v2f = E(SB("v2f", [128, NH, TK], F32))
        i1f = E(SB("i1f", [128, NH, TK], F32))
        i2f = E(SB("i2f", [128, NH, TK], F32))
        cs4 = E(SB("cs4", [128, NH, TK, TK], F32))
        ea4 = E(SB("ea4", [128, NH, TK, TK], F32))
        csm = E(SB("csm", [128, NH, 100], F32))
        cm1 = E(SB("cm1", [128, NH, 8], F32))
        cm2 = E(SB("cm2", [128, NH, 8], F32))
        cvals = E(SB("cvals", [128, NH, TK], F32))
        eqm = E(SB("eqm", [128, NH, 100], F32))
        esf = E(SB("esf", [128, NH, TK], F32))
        esel = E(SB("esel", [128, NH, TK], U32))
        hsc = E(SB("hsc", [128, NH, TK], F32))
        hd = E(SB("hd", [128, NH, TK], F32))
        den8 = E(SB("den8", [128, NH], F32))
        Wg = E(SB("Wg", [128, 2, 8, 1024], BF16))
        dtsb = E(SB("dtsb", [128, 2, 8, 1024], BF16))
        acc = E(SB("acc", [128, D], F32))
        p1 = E(PS("p1", [128, 512], F32))
        p2 = E(PS("p2", [128, 512], F32))
        p3 = E(PS("p3", [128, 1024], F32))
        p4 = E(PS("p4", [128, 1024], F32))
        ptr = E(PS("ptr", [128, 1024], BF16))
        ptr2 = E(PS("ptr2", [128, 1024], BF16))


        em.dma("sync", lambda s_: s_.dma_start(out=iden[:], in_=iden_d))
        em.dma("gpsimd", lambda g_: g_.dma_start(out=Wq_s[:], in_=Wq_d))
        em.dma("gpsimd", lambda g_: g_.dma_start(out=K1T_s[:], in_=K1T_d))
        em.dma("gpsimd", lambda g_: g_.dma_start(out=K2T_s[:], in_=K2T_d))
        em.dma("sync", lambda s_: s_.dma_start(
            out=bq_b[:], in_=bq_d.broadcast_to([128, 2048])))
        em.op("vector", lambda v: v.memset(eps_c[:], 1e-5))
        em.bar()

        for t_i in range(NT):
            em.dma("sync", lambda s_, t_i=t_i: s_.dma_start(
                out=xin_t[:], in_=x_t[:, t_i, :]))
            em.bar()
            emit_ln_group(em, [xin_t[:]], [x0[:]], [x0[:]], sq[:, 0:D], cols, D, 1)
            em.op("vector", lambda v: v.tensor_copy(x0b[:], x0[:]))
            em.bar()
            def trx(t):
                r = None
                for kc in range(8):
                    r = t.transpose(ptr[:, kc * 128:(kc + 1) * 128],
                                    x0b[:, kc * 128:(kc + 1) * 128], iden[:])
                return r
            em.op("tensor", trx)
            em.bar()
            em.op("vector", lambda v: v.tensor_copy(
                xTt[:].rearrange("p a m -> p (a m)"), ptr[:]))
            em.bar()
            for half in range(2):
                def mmq(t, half=half):
                    r = None
                    for j in range(2):
                        nb = half * 2 + j
                        pp = p1 if j == 0 else p2
                        for kc in range(8):
                            r = t.matmul(pp[:], xTt[:, kc, :],
                                         Wq_s[:, kc, nb * 512:(nb + 1) * 512],
                                         start=(kc == 0), stop=(kc == 7))
                    return r
                em.op("tensor", mmq)
                em.bar()
                def evq(v, half=half):
                    v.tensor_add(qf[:, half * 1024:half * 1024 + 512], p1[:],
                                 bq_b[:, half * 1024:half * 1024 + 512])
                    return v.tensor_add(qf[:, half * 1024 + 512:half * 1024 + 1024],
                                        p2[:], bq_b[:, half * 1024 + 512:half * 1024 + 1024])
                em.op("vector", evq)
                em.bar()
            em.op("vector", lambda v: v.tensor_copy(qb[:], qf[:]))
            em.bar()
            for half in range(2):
                def trq(t, half=half):
                    r = None
                    for c8 in range(8):
                        r = t.transpose(ptr[:, c8 * 128:(c8 + 1) * 128],
                                        qb[:, half * 1024 + c8 * 128:half * 1024 + (c8 + 1) * 128],
                                        iden[:])
                    return r
                em.op("tensor", trq)
                em.bar()
                em.op("vector", lambda v, half=half: v.tensor_copy(
                    qT[:, half * 8:(half + 1) * 8, :].rearrange("p a m -> p (a m)"),
                    ptr[:]))
                em.bar()
            for rr_ in range(2):
                def mms(t, rr_=rr_):
                    r = None
                    for hh in range(4):
                        h = rr_ * 4 + hh
                        r = t.matmul(p3[:, hh * 256:(hh + 1) * 256],
                                     qT[:, h * 2, :], K1T_s[:], start=True, stop=True)
                        r = t.matmul(p4[:, hh * 256:(hh + 1) * 256],
                                     qT[:, h * 2 + 1, :], K2T_s[:], start=True, stop=True)
                    return r
                em.op("tensor", mms)
                em.bar()
                def evs(v, rr_=rr_):
                    v.tensor_copy(sc1[:, rr_ * 4:(rr_ + 1) * 4, :]
                                  .rearrange("p a b -> p (a b)"), p3[:])
                    return v.tensor_copy(sc2[:, rr_ * 4:(rr_ + 1) * 4, :]
                                         .rearrange("p a b -> p (a b)"), p4[:])
                em.op("vector", evs)
                em.bar()
            def mx1(v):
                r = None
                for h in range(NH):
                    v.max(out=m1a[:, h, :], in_=sc1[:, h, :])
                    r = v.max(out=m2a[:, h, :], in_=sc2[:, h, :])
                return r
            em.op("vector", mx1)
            em.bar()
            def ix1(v):
                r = None
                for h in range(NH):
                    v.max_index(out=i1a[:, h, :], in_max=m1a[:, h, :], in_values=sc1[:, h, :])
                    r = v.max_index(out=i2a[:, h, :], in_max=m2a[:, h, :], in_values=sc2[:, h, :])
                return r
            em.op("vector", ix1)
            em.bar()
            def mr1(v):
                r = None
                for h in range(NH):
                    r = v.match_replace(out=scm[:, h, :], in_to_replace=m1a[:, h, :],
                                        in_values=sc1[:, h, :], imm_value=NEG)
                return r
            em.op("vector", mr1)
            em.bar()
            def mx1b(v):
                r = None
                for h in range(NH):
                    r = v.max(out=m1b[:, h, :], in_=scm[:, h, :])
                return r
            em.op("vector", mx1b)
            em.bar()
            def ix1b(v):
                r = None
                for h in range(NH):
                    r = v.max_index(out=i1b[:, h, :], in_max=m1b[:, h, :], in_values=scm[:, h, :])
                return r
            em.op("vector", ix1b)
            em.bar()
            def mr2(v):
                r = None
                for h in range(NH):
                    r = v.match_replace(out=scm[:, h, :], in_to_replace=m2a[:, h, :],
                                        in_values=sc2[:, h, :], imm_value=NEG)
                return r
            em.op("vector", mr2)
            em.bar()
            def mx2b(v):
                r = None
                for h in range(NH):
                    r = v.max(out=m2b[:, h, :], in_=scm[:, h, :])
                return r
            em.op("vector", mx2b)
            em.bar()
            def ix2b(v):
                r = None
                for h in range(NH):
                    r = v.max_index(out=i2b[:, h, :], in_max=m2b[:, h, :], in_values=scm[:, h, :])
                return r
            em.op("vector", ix2b)
            em.bar()
            def asm(v):
                v.tensor_copy(v1f[:, :, 0:8], m1a[:])
                v.tensor_copy(v1f[:, :, 8:10], m1b[:, :, 0:2])
                v.tensor_copy(v2f[:, :, 0:8], m2a[:])
                v.tensor_copy(v2f[:, :, 8:10], m2b[:, :, 0:2])
                v.tensor_copy(i1f[:, :, 0:8], i1a[:])
                v.tensor_copy(i1f[:, :, 8:10], i1b[:, :, 0:2])
                v.tensor_copy(i2f[:, :, 0:8], i2a[:])
                return v.tensor_copy(i2f[:, :, 8:10], i2b[:, :, 0:2])
            em.op("vector", asm)
            em.bar()
            em.op("vector", lambda v: v.tensor_scalar(
                i1f[:].rearrange("p a b -> p (a b)"),
                i1f[:].rearrange("p a b -> p (a b)"), 256.0, None, ALU.mult))
            em.bar()
            def bld(v):
                v.tensor_scalar(cs4[:],
                                v1f[:].unsqueeze(2).broadcast_to([128, NH, TK, TK]),
                                1.0, None, ALU.mult)
                return v.tensor_scalar(
                    ea4[:], i1f[:].unsqueeze(2).broadcast_to([128, NH, TK, TK]),
                    1.0, None, ALU.mult)
            em.op("vector", bld)
            em.bar()
            def bld2(v):
                cs_sw = cs4[:].rearrange("p h b a -> p h a b")
                ea_sw = ea4[:].rearrange("p h b a -> p h a b")
                v.tensor_add(cs_sw, cs_sw,
                             v2f[:].unsqueeze(2).broadcast_to([128, NH, TK, TK]))
                return v.tensor_add(ea_sw, ea_sw,
                                    i2f[:].unsqueeze(2).broadcast_to([128, NH, TK, TK]))
            em.op("vector", bld2)
            em.bar()
            def cx1(v):
                r = None
                for h in range(NH):
                    r = v.max(out=cm1[:, h, :],
                              in_=cs4[:, h, :, :].rearrange("p a b -> p (a b)"))
                return r
            em.op("vector", cx1)
            em.bar()
            def cmr(v):
                r = None
                for h in range(NH):
                    r = v.match_replace(out=csm[:, h, :], in_to_replace=cm1[:, h, :],
                                        in_values=cs4[:, h, :, :].rearrange("p a b -> p (a b)"),
                                        imm_value=NEG)
                return r
            em.op("vector", cmr)
            em.bar()
            def cx2(v):
                r = None
                for h in range(NH):
                    r = v.max(out=cm2[:, h, :], in_=csm[:, h, :])
                return r
            em.op("vector", cx2)
            em.bar()
            def casm(v):
                v.tensor_copy(cvals[:, :, 0:8], cm1[:])
                return v.tensor_copy(cvals[:, :, 8:10], cm2[:, :, 0:2])
            em.op("vector", casm)
            em.bar()
            for k in range(TK):
                def eq1(v, k=k):
                    r = None
                    for h in range(NH):
                        r = v.tensor_scalar(eqm[:, h, :],
                                            cs4[:, h, :, :].rearrange("p a b -> p (a b)"),
                                            cvals[:, h, k:k + 1], None, ALU.is_equal)
                    return r
                em.op("vector", eq1)
                em.bar()
                em.op("vector", lambda v: v.tensor_mul(
                    eqm[:].rearrange("p a b -> p (a b)"),
                    eqm[:].rearrange("p a b -> p (a b)"),
                    ea4[:].rearrange("p h b a -> p (h b a)")))

                em.bar()
                em.op("vector", lambda v, k=k: v.tensor_reduce(
                    esf[:, :, k], eqm[:], AX.X, ALU.max))
                em.bar()
            em.op("vector", lambda v: v.tensor_copy(
                esel[:].rearrange("p a b -> p (a b)"),
                esf[:].rearrange("p a b -> p (a b)")))
            def sfx(v):
                r = None
                for h in range(NH):
                    r = v.tensor_scalar(hsc[:, h, :], cvals[:, h, :],
                                        cvals[:, h, 0:1], None, ALU.subtract)
                return r
            em.op("vector", sfx)
            em.bar()
            em.op("scalar", lambda sc: sc.activation(
                hsc[:].rearrange("p a b -> p (a b)"),
                hsc[:].rearrange("p a b -> p (a b)"), AF.Exp))
            em.bar()
            em.op("vector", lambda v: v.tensor_reduce(den8[:], hsc[:], AX.X, ALU.add))
            em.bar()
            em.op("vector", lambda v: v.reciprocal(den8[:], den8[:]))
            em.bar()
            def sfx2(v):
                r = None
                for h in range(NH):
                    r = v.tensor_scalar(hsc[:, h, :], hsc[:, h, :],
                                        den8[:, h:h + 1], None, ALU.mult)
                return r
            em.op("vector", sfx2)
            em.bar()
            for si in range(TK + 1):
                if si < TK:
                    for sl in range(8):
                        def gd(g_, k=si, sl=sl, pb=si % 2):
                            return g_.indirect_dma_start(
                                out=Wg[:, pb, sl, :], out_offset=None,
                                in_=Wdn_d[0:128, :],
                                in_offset=bass.IndirectOffsetOnAxis(
                                    ap=esel[:, sl, k:k + 1], axis=0))
                        em.dma("gpsimd", gd)
                if si >= 1:
                    def dodv(v, pk=si - 1, ppb=(si - 1) % 2):
                        for sl in range(8):
                            v.tensor_mul(dtsb[:, ppb, sl, :],
                                         Wg[:, ppb, sl, :], x0b[:])
                        r = None
                        for sl in range(8):
                            r = v.tensor_reduce(
                                hd[:, sl, pk:pk + 1], dtsb[:, ppb, sl, :],
                                AX.X, ALU.add)
                        return r
                    em.op("vector", dodv)
                em.bar()
            em.op("scalar", lambda sc: sc.activation(
                esf[:].rearrange("p a b -> p (a b)"),
                hd[:].rearrange("p a b -> p (a b)"), AF.Erf,
                scale=0.7071067811865476))
            em.bar()
            def gel(v):
                t1 = esf[:].rearrange("p a b -> p (a b)")
                h1 = hd[:].rearrange("p a b -> p (a b)")
                v.tensor_mul(t1, t1, h1)
                v.tensor_add(t1, t1, h1)
                v.tensor_mul(h1, t1, hsc[:].rearrange("p a b -> p (a b)"))
                return v.tensor_scalar(h1, h1, 0.5, None, ALU.mult)
            em.op("vector", gel)
            em.op("gpsimd", lambda g_: g_.memset(acc[:], 0.0))
            em.bar()
            for si in range(TK + 1):
                if si < TK:
                    for sl in range(8):
                        def gu(g_, k=si, sl=sl, pb=si % 2):
                            return g_.indirect_dma_start(
                                out=Wg[:, pb, sl, :], out_offset=None,
                                in_=Wup_d[0:128, :],
                                in_offset=bass.IndirectOffsetOnAxis(
                                    ap=esel[:, sl, k:k + 1], axis=0))
                        em.dma("gpsimd", gu)
                if si >= 1:
                    def upav(v, pk=si - 1, ppb=(si - 1) % 2):
                        r = None
                        for sl in range(8):
                            r = v.scalar_tensor_tensor(
                                acc[:], Wg[:, ppb, sl, :],
                                hd[:, sl, pk:pk + 1],
                                acc[:], ALU.mult, ALU.add)
                        return r
                    em.op("vector", upav)
                em.bar()
            emit_ln_group(em, [acc[:]], [acc[:]], [acc[:]], sq[:, 0:D], cols, D, 1)
            em.op("vector", lambda v: v.tensor_add(xin_t[:], xin_t[:], acc[:]))
            em.bar()
            em.dma("sync", lambda s_, t_i=t_i: s_.dma_start(
                out=y_t[:, t_i, :], in_=xin_t[:]))
            em.bar()



def run_l1_ffn(x_full, Wq, bq, K1, K2, W_up, W_down):
    if "l1f" not in _CACHE:
        _CACHE["l1f"] = build_l1_ffn()
    nc = _CACHE["l1f"]
    consts = {"iden": _ident_bf16(), "Wq": _wtile(np.asarray(Wq)),
              "bq": np.asarray(bq).reshape(1, 2048),
              "K1T": np.ascontiguousarray(np.asarray(K1).T),
              "K2T": np.ascontiguousarray(np.asarray(K2).T),
              "Wdn": np.asarray(W_down), "Wup": np.asarray(W_up)}
    in_maps = [{"x": np.ascontiguousarray(x_full[c]), **consts}
               for c in range(NCORES)]
    res = run_bass_kernel_spmd(nc, in_maps, core_ids=list(range(NCORES)))
    return np.stack([np.asarray(r["y"]) for r in res.results])


_CACHE = {}

_L0A_KEYS = ("Win", "Wctrl", "Wout", "bctrl", "bout", "ramp", "wconv",
             "utones", "trilS", "triuS", "triuI", "idf32", "ones1", "ones128")
_L0F_KEYS = tuple([f"{k}{l}" for l in range(2)
                   for k in ("wqkv", "wo", "w1a", "w1g", "w2p", "qs", "ks",
                             "lng", "lnb", "b1a", "b1g", "b2")]
                  + ["fqc", "fqs", "fkc", "fks", "mask"])
_L1A_KEYS = ("Wq", "Wkv", "Wo", "Wo0", "bo", "bo0", "mask", "qcos", "qsin",
             "kcos", "ksin")
_L1F_KEYS = ("Wq", "bq", "K1T", "K2T", "Wdn", "Wup")


def build_fused():
    nc = bass.Bass("TRN2", target_bir_lowering=False)
    x_in = nc.dram_tensor("x", [S, D], F32, kind="ExternalInput").ap()
    y_out = nc.dram_tensor("y", [S, D], F32, kind="ExternalOutput").ap()
    iden_d = nc.dram_tensor("iden", [128, 128], BF16, kind="ExternalInput").ap()
    x01 = nc.dram_tensor("x01", [S, D], F32).ap()
    x12 = nc.dram_tensor("x12", [S, D], F32).ap()
    x23 = nc.dram_tensor("x23", [S, D], F32).ap()

    def DI(name, shape, dt=F32):
        return nc.dram_tensor(name, shape, dt, kind="ExternalInput").ap()

    dd_a = {"Win": DI("a_Win", [128, 8, 3072]),
            "Wctrl": DI("a_Wctrl", [128, 8, 1032]),
            "Wout": DI("a_Wout", [128, 8, 1024]),
            "bctrl": DI("a_bctrl", [1, 1032]), "bout": DI("a_bout", [1, D]),
            "ramp": DI("a_ramp", [1, D]), "wconv": DI("a_wconv", [128, 8, 12]),
            "iden": iden_d}
    for k, v in _l0attn_consts().items():
        dd_a[k] = DI("a_" + k, list(v.shape))

    dd_f = {"iden": iden_d, "mask": DI("f_mask", [128, 4, 1024])}
    for k in ("fqc", "fqs", "fkc", "fks"):
        dd_f[k] = DI("f_" + k, [128, NT, 32])
    shapes_f = {"wqkv": [128, 8, 1536], "wo": [128, 4, 1024],
                "w1a": [128, 8, 2730], "w1g": [128, 8, 2730],
                "w2p": [128, 22, 1024], "qs": [1, 64], "ks": [1, 64],
                "lng": [1, D], "lnb": [1, D], "b1a": [1, 2730],
                "b1g": [1, 2730], "b2": [1, D]}
    for l in range(2):
        for k, shp in shapes_f.items():
            dd_f[f"{k}{l}"] = DI(f"f_{k}{l}", shp)

    dd_s = {"Wq": DI("s_Wq", [128, 8, D]), "Wkv": DI("s_Wkv", [128, 8, D]),
            "Wo": DI("s_Wo", [128, 8, D]), "Wo0": DI("s_Wo0", [128, 8, D]),
            "bo": DI("s_bo", [1, D]), "bo0": DI("s_bo0", [1, D]),
            "iden": iden_d, "mask": DI("s_mask", [128, 4, 1024])}
    for k in ("qcos", "qsin", "kcos", "ksin"):
        dd_s[k] = DI("s_" + k, [128, NT, 32])

    dd_p = {"Wq": DI("p_Wq", [128, 8, 2048]), "bq": DI("p_bq", [1, 2048]),
            "K1T": DI("p_K1T", [128, 256]), "K2T": DI("p_K2T", [128, 256]),
            "Wdn": DI("p_Wdn", [65536, 1024]), "Wup": DI("p_Wup", [65536, 1024]),
            "iden": iden_d}

    with ExitStack() as ctx:
        em = Em(nc)
        emit_l0_attn(nc, em, "a_", x_in, x01, dd_a)
        emit_l0_ffn(nc, em, "f_", x01, x12, dd_f)
        emit_l1_attn(nc, em, "s_", x12, x23, dd_s)
        emit_l1_ffn(nc, em, "p_", x23, y_out, dd_p)
        em.build(ctx)
    return nc


def _host_consts(inputs):
    c = {"iden": _ident_bf16()}
    wcv = np.zeros((1024, 12), np.float32)
    wcv[:, 0:4] = np.asarray(inputs["L0_attn_w_q"])[:, 0, :]
    wcv[:, 4:8] = np.asarray(inputs["L0_attn_w_k"])[:, 0, :]
    wcv[:, 8:12] = np.asarray(inputs["L0_attn_w_v"])[:, 0, :]
    c["a_wconv"] = np.ascontiguousarray(wcv.reshape(8, 128, 12).transpose(1, 0, 2))
    c["a_Win"] = _wtile(np.asarray(inputs["L0_attn_W_in"]))
    c["a_Wctrl"] = _wtile(np.asarray(inputs["L0_attn_W_ctrl"]))
    c["a_Wout"] = _wtile(np.asarray(inputs["L0_attn_W_out"]))
    c["a_bctrl"] = np.asarray(inputs["L0_attn_b_ctrl"]).reshape(1, 1032)
    c["a_bout"] = np.asarray(inputs["L0_attn_b_out"]).reshape(1, D)
    c["a_ramp"] = np.asarray(inputs["L0_attn_ramp_logit"]).reshape(1, D)
    for k, v in _l0attn_consts().items():
        c["a_" + k] = v
    for k, v in _l0ffn_xpos().items():
        c["f_" + k] = v
    c["f_mask"] = _l0ffn_mask()
    w1 = np.asarray(inputs["L0_ffn_w1"])
    w2 = np.asarray(inputs["L0_ffn_w2"])
    for l in range(2):
        w2p = np.zeros((2816, 1024), np.float32)
        w2p[0:2730] = w2[l]
        c[f"f_wqkv{l}"] = _wtile(np.asarray(inputs["L0_ffn_wqkv"][l]))
        c[f"f_wo{l}"] = _wtile(np.asarray(inputs["L0_ffn_wo"][l]))
        c[f"f_w1a{l}"] = _wtile(np.ascontiguousarray(w1[l][:, 0:2730]))
        c[f"f_w1g{l}"] = _wtile(np.ascontiguousarray(w1[l][:, 2730:]))
        c[f"f_w2p{l}"] = _wtile(w2p)
        c[f"f_qs{l}"] = np.asarray(inputs["L0_ffn_q_scale"][l]).reshape(1, 64)
        c[f"f_ks{l}"] = np.asarray(inputs["L0_ffn_k_scale"][l]).reshape(1, 64)
        c[f"f_lng{l}"] = np.asarray(inputs["L0_ffn_ln_g"][l]).reshape(1, D)
        c[f"f_lnb{l}"] = np.asarray(inputs["L0_ffn_ln_b"][l]).reshape(1, D)
        c[f"f_b1a{l}"] = np.ascontiguousarray(
            np.asarray(inputs["L0_ffn_b1"][l])[0:2730]).reshape(1, 2730)
        c[f"f_b1g{l}"] = np.ascontiguousarray(
            np.asarray(inputs["L0_ffn_b1"][l])[2730:]).reshape(1, 2730)
        c[f"f_b2{l}"] = np.asarray(inputs["L0_ffn_b2"][l]).reshape(1, D)
    c["s_Wq"] = _wtile(np.asarray(inputs["L1_attn_Wq"]))
    c["s_Wkv"] = _wtile(np.asarray(inputs["L1_attn_Wkv"]))
    c["s_Wo"] = _wtile(np.asarray(inputs["L1_attn_Wo"]))
    c["s_Wo0"] = _wtile(np.asarray(inputs["L1_attn_Wo0"]))
    c["s_bo"] = np.asarray(inputs["L1_attn_bo"]).reshape(1, D)
    c["s_bo0"] = np.asarray(inputs["L1_attn_bo0"]).reshape(1, D)
    c["s_mask"] = _l1_mask()
    for k, v in _l1_rot().items():
        c["s_" + k] = v
    c["p_Wq"] = _wtile(np.asarray(inputs["L1_ffn_Wq"]))
    c["p_bq"] = np.asarray(inputs["L1_ffn_bq"]).reshape(1, 2048)
    c["p_K1T"] = np.ascontiguousarray(np.asarray(inputs["L1_ffn_K1"]).T)
    c["p_K2T"] = np.ascontiguousarray(np.asarray(inputs["L1_ffn_K2"]).T)
    c["p_Wdn"] = np.asarray(inputs["L1_ffn_W_down"])
    c["p_Wup"] = np.asarray(inputs["L1_ffn_W_up"])
    return c


def kernel(**inputs):
    if "fused" not in _CACHE:
        _CACHE["fused"] = build_fused()
    nc = _CACHE["fused"]
    c = _host_consts(inputs)
    xall = np.asarray(inputs["L0_attn_x"], np.float32)
    in_maps = [{"x": np.ascontiguousarray(xall[ci]), **c} for ci in range(NCORES)]
    res = run_bass_kernel_spmd(nc, in_maps, core_ids=list(range(NCORES)))
    return np.stack([np.asarray(r["y"]) for r in res.results]).astype(np.float32)
